# version 10
# baseline (speedup 1.0000x reference)
import os
import numpy as np
from contextlib import ExitStack

import concourse.bass as bass
import concourse.mybir as mybir
from concourse.bass_utils import run_bass_kernel_spmd

F32 = mybir.dt.float32
BF16 = mybir.dt.bfloat16
AF = mybir.ActivationFunctionType
ALU = mybir.AluOpType
AXX = mybir.AxisListType.X

MAGIC = float(np.float32(1.5 * 2 ** 23))
NCORE = 8
TOK = 2048
SEQ = 4096
D = 1024
NCH = 8
FF = 4096
NFH = 32
NH = 16
DH = 64
NJC = 32
BLK = 512
NBLK = TOK // BLK
FT = 512
NFT = TOK // FT
EXPSHIFT = 70.0
GROUPS = [[0, 1], [2, 3], [4, 5], [6, 7]]

DMA_LANES = {"sync", "sdma2", "gdma", "sbc", "stw", "svt", "sbb", "swl",
             "swt"}
SERIAL_LANES = {"sbc", "stw", "sbb"}
DELAY_SRC = set(os.environ.get("DELAY_SRC", "tensor").split(",")) - {""}
DELAY_CYC = int(os.environ.get("DELAY_CYC", "96"))
LANES = ["tensor", "vector", "scalar", "gpsimd", "sync", "sdma2", "gdma",
         "sbc", "stw", "svt", "sbb", "swl", "swt"]
LANE_STREAM = {"tensor": "tensor", "vector": "vector", "scalar": "scalar",
               "gpsimd": "gpsimd", "sync": "sync", "sdma2": "scalar",
               "gdma": "gpsimd", "sbc": "sync", "stw": "scalar",
               "svt": "sync", "sbb": "sync", "swl": "sync", "swt": "scalar"}


class Step:
    __slots__ = ("lane", "fn", "val", "deps", "label")

    def __init__(self, lane, fn, val, deps, label):
        self.lane, self.fn, self.val, self.deps, self.label = \
            lane, fn, val, deps, label


class Buf:
    __slots__ = ("name", "w", "readers")

    def __init__(self, name):
        self.name = name
        self.w = None
        self.readers = []


class Plan:
    def __init__(self):
        self.steps = []
        self.count = {e: 0 for e in LANES}

    def add(self, lane, fn, reads=(), writes=(), deps=(), label=""):
        d = [x for x in deps if x is not None]
        for b in reads:
            if b.w is not None:
                d.append(b.w)
        for b in writes:
            if b.w is not None:
                d.append(b.w)
            d.extend(b.readers)
        inc = 16 if lane in DMA_LANES else 1
        self.count[lane] += inc
        st = Step(lane, fn, self.count[lane], d, label)
        self.steps.append(st)
        for b in reads:
            b.readers.append(st)
        for b in writes:
            b.w = st
            b.readers = []
        return st


def _skip_dep(lane, dep):
    if dep.lane == lane and lane == "tensor":
        return True
    if dep.lane == lane and lane in DMA_LANES:
        return True
    return False


def emit_all(nc, block, plan):
    sems = {}
    for e in LANES:
        if any(st.lane == e for st in plan.steps):
            sems[e] = nc.semaphore(f"sem_{e}").__enter__()

    def make_body(lanes):
        steps = [st for st in plan.steps if st.lane in lanes]

        def body(eng):
            seen = {e: 0 for e in LANES}
            for st in steps:
                need = {}
                for dep in st.deps:
                    if _skip_dep(st.lane, dep):
                        continue
                    if dep.val > need.get(dep.lane, 0):
                        need[dep.lane] = dep.val
                fresh_wait_lanes = set()
                for lane_, val in need.items():
                    if seen[lane_] >= val:
                        continue
                    eng.wait_ge(sems[lane_], val)
                    seen[lane_] = val
                    fresh_wait_lanes.add(lane_)
                if st.lane in ("vector", "scalar", "gpsimd") and \
                        fresh_wait_lanes & DELAY_SRC:
                    eng.nop()
                    eng.nop()
                    eng.nop()
                if st.lane in SERIAL_LANES and st.val > 16 and \
                        seen[st.lane] < st.val - 16:
                    eng.wait_ge(sems[st.lane], st.val - 16)
                    seen[st.lane] = st.val - 16
                inst = st.fn(eng)
                if inst is None:
                    inst = eng.nop()
                inst.then_inc(sems[st.lane], 16 if st.lane in DMA_LANES else 1)
            for lane_ in LANES:
                if lane_ in sems and plan.count[lane_] > seen[lane_]:
                    eng.wait_ge(sems[lane_], plan.count[lane_])
        return body

    streams = {}
    for lane in LANES:
        if any(st.lane == lane for st in plan.steps):
            streams.setdefault(LANE_STREAM[lane], []).append(lane)
    for phys, fn in (("tensor", block.tensor), ("vector", block.vector),
                     ("scalar", block.scalar), ("gpsimd", block.gpsimd),
                     ("sync", block.sync)):
        if phys in streams:
            fn(make_body(tuple(streams[phys])))
    return sems



def _tr(e, out, in_, ident):
    return e.transpose(out, in_, ident)

def make_identity(nc, identity, base=0):
    nc.gpsimd.memset(identity, 0.0)
    nc.gpsimd.nop()
    nc.gpsimd.affine_select(
        out=identity, in_=identity, compare_op=ALU.not_equal,
        fill=1.0, base=base, pattern=[[-1, identity.shape[1]]],
        channel_multiplier=1)


class K:
    def __init__(self, stage=8):
        self.stage = stage
        self.want_taps = set(os.environ.get("TAPS", "").split(",")) - {""}
        nc = bass.Bass("TRN2", target_bir_lowering=False)
        self.nc = nc
        self.p = Plan()
        self.stk = ExitStack()
        self.bufs = {}

        def inp(name, shape):
            return nc.dram_tensor(name, shape, F32, kind="ExternalInput").ap()

        self.x_own = inp("x_own", [TOK, D])
        self.x_oth = inp("x_oth", [TOK, D])
        self.hmask = inp("hmask", [1, 2])
        self.wq0 = inp("L0_attn_Wq", [D, D])
        self.wkv0 = inp("L0_attn_Wkv", [D, 512])
        self.kscale = inp("L0_attn_k_scale", [NH * DH])
        self.w1_0 = inp("L0_ffn_w1", [FF, D])
        self.cw0 = inp("L0_ffn_conv_w", [FF, 3])
        self.cb0 = inp("L0_ffn_conv_b", [FF])
        self.w2_0 = inp("L0_ffn_w2", [D, FF])
        self.wq1 = inp("L1_attn_Wq", [D, D])
        self.wk1 = inp("L1_attn_Wk", [D, D])
        self.wv1 = inp("L1_attn_Wv", [D, D])
        self.wo1 = inp("L1_attn_Wo", [D, D])
        self.qw1 = inp("L1_attn_quadrant_weights", [1, 4])
        self.w1_1 = inp("L1_ffn_w1", [FF, D])
        self.cw1 = inp("L1_ffn_conv_w", [FF, 3])
        self.cb1 = inp("L1_ffn_conv_b", [FF])
        self.w2_1 = inp("L1_ffn_w2", [D, FF])
        self.out = nc.dram_tensor("out", [D, TOK], F32, kind="ExternalOutput").ap()

        def scr(name, shape, dt=BF16):
            t = nc.dram_tensor(name, shape, dt).ap()
            self.bufs[name] = Buf(name)
            return t

        self.xb16 = scr("xb16", [SEQ, D])
        self.wqnat = [[scr(f"w{j}qnat_l{l}", [FF, D] if j == 1 else [D, FF])
                       for j in (1, 2)] for l in (0, 1)]
        self.wqT = [[scr(f"w{j}qT_l{l}", [D, FF] if j == 1 else [FF, D])
                     for j in (1, 2)] for l in (0, 1)]
        self.k0scr = scr("k0scr", [256, SEQ])
        self.v0scr = scr("v0scr", [SEQ, 256])
        self.kin = scr("kin", [D + 1, TOK])
        self.kout = scr("kout", [2 * (D + 1), TOK])
        self.vin = scr("vin", [TOK, D])
        self.vout = scr("vout", [SEQ, D])
        self.hin = [scr(f"hin{l}", [2, FF], F32) for l in (0, 1)]
        self.hout = [scr(f"hout{l}", [2, FF], F32) for l in (0, 1)]
        self.trip = scr("trip", [16, 16], F32)
        self.tripq = scr("tripq", [2, TOK], F32)
        self.qscr = scr("qscr", [128, 1024])
        self.tripa = scr("tripa", [1, 1024], F32)
        self.triph = scr("triph", [1, 8], F32)

        def sb(name, shape, dt):
            t = self.stk.enter_context(nc.sbuf_tensor(name, shape, dt))
            self.bufs[name] = Buf(name)
            return t

        self.stream = [sb(f"stream{c}", [128, TOK], F32) for c in range(NCH)]
        self.xrow = sb("xrow", [128, D], F32)
        self.xrow2 = sb("xrow2", [128, D], F32)
        self.xt = [sb(f"xt{i}", [128, BLK], BF16) for i in range(NCH)]
        self.qp = [sb(f"qp{i}", [128, BLK], BF16) for i in range(NCH)]
        self.kpair = sb("kpair", [128, SEQ], BF16)
        self.vt = [sb(f"vt{i}", [128, NJC * 65], BF16) for i in range(2)]
        self.pt = [sb(f"pt{i}", [128, 1024], BF16) for i in range(2)]
        self.nrmp = sb("nrmp", [128, 1024], BF16)
        self.bcsb = sb("bcsb", [128, 1024], F32)
        self.ao = [sb(f"ao{i}", [128, BLK], BF16) for i in range(NCH)]
        self.hs = sb("hs", [128, NFH * FT], BF16)
        self.w1buf = [sb(f"w1buf{i}", [128, NCH * 128], BF16) for i in range(2)]
        self.w2buf = sb("w2buf", [128, NFH * 128], BF16)
        self.hx = [sb("hx0", [128, 514], F32), sb("hx1", [128, 514], F32)]
        self.cst = [sb("cst0", [128, 512], F32), sb("cst1", [128, 512], F32)]
        self.ysc = sb("ysc", [128, FT], F32)
        self.tsc = sb("tsc", [128, FT], F32)
        self.tsc2 = sb("tsc2", [128, 3584], BF16)
        self.saved2 = sb("saved2", [128, NFH * 2], F32)
        self.h0sv = sb("h0sv", [128, NFH * 2], F32)
        self.bpay = sb("bpay", [128, NFH * 2], F32)
        self.bhalo = sb("bhalo", [128, NFH * 2], F32)
        self.runmax = sb("runmax", [128, FT], BF16)
        self.runmin = sb("runmin", [128, FT], BF16)
        self.cwsb = sb("cwsb", [128, NFH * 4], F32)
        self.kssb = sb("kssb", [128, NCH], F32)
        self.mrow = sb("mrow", [128, 16], F32)
        self.rowsSS = sb("rowsSS", [1, 2 * TOK], F32)
        self.rowA = self.cst[0]
        self.bufs["rowA"] = self.bufs["cst0"]
        self.srow = sb("srow", [1, 64], F32)
        self.nacc = sb("nacc", [2, 16], F32)
        self.expb = sb("expb", [128, 8], F32)
        self.onesF = sb("onesF", [128, 128], F32)
        self.ones2b = sb("ones2b", [128, 2], BF16)
        self.ones1b = sb("ones1b", [128, 1], BF16)
        self.ident = sb("ident", [128, 128], F32)
        self.identb = sb("identb", [128, 128], BF16)
        self.ishif = sb("ishif", [128, 128], BF16)
        self.wsrow = sb("wsrow", [1, 16], F32)

        def ps(name, shape):
            t = self.stk.enter_context(nc.psum_tensor(name, shape, F32))
            self.bufs[name] = Buf(name)
            return t
        self.psA = [ps(f"psA{i}", [128, 1024]) for i in range(2)]
        self.psE = ps("psE", [128, 512])
        self.psO = ps("psO", [128, 512])
        self.psD = ps("psD", [128, 1024])

    def b(self, name):
        return self.bufs[name]

    def tap(self, name, ap, buf, dt=F32):
        if name not in self.want_taps:
            return
        sh = list(ap.shape)
        t = self.nc.dram_tensor("tap_" + name, sh, dt, kind="ExternalOutput").ap()
        self.p.add("sync", lambda e, t=t, ap=ap: e.dma_start(out=t, in_=ap),
                   reads=[buf], label="tap_" + name)



    def tguard(self):
        return self.p.add("tensor", lambda e: e.matmul(
            self.psD[0:1, 1022:1024], self.ones1b[0:1, 0:1],
            self.identb[0:1, 0:2], start=True, stop=True),
            reads=[self.b("ones1b"), self.b("identb")], writes=[])

    def bcast_row(self, src_ap, src_buf, dst_ap, dst_buf, n, lane="vector"):
        p = self.p
        p.add("tensor", lambda e: e.matmul(
            self.psD[:, 0:n] if n <= 512 else None,
            self.onesF[0:1, :], src_ap, start=True, stop=True),
            reads=[src_buf, self.b("onesF")], writes=[self.b("psD")])
        p.add(lane, lambda e: e.tensor_copy(dst_ap, self.psD[0:dst_ap.shape[0], 0:n]),
              reads=[self.b("psD")], writes=[dst_buf])

    def wpv(self, dc):
        return self.hs[:, 1024 * dc:1024 * (dc + 1)]

    def wov(self, dc):
        return self.hs[:, 8192 + 1024 * dc:8192 + 1024 * (dc + 1)]

    def build(self):
        nc, p = self.nc, self.p
        with nc.Block() as block:
            self.emit_consts()
            self.emit_input_stage()
            self.tap("st0", self.stream[0][:, 0:512], self.b("stream0"))
            self.emit_weight_prep(0)
            self.tap("wsrow", self.wsrow[:], self.b("wsrow"))
            self.emit_l0_kv()
            self.tap("nacc0", self.nacc[0:2, 0:4], self.b("nacc"))
            self.emit_attention(0)
            if self.stage >= 4:
                self.emit_weight_prep(1)
                self.emit_ffn(0)
            if self.stage >= 6:
                self.emit_l1_kv()
                self.emit_attention(1)
            if self.stage >= 8:
                self.emit_ffn(1)
            self.emit_output()
            emit_all(nc, block, p)
        self.stk.close()
        return nc

    def emit_consts(self):
        nc, p = self.nc, self.p
        for nm, base in (("ident", 0), ("identb", 0), ("ishif", 64)):
            t_ = getattr(self, nm)
            p.add("gpsimd", lambda e, t_=t_: e.memset(t_[:], 0.0),
                  writes=[self.b(nm)])
            p.add("gpsimd", lambda e, t_=t_, base=base: e.affine_select(
                out=t_[:], in_=t_[:], compare_op=ALU.not_equal, fill=1.0,
                base=base, pattern=[[-1, t_.shape[1]]], channel_multiplier=1),
                reads=[self.b(nm)], writes=[self.b(nm)])
        p.add("vector", lambda e: e.memset(self.onesF[:], 1.0), writes=[self.b("onesF")])
        p.add("vector", lambda e: e.memset(self.ones1b[:], 1.0), writes=[self.b("ones1b")])
        p.add("vector", lambda e: e.memset(self.ones2b[:], 0.0), writes=[self.b("ones2b")])
        p.add("vector", lambda e: e.memset(self.ones2b[0:64, 0:1], 1.0),
              writes=[self.b("ones2b")])
        p.add("vector", lambda e: e.memset(self.ones2b[64:128, 1:2], 1.0),
              writes=[self.b("ones2b")])
        p.add("vector", lambda e: e.memset(self.srow[0:1, 30:31], 1e-5),
              writes=[self.b("srow")])
        p.add("vector", lambda e: e.memset(self.mrow[:, 3:4], -MAGIC),
              writes=[self.b("mrow")])
        p.add("sbc", lambda e: e.dma_start(out=self.mrow[:, 0:2],
                                            in_=self.hmask.broadcast_to([128, 2])),
              writes=[self.b("mrow")])
        p.add("sbc", lambda e: e.dma_start(
            out=self.kssb[:], in_=self.kscale.rearrange("(c p) -> p c", p=128),
            allow_slow_non_contiguous=True),
            writes=[self.b("kssb")])
        for i in range(2):
            p.add("vector", lambda e, i=i: e.memset(
                self.vt[i][:].rearrange("p (c d) -> p c d", d=65)[:, :, 64:65], 1.0),
                writes=[self.b(f"vt{i}")])

    def emit_input_stage(self):
        nc, p = self.nc, self.p
        bxr = self.b("xrow")
        for si, (src, roff) in enumerate([(self.x_own, 0), (self.x_oth, TOK)]):
            for t in range(TOK // 128):
                p.add("sync", lambda e, src=src, t=t: e.dma_start(
                    out=self.xrow[:], in_=src[128 * t:128 * (t + 1), :]), writes=[bxr])
                p.add("vector", lambda e: e.tensor_copy(self.tsc2[:, 0:D], self.xrow[:]),
                      reads=[bxr], writes=[self.b("tsc2")])
                p.add("sdma2", lambda e, roff=roff, t=t: e.dma_start(
                    out=self.xb16[roff + 128 * t:roff + 128 * (t + 1), :],
                    in_=self.tsc2[:, 0:D]), reads=[self.b("tsc2")],
                    writes=[self.b("xb16")])
                if si == 0:
                    for c in range(NCH):
                        psv = self.psA[c % 2][:, 0:128]
                        bps = self.b(f"psA{c % 2}")
                        p.add("tensor", lambda e, c=c, psv=psv: _tr(e, 
                            psv, self.xrow[:, 128 * c:128 * (c + 1)], self.ident[:]),
                            reads=[bxr, self.b("ident")], writes=[bps])
                        p.add("vector", lambda e, c=c, t=t, psv=psv: e.tensor_copy(
                            self.stream[c][:, 128 * t:128 * (t + 1)], psv),
                            reads=[bps], writes=[self.b(f"stream{c}")])

    def emit_weight_prep(self, layer):
        nc, p = self.nc, self.p
        ws = [(self.w1_0, self.w1_1)[layer], (self.w2_0, self.w2_1)[layer]]
        for j, w in enumerate(ws):
            R, C = (FF, D) if j == 0 else (D, FF)
            nat, wt = self.wqnat[layer][j], self.wqT[layer][j]
            if layer == 0 and j == 0:
                self.tap_w1 = True
            bnat = self.b(f"w{j + 1}qnat_l{layer}")
            bwt = self.b(f"w{j + 1}qT_l{layer}")
            col = 2 * layer + j
            bxr = self.b("xrow")
            ncol = C // D
            first = True
            for r in range(R // 128):
                for cc in range(ncol):
                    p.add("swl", lambda e, w=w, r=r, cc=cc: e.dma_start(
                        out=self.xrow[:], in_=w[128 * r:128 * (r + 1),
                                                D * cc:D * (cc + 1)]), writes=[bxr])
                    p.add("vector", lambda e: e.reduce_sum(
                        self.tsc[:, 0:1], self.xrow[:], axis=AXX,
                        apply_absolute_value=True),
                        reads=[bxr], writes=[self.b("tsc")])
                    if first:
                        p.add("vector", lambda e: e.tensor_copy(
                            self.tsc[:, 1:2], self.tsc[:, 0:1]),
                            reads=[self.b("tsc")], writes=[self.b("tsc")])
                        first = False
                    else:
                        p.add("vector", lambda e: e.tensor_add(
                            self.tsc[:, 1:2], self.tsc[:, 1:2], self.tsc[:, 0:1]),
                            reads=[self.b("tsc")], writes=[self.b("tsc")])
            p.add("tensor", lambda e: _tr(e, self.psD[0:1, 0:128],
                                          self.tsc[:, 1:2], self.ident[:]),
                  reads=[self.b("tsc"), self.b("ident")], writes=[self.b("psD")])
            g = self.tguard()
            p.add("vector", lambda e: e.tensor_copy(self.ysc[0:1, 0:128],
                                                    self.psD[0:1, 0:128]),
                  reads=[self.b("psD")], writes=[self.b("ysc")], deps=[g])
            p.add("vector", lambda e: e.reduce_sum(self.tsc[0:1, 2:3],
                                                   self.ysc[0:1, 0:128], axis=AXX),
                  reads=[self.b("ysc")], writes=[self.b("tsc")])
            p.add("vector", lambda e, R=R, C=C, col=col: e.tensor_scalar(
                out=self.wsrow[0:1, 4 * col:4 * col + 1], in0=self.tsc[0:1, 2:3],
                scalar1=1.0 / (R * C), scalar2=1e-5, op0=ALU.mult, op1=ALU.max),
                reads=[self.b("tsc")], writes=[self.b("wsrow")])
            p.add("vector", lambda e, col=col: e.reciprocal(
                self.wsrow[0:1, 4 * col + 1:4 * col + 2],
                self.wsrow[0:1, 4 * col:4 * col + 1]),
                reads=[self.b("wsrow")], writes=[self.b("wsrow")])
            p.add("vector", lambda e, col=col: e.tensor_scalar(
                out=self.wsrow[0:1, 4 * col + 2:4 * col + 3],
                in0=self.wsrow[0:1, 4 * col:4 * col + 1],
                scalar1=1.0 / 127.0, scalar2=None, op0=ALU.mult),
                reads=[self.b("wsrow")], writes=[self.b("wsrow")])
            p.add("tensor", lambda e, col=col: e.matmul(
                self.psD[:, 0:2], self.onesF[0:1, :],
                self.wsrow[0:1, 4 * col + 1:4 * col + 3], start=True, stop=True),
                reads=[self.b("wsrow"), self.b("onesF")], writes=[self.b("psD")])
            p.add("vector", lambda e, col=col: e.tensor_copy(
                self.mrow[:, 8 + col:9 + col], self.psD[:, 0:1]),
                reads=[self.b("psD")], writes=[self.b("mrow")])
            p.add("vector", lambda e, col=col: e.tensor_copy(
                self.mrow[:, 12 + col:13 + col], self.psD[:, 1:2]),
                reads=[self.b("psD")], writes=[self.b("mrow")])
            for r in range(R // 128):
                for cc in range(ncol):
                    idx = r * ncol + cc
                    xr = self.xrow if idx % 2 == 0 else self.xrow2
                    bx = bxr if idx % 2 == 0 else self.b("xrow2")
                    p.add("swl", lambda e, w=w, r=r, cc=cc, xr=xr: e.dma_start(
                        out=xr[:], in_=w[128 * r:128 * (r + 1),
                                         D * cc:D * (cc + 1)]), writes=[bx])
                    p.add("vector", lambda e, col=col, xr=xr: e.tensor_scalar(
                        out=xr[:], in0=xr[:],
                        scalar1=self.mrow[:, 8 + col:9 + col], scalar2=1.25,
                        op0=ALU.mult, op1=ALU.min),
                        reads=[bx, self.b("mrow")], writes=[bx])
                    p.add("vector", lambda e, xr=xr: e.tensor_scalar(
                        out=xr[:], in0=xr[:], scalar1=-1.25,
                        scalar2=MAGIC, op0=ALU.max, op1=ALU.add),
                        reads=[bx], writes=[bx])
                    p.add("scalar", lambda e, cc=cc, xr=xr: e.activation(
                        self.tsc2[:, D * (cc % 2):D * (cc % 2 + 1)], xr[:],
                        AF.Copy, bias=-MAGIC),
                        reads=[bx], writes=[self.b("tsc2")])
                    p.add("sdma2", lambda e, nat=nat, r=r, cc=cc: e.dma_start(
                        out=nat[128 * r:128 * (r + 1), D * cc:D * (cc + 1)],
                        in_=self.tsc2[:, D * (cc % 2):D * (cc % 2 + 1)]),
                        reads=[self.b("tsc2")], writes=[bnat])
            for cc in range(C // 128):
                for rh in range(0, R, 2048):
                    rw = min(2048, R - rh)
                    p.add("swt", lambda e, nat=nat, cc=cc, rh=rh, rw=rw:
                          e.dma_start(
                        out=self.tsc2[:, 0:rw],
                        in_=nat[rh:rh + rw, 128 * cc:128 * (cc + 1)],
                        transpose=True), reads=[bnat], writes=[self.b("tsc2")])
                    p.add("sdma2", lambda e, wt=wt, cc=cc, rh=rh, rw=rw:
                          e.dma_start(
                        out=wt[128 * cc:128 * (cc + 1), rh:rh + rw],
                        in_=self.tsc2[:, 0:rw]),
                        reads=[self.b("tsc2")], writes=[bwt])

    def load_proj_weight(self, w, into="wp"):
        p = self.p
        Fout = w.shape[1]
        view = self.wpv if into == "wp" else self.wov
        for dc in range(NCH):
            p.add("gdma", lambda e, w=w, dc=dc, Fout=Fout, view=view: e.dma_start(
                out=view(dc)[:, 0:Fout], in_=w[128 * dc:128 * (dc + 1), :]),
                writes=[self.b("hs")])

    def kv_norm_acc(self, ncol, first):
        p = self.p
        p.add("vector", lambda e: e.tensor_mul(
            self.tsc2[:, 2048:2560], self.tsc2[:, 0:512], self.tsc2[:, 0:512]),
            reads=[self.b("tsc2")], writes=[self.b("tsc2")])
        p.add("tensor", lambda e: e.matmul(self.psD[0:2, 0:512], self.ones2b[:],
                                           self.tsc2[:, 2048:2560],
                                           start=True, stop=True),
              reads=[self.b("tsc2"), self.b("ones2b")], writes=[self.b("psD")])
        p.add("vector", lambda e: e.tensor_copy(self.tsc2[0:2, 2560:3072],
                                                self.psD[0:2, 0:512]),
              reads=[self.b("psD")], writes=[self.b("tsc2")])
        p.add("vector", lambda e: e.reduce_max(self.tsc[0:2, 3:4],
                                               self.tsc2[0:2, 2560:3072], axis=AXX),
              reads=[self.b("tsc2")], writes=[self.b("tsc")])
        if first:
            p.add("vector", lambda e, ncol=ncol: e.tensor_copy(
                self.nacc[0:2, ncol:ncol + 1], self.tsc[0:2, 3:4]),
                reads=[self.b("tsc")], writes=[self.b("nacc")])
        else:
            p.add("vector", lambda e, ncol=ncol: e.tensor_max(
                self.nacc[0:2, ncol:ncol + 1], self.nacc[0:2, ncol:ncol + 1],
                self.tsc[0:2, 3:4]),
                reads=[self.b("tsc")], writes=[self.b("nacc")])

    def flatten2_max(self, src_ap, src_buf, dst_ap, trip_slot):
        p = self.p
        p.add("tensor", lambda e, src_ap=src_ap: _tr(
            e, self.psD[0:1, 0:2], src_ap, self.ident[0:2, 0:2]),
            reads=[src_buf, self.b("ident")], writes=[self.b("psD")])
        g = self.tguard()
        p.add("vector", lambda e: e.tensor_copy(self.srow[0:1, 8:10],
                                                self.psD[0:1, 0:2]),
            reads=[self.b("psD")], writes=[self.b("srow")], deps=[g])
        p.add("vector", lambda e, dst_ap=dst_ap: e.reduce_max(
            dst_ap, self.srow[0:1, 8:10], axis=AXX),
            reads=[self.b("srow")], writes=[self.b("srow")])

    def emit_l0_kv(self):
        nc, p = self.nc, self.p
        self.load_proj_weight(self.wkv0)
        first_n = True
        for t in range(SEQ // BLK):
            for dc in range(NCH):
                p.add("sync", lambda e, t=t, dc=dc: e.dma_start(
                    out=self.xt[dc][:], in_=self.xb16[BLK * t:BLK * (t + 1),
                                                      128 * dc:128 * (dc + 1)],
                    transpose=True), reads=[self.b("xb16")], writes=[self.b(f"xt{dc}")])
            for kc in range(2):
                psv = self.psA[0][:, 512 * kc:512 * (kc + 1)]
                bps = self.b("psA0")
                for dc in range(NCH):
                    p.add("tensor", lambda e, kc=kc, dc=dc, psv=psv: e.matmul(
                        psv, self.wpv(dc)[:, 128 * kc:128 * (kc + 1)], self.xt[dc][:],
                        start=(dc == 0), stop=(dc == NCH - 1)),
                        reads=[self.b("hs"), self.b(f"xt{dc}")], writes=[bps])
                p.add("vector", lambda e, psv=psv: e.tensor_copy(
                    self.tsc2[:, 0:512], psv), reads=[bps], writes=[self.b("tsc2")])
                p.add("sdma2", lambda e, kc=kc, t=t: e.dma_start(
                    out=self.k0scr[128 * kc:128 * (kc + 1), BLK * t:BLK * (t + 1)],
                    in_=self.tsc2[:, 0:512]), reads=[self.b("tsc2")],
                    writes=[self.b("k0scr")])
                self.kv_norm_acc(0, first_n)
                first_n = False
            for s in range(BLK // 128):
                psv = self.psA[1][:, 0:256]
                bps = self.b("psA1")
                for dc in range(NCH):
                    p.add("tensor", lambda e, s=s, dc=dc, psv=psv: e.matmul(
                        psv, self.xt[dc][:, 128 * s:128 * (s + 1)],
                        self.wpv(dc)[:, 256:512], start=(dc == 0),
                        stop=(dc == NCH - 1)),
                        reads=[self.b("hs"), self.b(f"xt{dc}")], writes=[bps])
                p.add("vector", lambda e, psv=psv: e.tensor_copy(
                    self.tsc2[:, 1024:1280], psv), reads=[bps], writes=[self.b("tsc2")])
                p.add("sdma2", lambda e, t=t, s=s: e.dma_start(
                    out=self.v0scr[BLK * t + 128 * s:BLK * t + 128 * (s + 1), :],
                    in_=self.tsc2[:, 1024:1280]), reads=[self.b("tsc2")],
                    writes=[self.b("v0scr")])
        self.flatten2_max(self.nacc[0:2, 0:1], self.b("nacc"), self.srow[0:1, 1:2], 0)

    def emit_l1_kv(self):
        nc, p = self.nc, self.p
        p.add("sbc", lambda e: e.dma_start(out=self.srow[0:1, 16:20], in_=self.qw1),
              writes=[self.b("srow")])
        p.add("scalar", lambda e: e.activation(self.srow[0:1, 20:24],
                                               self.srow[0:1, 16:20], AF.Exp),
              reads=[self.b("srow")], writes=[self.b("srow")])
        p.add("vector", lambda e: e.reduce_sum(self.srow[0:1, 24:25],
                                               self.srow[0:1, 20:24], axis=AXX),
              reads=[self.b("srow")], writes=[self.b("srow")])
        p.add("vector", lambda e: e.reciprocal(self.srow[0:1, 25:26],
                                               self.srow[0:1, 24:25]),
              reads=[self.b("srow")], writes=[self.b("srow")])
        p.add("vector", lambda e: e.tensor_tensor(
            self.srow[0:1, 26:30], self.srow[0:1, 20:24],
            self.srow[0:1, 25:26].broadcast_to([1, 4]), ALU.mult),
            reads=[self.b("srow")], writes=[self.b("srow")])
        p.add("tensor", lambda e: e.matmul(
            self.psD[:, 0:4], self.onesF[0:1, :], self.srow[0:1, 26:30],
            start=True, stop=True),
            reads=[self.b("srow"), self.b("onesF")], writes=[self.b("psD")])
        p.add("vector", lambda e: e.tensor_copy(self.mrow[:, 4:8],
                                                self.psD[:, 0:4]),
            reads=[self.b("psD")], writes=[self.b("mrow")])
        for dc in range(NCH):
            bxr = self.b("xrow")
            p.add("sync", lambda e, dc=dc: e.dma_start(
                out=self.xrow[:], in_=self.wk1[128 * dc:128 * (dc + 1), :]),
                writes=[bxr])
            A = self.xrow[:].rearrange("p (h two c) -> p h two c", two=2, c=32)
            W = self.wpv(dc).rearrange("p (h two c) -> p h two c", two=2, c=32)
            T1 = self.tsc[:, 0:512].rearrange("p (h one c) -> p h one c",
                                              one=1, c=32)
            p.add("vector", lambda e, A=A, T1=T1: e.tensor_scalar(
                out=T1, in0=A[:, :, 0:1, :], scalar1=self.mrow[:, 4:5], scalar2=None,
                op0=ALU.mult), reads=[bxr, self.b("mrow")], writes=[self.b("tsc")])
            p.add("vector", lambda e, A=A, W=W, T1=T1: e.scalar_tensor_tensor(
                out=W[:, :, 0:1, :], in0=A[:, :, 1:2, :], scalar=self.mrow[:, 5:6],
                in1=T1, op0=ALU.mult, op1=ALU.add),
                reads=[bxr, self.b("mrow"), self.b("tsc")], writes=[self.b("hs")])
            p.add("vector", lambda e, A=A, T1=T1: e.tensor_scalar(
                out=T1, in0=A[:, :, 0:1, :], scalar1=self.mrow[:, 6:7], scalar2=None,
                op0=ALU.mult), reads=[bxr, self.b("mrow")], writes=[self.b("tsc")])
            p.add("vector", lambda e, A=A, W=W, T1=T1: e.scalar_tensor_tensor(
                out=W[:, :, 1:2, :], in0=A[:, :, 1:2, :], scalar=self.mrow[:, 7:8],
                in1=T1, op0=ALU.mult, op1=ALU.add),
                reads=[bxr, self.b("mrow"), self.b("tsc")], writes=[self.b("hs")])
        first_n = True
        for t in range(NBLK):
            for dc in range(NCH):
                p.add("vector", lambda e, t=t, dc=dc: e.tensor_copy(
                    self.xt[dc][:], self.stream[dc][:, BLK * t:BLK * (t + 1)]),
                    reads=[self.b(f"stream{dc}")], writes=[self.b(f"xt{dc}")])
            for fc in range(NCH):
                psv = self.psA[0][:, 0:512]
                bps = self.b("psA0")
                for dc in range(NCH):
                    p.add("tensor", lambda e, fc=fc, dc=dc, psv=psv: e.matmul(
                        psv, self.wpv(dc)[:, 128 * fc:128 * (fc + 1)], self.xt[dc][:],
                        start=(dc == 0), stop=(dc == NCH - 1)),
                        reads=[self.b("hs"), self.b(f"xt{dc}")], writes=[bps])
                p.add("vector", lambda e, psv=psv: e.tensor_copy(self.tsc2[:, 0:512], psv),
                      reads=[bps], writes=[self.b("tsc2")])
                p.add("sdma2", lambda e, fc=fc, t=t: e.dma_start(
                    out=self.kin[128 * fc:128 * (fc + 1), BLK * t:BLK * (t + 1)],
                    in_=self.tsc2[:, 0:512]), reads=[self.b("tsc2")],
                    writes=[self.b("kin")])
                self.kv_norm_acc(1, first_n)
                first_n = False
        self.flatten2_max(self.nacc[0:2, 1:2], self.b("nacc"), self.srow[0:1, 2:3], 1)
        p.add("vector", lambda e: e.tensor_copy(self.tsc2[0:1, 0:1], self.srow[0:1, 2:3]),
              reads=[self.b("srow")], writes=[self.b("tsc2")])
        p.add("stw", lambda e: e.dma_start(out=self.kin[D:D + 1, 0:1],
                                             in_=self.tsc2[0:1, 0:1]),
              reads=[self.b("tsc2")], writes=[self.b("kin")])
        self.load_proj_weight(self.wv1)
        for t in range(NBLK):
            for dc in range(NCH):
                p.add("vector", lambda e, t=t, dc=dc: e.tensor_copy(
                    self.xt[dc][:], self.stream[dc][:, BLK * t:BLK * (t + 1)]),
                    reads=[self.b(f"stream{dc}")], writes=[self.b(f"xt{dc}")])
            for s in range(BLK // 128):
                for half in range(2):
                    psv = self.psA[1][:, 512 * half:512 * (half + 1)]
                    bps = self.b("psA1")
                    for dc in range(NCH):
                        p.add("tensor", lambda e, s=s, dc=dc, half=half, psv=psv:
                              e.matmul(
                            psv, self.xt[dc][:, 128 * s:128 * (s + 1)],
                            self.wpv(dc)[:, 512 * half:512 * (half + 1)],
                            start=(dc == 0), stop=(dc == NCH - 1)),
                            reads=[self.b("hs"), self.b(f"xt{dc}")], writes=[bps])
                p.add("vector", lambda e: e.tensor_copy(self.tsc2[:, 0:1024],
                                                        self.psA[1][:]),
                      reads=[self.b("psA1")], writes=[self.b("tsc2")])
                p.add("sdma2", lambda e, t=t, s=s: e.dma_start(
                    out=self.vin[BLK * t + 128 * s:BLK * t + 128 * (s + 1), :],
                    in_=self.tsc2[:, 0:1024]), reads=[self.b("tsc2")],
                    writes=[self.b("vin")])
        p.add("gpsimd", lambda e: e.collective_compute(
            "AllGather", ALU.bypass, replica_groups=GROUPS,
            ins=[self.kin], outs=[self.kout]),
            reads=[self.b("kin")], writes=[self.b("kout")])
        p.add("gpsimd", lambda e: e.collective_compute(
            "AllGather", ALU.bypass, replica_groups=GROUPS,
            ins=[self.vin], outs=[self.vout]),
            reads=[self.b("vin")], writes=[self.b("vout")])
        p.add("sbc", lambda e: e.dma_start(out=self.tsc2[0:1, 4:5],
                                            in_=self.kout[D:D + 1, 0:1]),
              reads=[self.b("kout")], writes=[self.b("tsc2")])
        p.add("sbc", lambda e: e.dma_start(out=self.tsc2[0:1, 5:6],
                                            in_=self.kout[2 * D + 1:2 * D + 2, 0:1]),
              reads=[self.b("kout")], writes=[self.b("tsc2")])
        p.add("vector", lambda e: e.tensor_copy(self.srow[0:1, 8:10],
                                                self.tsc2[0:1, 4:6]),
              reads=[self.b("tsc2")], writes=[self.b("srow")])
        p.add("vector", lambda e: e.reduce_max(self.srow[0:1, 2:3],
                                               self.srow[0:1, 8:10], axis=AXX),
              reads=[self.b("srow")], writes=[self.b("srow")])

    def emit_attention(self, layer):
        nc, p = self.nc, self.p
        scale = 0.125 if layer == 0 else float(np.float32(32.0 ** -0.5))
        wq = self.wq0 if layer == 0 else self.wq1
        self.load_proj_weight(wq, into="wp")
        if layer == 1:
            self.load_proj_weight(self.wo1, into="wo")
        for b_ in range(NBLK):
            self.emit_qproj(layer, b_, scale)
            if layer == 0 and b_ == 0:
                self.tap("srowA", self.srow[0:1, 0:16], self.b("srow"))
                self.tap("expb0", self.expb[:, 0:4], self.b("expb"))
                self.tap("qpA", self.qp[0][:], self.b("qp0"), BF16)
            pending = None
            for P in range(NH // 2):
                pending = self.emit_pair(layer, b_, P, scale, pending_tail=pending)
                if layer == 0 and b_ == 0 and P == 0:
                    self.tap("ao0", self.ao[0][:], self.b("ao0"), BF16)
            pending()
            if layer == 1:
                self.emit_oproj(b_)
            else:
                for fc in range(NCH):
                    p.add("vector", lambda e, fc=fc, b_=b_: e.tensor_tensor(
                        self.stream[fc][:, BLK * b_:BLK * (b_ + 1)],
                        self.stream[fc][:, BLK * b_:BLK * (b_ + 1)],
                        self.ao[fc][:], ALU.add),
                        reads=[self.b(f"ao{fc}")], writes=[self.b(f"stream{fc}")])
        self.emit_ln_full()

    def emit_qproj(self, layer, b_, scale):
        nc, p = self.nc, self.p
        for dc in range(NCH):
            if layer == 0:
                p.add("sync", lambda e, b_=b_, dc=dc: e.dma_start(
                    out=self.xt[dc][:], in_=self.xb16[BLK * b_:BLK * (b_ + 1),
                                                      128 * dc:128 * (dc + 1)],
                    transpose=True), reads=[self.b("xb16")], writes=[self.b(f"xt{dc}")])
            else:
                p.add("vector", lambda e, b_=b_, dc=dc: e.tensor_copy(
                    self.xt[dc][:], self.stream[dc][:, BLK * b_:BLK * (b_ + 1)]),
                    reads=[self.b(f"stream{dc}")], writes=[self.b(f"xt{dc}")])
        first = True
        for fc in range(NCH):
            psv = self.psA[0][:, 0:512]
            bps = self.b("psA0")
            for dc in range(NCH):
                p.add("tensor", lambda e, fc=fc, dc=dc, psv=psv: e.matmul(
                    psv, self.wpv(dc)[:, 128 * fc:128 * (fc + 1)], self.xt[dc][:],
                    start=(dc == 0), stop=(dc == NCH - 1)),
                    reads=[self.b("hs"), self.b(f"xt{dc}")], writes=[bps])
            if layer == 0:
                p.add("vector", lambda e, fc=fc, psv=psv: e.tensor_scalar(
                    out=self.qp[fc][:], in0=psv, scalar1=self.kssb[:, fc:fc + 1],
                    scalar2=None, op0=ALU.mult),
                    reads=[bps, self.b("kssb")], writes=[self.b(f"qp{fc}")])
            else:
                p.add("vector", lambda e, fc=fc, psv=psv: e.tensor_copy(
                    self.qp[fc][:], psv), reads=[bps], writes=[self.b(f"qp{fc}")])
            p.add("vector", lambda e, fc=fc: e.tensor_mul(
                self.tsc2[:, 2048:2560], self.qp[fc][:], self.qp[fc][:]),
                reads=[self.b(f"qp{fc}")], writes=[self.b("tsc2")])
            p.add("tensor", lambda e: e.matmul(self.psD[0:2, 512:1024], self.ones2b[:],
                                               self.tsc2[:, 2048:2560],
                                               start=True, stop=True),
                  reads=[self.b("tsc2"), self.b("ones2b")], writes=[self.b("psD")])
            p.add("vector", lambda e: e.tensor_copy(self.tsc2[0:2, 2560:3072],
                                                    self.psD[0:2, 512:1024]),
                  reads=[self.b("psD")], writes=[self.b("tsc2")])
            p.add("vector", lambda e: e.reduce_max(self.tsc[0:2, 2:3],
                                                   self.tsc2[0:2, 2560:3072], axis=AXX),
                  reads=[self.b("tsc2")], writes=[self.b("tsc")])
            if layer == 0 and b_ == 0:
                self.tap(f"cp{fc}", self.tsc2[0:2, 2560:3072], self.b("tsc2"), BF16)
                self.tap(f"ts{fc}", self.tsc[0:2, 2:3], self.b("tsc"))
            if first:
                p.add("vector", lambda e: e.tensor_copy(self.nacc[0:2, 2:3],
                                                        self.tsc[0:2, 2:3]),
                      reads=[self.b("tsc")], writes=[self.b("nacc")])
                first = False
            else:
                p.add("vector", lambda e: e.tensor_max(
                    self.nacc[0:2, 2:3], self.nacc[0:2, 2:3], self.tsc[0:2, 2:3]),
                    reads=[self.b("tsc")], writes=[self.b("nacc")])
        if layer == 0 and b_ == 0:
            self.tap("naccq", self.nacc[0:2, 2:3], self.b("nacc"))
        self.flatten2_max(self.nacc[0:2, 2:3], self.b("nacc"), self.srow[0:1, 3:4], 2)
        kcol = 1 if layer == 0 else 2
        p.add("vector", lambda e, kcol=kcol: e.tensor_mul(
            self.srow[0:1, 4:5], self.srow[0:1, 3:4], self.srow[0:1, kcol:kcol + 1]),
            reads=[self.b("srow")], writes=[self.b("srow")])
        p.add("scalar", lambda e: e.activation(self.srow[0:1, 5:6], self.srow[0:1, 4:5],
                                               AF.Ln),
              reads=[self.b("srow")], writes=[self.b("srow")])
        p.add("vector", lambda e, scale=scale: e.memset(self.srow[0:1, 6:7],
                                                        float(np.log(scale))),
              writes=[self.b("srow")])
        p.add("vector", lambda e: e.tensor_scalar(
            out=self.srow[0:1, 7:8], in0=self.srow[0:1, 5:6], scalar1=0.5,
            scalar2=self.srow[0:1, 6:7], op0=ALU.mult, op1=ALU.add),
            reads=[self.b("srow")], writes=[self.b("srow")])
        p.add("scalar", lambda e: e.activation(self.srow[0:1, 12:13],
                                               self.srow[0:1, 7:8], AF.Exp),
              reads=[self.b("srow")], writes=[self.b("srow")])
        p.add("vector", lambda e: e.tensor_scalar(
            out=self.srow[0:1, 13:14], in0=self.srow[0:1, 12:13], scalar1=-1.0,
            scalar2=EXPSHIFT, op0=ALU.mult, op1=ALU.add),
            reads=[self.b("srow")], writes=[self.b("srow")])
        col = 4 * layer + b_
        p.add("tensor", lambda e: e.matmul(
            self.psD[:, 0:1], self.onesF[0:1, :], self.srow[0:1, 13:14],
            start=True, stop=True),
            reads=[self.b("srow"), self.b("onesF")], writes=[self.b("psD")])
        p.add("vector", lambda e, col=col: e.tensor_copy(
            self.expb[:, col:col + 1], self.psD[:, 0:1]),
            reads=[self.b("psD")], writes=[self.b("expb")])

    def emit_pair(self, layer, b_, P, scale, pending_tail=None):
        nc, p = self.nc, self.p
        h0, h1 = 2 * P, 2 * P + 1
        bk = self.b("kpair")
        bv0, bv1 = self.b("vt0"), self.b("vt1")
        vv0 = self.vt[0][:].rearrange("p (c d) -> p c d", d=65)
        vv1 = self.vt[1][:].rearrange("p (c d) -> p c d", d=65)
        if layer == 0:
            kv = 64 * (h0 // 4)
            if P % 2 == 0:
                p.add("svt", lambda e, kv=kv: e.dma_start(
                    out=self.kpair[0:64, :], in_=self.k0scr[kv:kv + 64, :]),
                    reads=[self.b("k0scr")], writes=[bk])
                p.add("svt", lambda e, kv=kv: e.dma_start(
                    out=self.kpair[64:128, :], in_=self.k0scr[kv:kv + 64, :]),
                    reads=[self.b("k0scr")], writes=[bk])
                p.add("svt", lambda e, kv=kv, vv0=vv0: e.dma_start(
                    out=vv0[:, :, 0:64],
                    in_=self.v0scr.rearrange("(c p) f -> p c f", p=128)[
                        :, :, kv:kv + 64]),
                    reads=[self.b("v0scr")], writes=[bv0])
                p.add("svt", lambda e, kv=kv, vv1=vv1: e.dma_start(
                    out=vv1[:, :, 0:64],
                    in_=self.v0scr.rearrange("(c p) f -> p c f", p=128)[
                        :, :, kv:kv + 64]),
                    reads=[self.b("v0scr")], writes=[bv1])
        else:
            for half, h in ((0, h0), (1, h1)):
                prows = slice(64 * half, 64 * half + 64)
                p.add("sync", lambda e, h=h, prows=prows: e.dma_start(
                    out=self.kpair[prows, 0:TOK],
                    in_=self.kout[64 * h:64 * h + 64, :]),
                    reads=[self.b("kout")], writes=[bk])
                p.add("svt", lambda e, h=h, prows=prows: e.dma_start(
                    out=self.kpair[prows, TOK:SEQ],
                    in_=self.kout[D + 1 + 64 * h:D + 1 + 64 * h + 64, :]),
                    reads=[self.b("kout")], writes=[bk])
            p.add("svt", lambda e, h0=h0, vv0=vv0: e.dma_start(
                out=vv0[:, :, 0:64],
                in_=self.vout.rearrange("(c p) f -> p c f", p=128)[
                    :, :, 64 * h0:64 * h0 + 64]),
                reads=[self.b("vout")], writes=[bv0])
            p.add("svt", lambda e, h1=h1, vv1=vv1: e.dma_start(
                out=vv1[:, :, 0:64],
                in_=self.vout.rearrange("(c p) f -> p c f", p=128)[
                    :, :, 64 * h1:64 * h1 + 64]),
                reads=[self.b("vout")], writes=[bv1])
        col = 4 * layer + b_
        qbuf = self.qp[P]
        bq = self.b(f"qp{P}")
        bexp = self.b("expb")
        bpsE, bpsO = self.b("psE"), self.b("psO")
        def emit_score(c):
            psA = self.psA[c % 2]
            bpsA = self.b(f"psA{c % 2}")
            p.add("tensor", lambda e, c=c, psA=psA, qbuf=qbuf: e.matmul(
                psA[:, 0:512], self.kpair[0:64, 128 * c:128 * (c + 1)], qbuf[0:64, :],
                start=True, stop=True, tile_position=(0, 0)),
                reads=[bk, bq], writes=[bpsA])
            p.add("tensor", lambda e, c=c, psA=psA, qbuf=qbuf: e.matmul(
                psA[:, 512:1024], self.kpair[64:128, 128 * c:128 * (c + 1)],
                qbuf[64:128, :], start=True, stop=True, tile_position=(64, 0)),
                reads=[bk, bq], writes=[bpsA])

        emit_score(0)
        for c in range(NJC):
            if c + 1 < NJC:
                emit_score(c + 1)
            if c == 0 and pending_tail is not None:
                pending_tail()
                pending_tail = None
            psA = self.psA[c % 2]
            bpsA = self.b(f"psA{c % 2}")
            ptb = self.pt[c % 2]
            bpt = self.b(f"pt{c % 2}")
            p.add("scalar", lambda e, psA=psA, ptb=ptb, col=col, scale=scale:
                  e.activation(
                ptb[:], psA[:], AF.Exp, bias=self.expb[:, col:col + 1], scale=scale),
                reads=[bpsA, bexp], writes=[bpt])
            p.add("tensor", lambda e, c=c, ptb=ptb, vv0=vv0: e.matmul(
                self.psE[0:65, :], vv0[:, c, :], ptb[:, 0:512],
                start=(c == 0), stop=(c == NJC - 1)),
                reads=[bpt, bv0], writes=[bpsE])
            p.add("tensor", lambda e, c=c, ptb=ptb, vv1=vv1: e.matmul(
                self.psO[0:65, :], vv1[:, c, :], ptb[:, 512:1024],
                start=(c == 0), stop=(c == NJC - 1)),
                reads=[bpt, bv1], writes=[bpsO])
        if pending_tail is not None:
            pending_tail()

        def tail():
            p.add("vector", lambda e: e.reciprocal(self.bcsb[64:65, 0:512],
                                                   self.psE[64:65, :]),
                  reads=[bpsE], writes=[self.b("bcsb")])
            p.add("vector", lambda e: e.reciprocal(self.bcsb[64:65, 512:1024],
                                                   self.psO[64:65, :]),
                  reads=[bpsO], writes=[self.b("bcsb")])
            p.add("tensor", lambda e: e.matmul(
                self.psD[0:64, 0:512], self.onesF[64:65, 0:64],
                self.bcsb[64:65, 0:512],
                start=True, stop=True, tile_position=(64, 0)),
                reads=[self.b("bcsb"), self.b("onesF")], writes=[self.b("psD")])
            p.add("tensor", lambda e: e.matmul(
                self.psD[0:64, 512:1024], self.onesF[64:65, 0:64],
                self.bcsb[64:65, 512:1024],
                start=True, stop=True, tile_position=(64, 0)),
                reads=[self.b("bcsb"), self.b("onesF")], writes=[self.b("psD")])
            p.add("vector", lambda e: e.tensor_copy(self.bcsb[0:64, 0:1024],
                                                    self.psD[0:64, 0:1024]),
                  reads=[self.b("psD")], writes=[self.b("bcsb")])
            p.add("vector", lambda e: e.tensor_tensor(
                self.nrmp[0:64, 0:512], self.psE[0:64, :], self.bcsb[0:64, 0:512],
                ALU.mult),
                reads=[bpsE, self.b("bcsb")], writes=[self.b("nrmp")])
            p.add("vector", lambda e: e.tensor_tensor(
                self.nrmp[0:64, 512:1024], self.psO[0:64, :],
                self.bcsb[0:64, 512:1024],
                ALU.mult),
                reads=[bpsO, self.b("bcsb")], writes=[self.b("nrmp")])
            p.add("tensor", lambda e: e.matmul(self.psD[:, 0:512],
                                               self.identb[0:64, :],
                                               self.nrmp[0:64, 0:512],
                                               start=True, stop=False),
                  reads=[self.b("nrmp"), self.b("identb")], writes=[self.b("psD")])
            p.add("tensor", lambda e: e.matmul(self.psD[:, 0:512],
                                               self.ishif[0:64, :],
                                               self.nrmp[0:64, 512:1024],
                                               start=False, stop=True),
                  reads=[self.b("nrmp"), self.b("ishif")], writes=[self.b("psD")])
            p.add("vector", lambda e, P=P: e.tensor_copy(self.ao[P][:],
                                                         self.psD[:, 0:512]),
                  reads=[self.b("psD")], writes=[self.b(f"ao{P}")])
        return tail

    def emit_oproj(self, b_):
        nc, p = self.nc, self.p
        for fc in range(NCH):
            psv = self.psA[0][:, 0:512]
            bps = self.b("psA0")
            for dc in range(NCH):
                p.add("tensor", lambda e, fc=fc, dc=dc, psv=psv: e.matmul(
                    psv, self.wov(dc)[:, 128 * fc:128 * (fc + 1)], self.ao[dc][:],
                    start=(dc == 0), stop=(dc == NCH - 1)),
                    reads=[self.b("hs"), self.b(f"ao{dc}")], writes=[bps])
            p.add("vector", lambda e, fc=fc, b_=b_, psv=psv: e.tensor_tensor(
                self.stream[fc][:, BLK * b_:BLK * (b_ + 1)],
                self.stream[fc][:, BLK * b_:BLK * (b_ + 1)], psv, ALU.add),
                reads=[bps], writes=[self.b(f"stream{fc}")])

    def emit_ln_full(self):
        nc, p = self.nc, self.p
        for tt in range(NBLK):
            sl = slice(BLK * tt, BLK * (tt + 1))
            for fc in range(NCH):
                p.add("vector", lambda e, fc=fc, sl=sl: e.tensor_copy(
                    self.tsc2[:, 0:512], self.stream[fc][:, sl]),
                    reads=[self.b(f"stream{fc}")], writes=[self.b("tsc2")])
                p.add("vector", lambda e, fc=fc, sl=sl: e.tensor_mul(
                    self.tsc2[:, 2048:2560], self.stream[fc][:, sl],
                    self.stream[fc][:, sl]),
                    reads=[self.b(f"stream{fc}")], writes=[self.b("tsc2")])
                p.add("tensor", lambda e, fc=fc: e.matmul(
                    self.psD[0:1, 0:512], self.ones1b[:], self.tsc2[:, 0:512],
                    start=(fc == 0), stop=(fc == NCH - 1)),
                    reads=[self.b("tsc2"), self.b("ones1b")], writes=[self.b("psD")])
                p.add("tensor", lambda e, fc=fc: e.matmul(
                    self.psD[0:1, 512:1024], self.ones1b[:], self.tsc2[:, 2048:2560],
                    start=(fc == 0), stop=(fc == NCH - 1)),
                    reads=[self.b("tsc2"), self.b("ones1b")], writes=[self.b("psD")])
            p.add("vector", lambda e, sl=sl: e.tensor_copy(
                self.rowsSS[0:1, sl], self.psD[0:1, 0:512]),
                reads=[self.b("psD")], writes=[self.b("rowsSS")])
            p.add("vector", lambda e, tt=tt: e.tensor_copy(
                self.rowsSS[0:1, TOK + BLK * tt:TOK + BLK * (tt + 1)],
                self.psD[0:1, 512:1024]),
                reads=[self.b("psD")], writes=[self.b("rowsSS")])
        self.tap("rows0", self.rowsSS[0:1, :], self.b("rowsSS"))
        self.emit_ln_apply()

    def emit_ln_apply(self):
        nc, p = self.nc, self.p
        p.add("vector", lambda e: e.tensor_scalar(
            out=self.rowsSS[0:1, 0:TOK], in0=self.rowsSS[0:1, 0:TOK],
            scalar1=1.0 / D, scalar2=None, op0=ALU.mult),
            reads=[self.b("rowsSS")], writes=[self.b("rowsSS")])
        p.add("vector", lambda e: e.tensor_scalar(
            out=self.rowsSS[0:1, TOK:2 * TOK], in0=self.rowsSS[0:1, TOK:2 * TOK],
            scalar1=1.0 / D, scalar2=None, op0=ALU.mult),
            reads=[self.b("rowsSS")], writes=[self.b("rowsSS")])
        for tt in range(NBLK):
            sl = slice(BLK * tt, BLK * (tt + 1))
            sl2 = slice(TOK + BLK * tt, TOK + BLK * (tt + 1))
            p.add("vector", lambda e, sl=sl: e.tensor_mul(
                self.rowA[0:1, :], self.rowsSS[0:1, sl], self.rowsSS[0:1, sl]),
                reads=[self.b("rowsSS")], writes=[self.b("rowA")])
            p.add("vector", lambda e, sl2=sl2: e.tensor_sub(
                self.rowsSS[0:1, sl2], self.rowsSS[0:1, sl2], self.rowA[0:1, :]),
                reads=[self.b("rowA")], writes=[self.b("rowsSS")])
            p.add("scalar", lambda e, sl2=sl2: e.activation(
                self.rowsSS[0:1, sl2], self.rowsSS[0:1, sl2], AF.Ln,
                bias=self.srow[0:1, 30:31]),
                reads=[self.b("srow")], writes=[self.b("rowsSS")])
            p.add("scalar", lambda e, sl2=sl2: e.activation(
                self.rowsSS[0:1, sl2], self.rowsSS[0:1, sl2], AF.Exp, scale=-0.5),
                writes=[self.b("rowsSS")])
            p.add("vector", lambda e, sl=sl, sl2=sl2: e.tensor_mul(
                self.rowA[0:1, :], self.rowsSS[0:1, sl], self.rowsSS[0:1, sl2]),
                reads=[self.b("rowsSS")], writes=[self.b("rowA")])
            p.add("vector", lambda e: e.tensor_scalar(
                out=self.rowA[0:1, :], in0=self.rowA[0:1, :], scalar1=-1.0,
                scalar2=None, op0=ALU.mult),
                reads=[self.b("rowA")], writes=[self.b("rowA")])
            p.add("tensor", lambda e, sl2=sl2: e.matmul(
                self.psD[:, 0:512], self.onesF[0:1, :], self.rowsSS[0:1, sl2],
                start=True, stop=True),
                reads=[self.b("rowsSS"), self.b("onesF")], writes=[self.b("psD")])
            p.add("tensor", lambda e: e.matmul(
                self.psD[:, 512:1024], self.onesF[0:1, :], self.rowA[0:1, :],
                start=True, stop=True),
                reads=[self.b("rowA"), self.b("onesF")], writes=[self.b("psD")])
            for fc in range(NCH):
                p.add("vector", lambda e, fc=fc, sl=sl: e.tensor_tensor(
                    self.tsc[:, 0:512], self.stream[fc][:, sl], self.psD[:, 0:512],
                    ALU.mult),
                    reads=[self.b(f"stream{fc}"), self.b("psD")],
                    writes=[self.b("tsc")])
                p.add("vector", lambda e, fc=fc, sl=sl: e.tensor_tensor(
                    self.stream[fc][:, sl], self.tsc[:, 0:512], self.psD[:, 512:1024],
                    ALU.add),
                    reads=[self.b("tsc"), self.b("psD")],
                    writes=[self.b(f"stream{fc}")])

    def quant_scale(self, maxbuf, bmaxb, minbuf, bminb, W, toks0, wscol, use_bf16):
        nc, p = self.nc, self.p
        Wpad = ((W + 127) // 128) * 128
        ns = Wpad // 128
        if use_bf16:
            p.add("sdma2", lambda e: e.dma_start(out=self.qscr[:, 0:Wpad],
                                                 in_=maxbuf[:, 0:Wpad]),
                  reads=[bmaxb], writes=[self.b("qscr")])
            p.add("sdma2", lambda e: e.dma_start(out=self.qscr[:, 512:512 + Wpad],
                                                 in_=minbuf[:, 0:Wpad]),
                  reads=[bminb], writes=[self.b("qscr")])
        else:
            p.add("vector", lambda e: e.tensor_copy(self.tsc2[:, 3072:3072 + Wpad],
                                                    maxbuf[:, 0:Wpad]),
                  reads=[bmaxb], writes=[self.b("tsc2")])
            p.add("sdma2", lambda e: e.dma_start(out=self.qscr[:, 0:Wpad],
                                                 in_=self.tsc2[:, 3072:3072 + Wpad]),
                  reads=[self.b("tsc2")], writes=[self.b("qscr")])
            p.add("vector", lambda e: e.tensor_copy(self.tsc2[:, 3072:3072 + Wpad],
                                                    minbuf[:, 0:Wpad]),
                  reads=[bminb], writes=[self.b("tsc2")])
            p.add("sdma2", lambda e: e.dma_start(out=self.qscr[:, 512:512 + Wpad],
                                                 in_=self.tsc2[:, 3072:3072 + Wpad]),
                  reads=[self.b("tsc2")], writes=[self.b("qscr")])
        for s_ in range(ns):
            p.add("sbb", lambda e, s_=s_: e.dma_start(
                out=self.tsc2[:, 128 * s_:128 * (s_ + 1)],
                in_=self.qscr[:, 128 * s_:128 * (s_ + 1)], transpose=True),
                reads=[self.b("qscr")], writes=[self.b("tsc2")])
            p.add("sbb", lambda e, s_=s_: e.dma_start(
                out=self.tsc2[:, 512 + 128 * s_:512 + 128 * (s_ + 1)],
                in_=self.qscr[:, 512 + 128 * s_:512 + 128 * (s_ + 1)], transpose=True),
                reads=[self.b("qscr")], writes=[self.b("tsc2")])
            p.add("vector", lambda e, s_=s_: e.reduce_max(
                self.tsc[:, 16 + s_:17 + s_], self.tsc2[:, 128 * s_:128 * (s_ + 1)],
                axis=AXX),
                reads=[self.b("tsc2")], writes=[self.b("tsc")])
            p.add("vector", lambda e, s_=s_: e.tensor_reduce(
                self.tsc[:, 20 + s_:21 + s_],
                self.tsc2[:, 512 + 128 * s_:512 + 128 * (s_ + 1)], axis=AXX,
                op=ALU.min),
                reads=[self.b("tsc2")], writes=[self.b("tsc")])
        p.add("vector", lambda e, ns=ns: e.tensor_scalar(
            out=self.tsc[:, 20:20 + ns], in0=self.tsc[:, 20:20 + ns], scalar1=-1.0,
            scalar2=None, op0=ALU.mult), reads=[self.b("tsc")], writes=[self.b("tsc")])
        p.add("vector", lambda e, ns=ns: e.tensor_max(
            self.tsc[:, 16:16 + ns], self.tsc[:, 16:16 + ns], self.tsc[:, 20:20 + ns]),
            reads=[self.b("tsc")], writes=[self.b("tsc")])
        p.add("vector", lambda e, ns=ns: e.tensor_scalar(
            out=self.tsc[:, 16:16 + ns], in0=self.tsc[:, 16:16 + ns], scalar1=1e-5,
            scalar2=None, op0=ALU.max), reads=[self.b("tsc")], writes=[self.b("tsc")])
        p.add("vector", lambda e, ns=ns: e.reciprocal(
            self.tsc[:, 20:20 + ns], self.tsc[:, 16:16 + ns]),
            reads=[self.b("tsc")], writes=[self.b("tsc")])
        p.add("vector", lambda e, ns=ns: e.tensor_scalar(
            out=self.tsc[:, 20:20 + ns], in0=self.tsc[:, 20:20 + ns], scalar1=127.0,
            scalar2=None, op0=ALU.mult), reads=[self.b("tsc")], writes=[self.b("tsc")])
        p.add("vector", lambda e, ns=ns, wscol=wscol: e.tensor_scalar(
            out=self.tsc[:, 16:16 + ns], in0=self.tsc[:, 16:16 + ns],
            scalar1=self.mrow[:, 12 + wscol:13 + wscol], scalar2=None, op0=ALU.mult),
            reads=[self.b("tsc"), self.b("mrow")], writes=[self.b("tsc")])
        slots = [(self.psA[0], "psA0", 0), (self.psA[0], "psA0", 512),
                 (self.psA[1], "psA1", 0), (self.psA[1], "psA1", 512),
                 (self.psD, "psD", 0), (self.psD, "psD", 512),
                 (self.psE, "psE", 0), (self.psO, "psO", 0)]
        for s_ in range(ns):
            for k_, srccol in ((0, 20 + s_), (4, 16 + s_)):
                pst, pnm, off = slots[k_ + s_]
                p.add("tensor", lambda e, pst=pst, off=off, srccol=srccol: _tr(
                    e, pst[0:1, off:off + 128], self.tsc[:, srccol:srccol + 1],
                    self.ident[:]),
                    reads=[self.b("tsc"), self.b("ident")], writes=[self.b(pnm)])
        g = self.tguard()
        for s_ in range(ns):
            for k_ in (0, 4):
                pst, pnm, off = slots[k_ + s_]
                dst = self.cst[0] if k_ == 0 else self.cst[1]
                dnm = "cst0" if k_ == 0 else "cst1"
                p.add("vector", lambda e, pst=pst, off=off, s_=s_, dst=dst:
                      e.tensor_copy(dst[0:1, 128 * s_:128 * (s_ + 1)],
                                    pst[0:1, off:off + 128]),
                    reads=[self.b(pnm)], writes=[self.b(dnm)], deps=[g])
        p.add("tensor", lambda e: e.matmul(
            self.psD[:, 0:512], self.onesF[0:1, :], self.cst[0][0:1, 0:512],
            start=True, stop=True),
            reads=[self.b("cst0"), self.b("onesF")], writes=[self.b("psD")])
        p.add("tensor", lambda e: e.matmul(
            self.psD[:, 512:1024], self.onesF[0:1, :], self.cst[1][0:1, 0:512],
            start=True, stop=True),
            reads=[self.b("cst1"), self.b("onesF")], writes=[self.b("psD")])
        p.add("vector", lambda e: e.tensor_copy(self.bcsb[:, 0:512],
                                                self.psD[:, 0:512]),
              reads=[self.b("psD")], writes=[self.b("bcsb")])
        p.add("vector", lambda e: e.tensor_copy(self.bcsb[:, 512:1024],
                                                self.psD[:, 512:1024]),
              reads=[self.b("psD")], writes=[self.b("bcsb")])

    def emit_ffn(self, layer):
        nc, p = self.nc, self.p
        w1T = self.wqT[layer][0]
        w2T = self.wqT[layer][1]
        bw1T = self.b(f"w1qT_l{layer}")
        bw2T = self.b(f"w2qT_l{layer}")
        cw = (self.cw0, self.cw1)[layer]
        cb = (self.cb0, self.cb1)[layer]
        jcol1 = 2 * layer + 0
        jcol2 = 2 * layer + 1
        cwv = self.cwsb[:].rearrange("p (c t) -> p c t", t=4)
        p.add("sync", lambda e, cw=cw, cwv=cwv: e.dma_start(
            out=cwv[:, :, 0:3], in_=cw.rearrange("(c p) t -> p c t", p=128),
            allow_slow_non_contiguous=True), writes=[self.b("cwsb")])
        p.add("sync", lambda e, cb=cb, cwv=cwv: e.dma_start(
            out=cwv[:, :, 3:4], in_=cb.rearrange("(c p one) -> p c one", p=128, one=1),
            allow_slow_non_contiguous=True), writes=[self.b("cwsb")])
        p.add("vector", lambda e: e.memset(self.saved2[:], 0.0),
              writes=[self.b("saved2")])
        hsv = self.hs[:].rearrange("p (c t) -> p c t", t=FT)
        sv2 = self.saved2[:].rearrange("p (c t) -> p c t", t=2)
        h0v = self.h0sv[:].rearrange("p (c t) -> p c t", t=2)
        bpv = self.bpay[:].rearrange("p (c t) -> p c t", t=2)
        for T in range(NFT):
            self.emit_quant_x(layer, T, jcol1)
            if T == 0 and layer == 0:
                self.tap("ser1", self.qp[7][:], self.b("qp7"), BF16)
            wN = 510 if T == 0 else 512
            w0_ = 2 if T == 0 else 0
            for hc in range(NFH):
                wb = self.w1buf[hc % 2]
                bwb = self.b(f"w1buf{hc % 2}")
                p.add("sync", lambda e, w1T=w1T, hc=hc, wb=wb: e.dma_start(
                    out=wb[:].rearrange("p (dc h) -> p dc h", h=128),
                    in_=w1T.rearrange("(dc p) h -> p dc h", p=128)[
                        :, :, 128 * hc:128 * (hc + 1)]),
                    reads=[bw1T], writes=[bwb])
                psv = self.psE if hc % 2 == 0 else self.psO
                bps = self.b("psE") if hc % 2 == 0 else self.b("psO")
                for dc in range(NCH):
                    p.add("tensor", lambda e, dc=dc, wb=wb, psv=psv: e.matmul(
                        psv[:], wb[:, 128 * dc:128 * (dc + 1)], self.qp[dc][:],
                        start=(dc == 0), stop=(dc == NCH - 1)),
                        reads=[bwb, self.b(f"qp{dc}")], writes=[bps])
                hx = self.hx[hc % 2]
                bhx = self.b(f"hx{hc % 2}")
                cst = self.cst[hc % 2]
                bcst = self.b(f"cst{hc % 2}")
                p.add("vector", lambda e, hc=hc, sv2=sv2, hx=hx: e.tensor_copy(
                    hx[:, 0:2], sv2[:, hc, :]),
                    reads=[self.b("saved2")], writes=[bhx])
                p.add("scalar", lambda e, psv=psv, hx=hx: e.copy(hx[:, 2:514],
                                                                 psv[:]),
                      reads=[bps], writes=[bhx])
                if T == 0 and layer == 0 and hc == 0:
                    self.tap("h0", hx[:], bhx)
                p.add("vector", lambda e, hc=hc, sv2=sv2, hx=hx: e.tensor_copy(
                    sv2[:, hc, :], hx[:, 512:514]),
                    reads=[bhx], writes=[self.b("saved2")])
                if T == 0:
                    p.add("vector", lambda e, hc=hc, h0v=h0v, hx=hx: e.tensor_copy(
                        h0v[:, hc, :], hx[:, 2:4]),
                        reads=[bhx], writes=[self.b("h0sv")])
                    p.add("vector", lambda e, hc=hc, bpv=bpv, hx=hx: e.tensor_copy(
                        bpv[:, hc, 1:2], hx[:, 2:3]),
                        reads=[bhx], writes=[self.b("bpay")])
                if T == NFT - 1:
                    p.add("vector", lambda e, hc=hc, bpv=bpv, hx=hx: e.tensor_copy(
                        bpv[:, hc, 0:1], hx[:, 513:514]),
                        reads=[bhx], writes=[self.b("bpay")])
                p.add("vector", lambda e, hc=hc, cwv=cwv, w0_=w0_, wN=wN, hx=hx,
                      cst=cst: e.tensor_scalar(
                    out=cst[:, 0:wN], in0=hx[:, w0_:w0_ + wN],
                    scalar1=cwv[:, hc, 0:1], scalar2=cwv[:, hc, 3:4],
                    op0=ALU.mult, op1=ALU.add),
                    reads=[bhx, self.b("cwsb")], writes=[bcst])
                p.add("vector", lambda e, hc=hc, cwv=cwv, w0_=w0_, wN=wN, hx=hx,
                      cst=cst: e.scalar_tensor_tensor(
                    out=cst[:, 0:wN], in0=hx[:, w0_ + 1:w0_ + 1 + wN],
                    scalar=cwv[:, hc, 1:2], in1=cst[:, 0:wN],
                    op0=ALU.mult, op1=ALU.add),
                    reads=[bhx, self.b("cwsb")], writes=[bcst])
                p.add("vector", lambda e, hc=hc, cwv=cwv, w0_=w0_, wN=wN, hx=hx,
                      cst=cst: e.scalar_tensor_tensor(
                    out=cst[:, 0:wN], in0=hx[:, w0_ + 2:w0_ + 2 + wN],
                    scalar=cwv[:, hc, 2:3], in1=cst[:, 0:wN],
                    op0=ALU.mult, op1=ALU.add),
                    reads=[bhx, self.b("cwsb")], writes=[bcst])
                p.add("scalar", lambda e, hc=hc, hsv=hsv, wN=wN, cst=cst:
                      e.activation(
                    hsv[:, hc, 0:wN], cst[:, 0:wN], AF.Silu),
                    reads=[bcst], writes=[self.b("hs")])
                if T == 0 and layer == 0 and hc == 0:
                    self.tap("y0", self.cst[0][:], self.b("cst0"))
                    self.tap("hs0", self.hs[:, 0:512], self.b("hs"), BF16)
                if T == 0 and layer == 0 and hc == 15:
                    self.tap("ser2", self.hs[:, 7680:8192], self.b("hs"), BF16)
                if hc == 0:
                    p.add("vector", lambda e, hsv=hsv, wN=wN: e.tensor_copy(
                        self.runmax[:, 0:wN], hsv[:, 0, 0:wN]),
                        reads=[self.b("hs")], writes=[self.b("runmax")])
                    p.add("vector", lambda e, hsv=hsv, wN=wN: e.tensor_copy(
                        self.runmin[:, 0:wN], hsv[:, 0, 0:wN]),
                        reads=[self.b("hs")], writes=[self.b("runmin")])
                else:
                    p.add("vector", lambda e, hc=hc, hsv=hsv, wN=wN: e.tensor_max(
                        self.runmax[:, 0:wN], self.runmax[:, 0:wN], hsv[:, hc, 0:wN]),
                        reads=[self.b("hs")], writes=[self.b("runmax")])
                    p.add("vector", lambda e, hc=hc, hsv=hsv, wN=wN: e.tensor_tensor(
                        self.runmin[:, 0:wN], self.runmin[:, 0:wN], hsv[:, hc, 0:wN],
                        ALU.min),
                        reads=[self.b("hs")], writes=[self.b("runmin")])
            self.emit_tile_tail(layer, T, jcol2, w2T, bw2T)
        self.emit_halo_fix(layer, jcol2, w2T, bw2T)
        self.emit_ln_apply()

    def emit_quant_x(self, layer, T, jcol1):
        nc, p = self.nc, self.p
        sl = slice(FT * T, FT * (T + 1))
        for fc in range(NCH):
            if fc == 0:
                p.add("vector", lambda e, sl=sl: e.tensor_copy(
                    self.tsc[:, 0:512], self.stream[0][:, sl]),
                    reads=[self.b("stream0")], writes=[self.b("tsc")])
                p.add("vector", lambda e, sl=sl: e.tensor_copy(
                    self.ysc[:, 0:512], self.stream[0][:, sl]),
                    reads=[self.b("stream0")], writes=[self.b("ysc")])
            else:
                p.add("vector", lambda e, fc=fc, sl=sl: e.tensor_max(
                    self.tsc[:, 0:512], self.tsc[:, 0:512], self.stream[fc][:, sl]),
                    reads=[self.b(f"stream{fc}")], writes=[self.b("tsc")])
                p.add("vector", lambda e, fc=fc, sl=sl: e.tensor_tensor(
                    self.ysc[:, 0:512], self.ysc[:, 0:512], self.stream[fc][:, sl],
                    ALU.min),
                    reads=[self.b(f"stream{fc}")], writes=[self.b("ysc")])
        self.quant_scale(self.tsc, self.b("tsc"), self.ysc, self.b("ysc"),
                         512, FT * T, jcol1, False)
        if T == 0 and layer == 0:
            self.tap("xasinv", self.bcsb[:], self.b("bcsb"))
            self.tap("mrowt", self.mrow[:], self.b("mrow"))
            self.tap("tsct", self.tsc[0:128, 14:26], self.b("tsc"))
        if T == 0 and layer == 1:
            self.tap("xasinv1", self.bcsb[:], self.b("bcsb"))
            self.tap("tsct1", self.tsc[0:128, 14:26], self.b("tsc"))
        for fc in range(NCH):
            p.add("vector", lambda e, fc=fc, sl=sl: e.tensor_tensor(
                self.ysc[:, 0:512], self.stream[fc][:, sl], self.bcsb[:, 0:512],
                ALU.mult),
                reads=[self.b(f"stream{fc}"), self.b("bcsb")], writes=[self.b("ysc")])
            p.add("vector", lambda e: e.tensor_scalar(
                out=self.ysc[:, 0:512], in0=self.ysc[:, 0:512], scalar1=MAGIC,
                scalar2=MAGIC, op0=ALU.add, op1=ALU.subtract),
                reads=[self.b("ysc")], writes=[self.b("ysc")])
            p.add("vector", lambda e, fc=fc: e.tensor_tensor(
                self.qp[fc][:], self.ysc[:, 0:512], self.bcsb[:, 512:1024], ALU.mult),
                reads=[self.b("ysc"), self.b("bcsb")], writes=[self.b(f"qp{fc}")])
            if T == 0 and layer == 0 and fc == 0:
                self.tap("xq0", self.qp[0][:], self.b("qp0"), BF16)

    def emit_tile_tail(self, layer, T, jcol2, w2T, bw2T):
        nc, p = self.nc, self.p
        wN = 510 if T == 0 else 512
        toks0 = FT * T - 1 if T > 0 else 1
        hsv = self.hs[:].rearrange("p (c t) -> p c t", t=FT)
        if T == 0 and layer == 0:
            self.tap("ser3", self.runmax[:], self.b("runmax"), BF16)
            self.tap("hsall", self.hs[:], self.b("hs"), BF16)
        self.quant_scale(self.runmax, self.b("runmax"), self.runmin, self.b("runmin"),
                         wN, toks0, jcol2, True)
        if T == 0 and layer == 0:
            self.tap("hasinv", self.bcsb[:], self.b("bcsb"))
        if T == 0 and layer == 1:
            self.tap("hasinv1", self.bcsb[:], self.b("bcsb"))
            self.tap("hsall1", self.hs[:], self.b("hs"), BF16)
        for hc in range(NFH):
            p.add("vector", lambda e, hc=hc, hsv=hsv, wN=wN: e.tensor_tensor(
                self.ysc[:, 0:wN], hsv[:, hc, 0:wN], self.bcsb[:, 0:wN], ALU.mult),
                reads=[self.b("hs"), self.b("bcsb")], writes=[self.b("ysc")])
            p.add("vector", lambda e, hc=hc, hsv=hsv, wN=wN: e.tensor_scalar(
                out=hsv[:, hc, 0:wN], in0=self.ysc[:, 0:wN], scalar1=MAGIC,
                scalar2=MAGIC, op0=ALU.add, op1=ALU.subtract),
                reads=[self.b("ysc")], writes=[self.b("hs")])
        for dc in range(NCH):
            bwb = self.b("w2buf")
            p.add("sync", lambda e, w2T=w2T, dc=dc: e.dma_start(
                out=self.w2buf[:].rearrange("p (hc d) -> p hc d", d=128),
                in_=w2T.rearrange("(hc p) d -> p hc d", p=128)[
                    :, :, 128 * dc:128 * (dc + 1)]),
                reads=[bw2T], writes=[bwb])
            psv = self.psE if dc % 2 == 0 else self.psO
            bps = self.b("psE") if dc % 2 == 0 else self.b("psO")
            for hc in range(NFH):
                p.add("tensor", lambda e, hc=hc, psv=psv, wN=wN, hsv=hsv: e.matmul(
                    psv[:, 0:wN], self.w2buf[:, 128 * hc:128 * (hc + 1)],
                    hsv[:, hc, 0:wN], start=(hc == 0), stop=(hc == NFH - 1)),
                    reads=[bwb, self.b("hs")], writes=[bps])
            sl = slice(toks0, toks0 + wN)
            p.add("vector", lambda e, psv=psv, wN=wN: e.tensor_tensor(
                self.tsc[:, 0:wN], psv[:, 0:wN], self.bcsb[:, 512:512 + wN], ALU.mult),
                reads=[bps, self.b("bcsb")], writes=[self.b("tsc")])
            p.add("vector", lambda e, dc=dc, sl=sl, wN=wN: e.tensor_tensor(
                self.stream[dc][:, sl], self.stream[dc][:, sl], self.tsc[:, 0:wN],
                ALU.add),
                reads=[self.b("tsc")], writes=[self.b(f"stream{dc}")])
            p.add("vector", lambda e, dc=dc, sl=sl, wN=wN: e.tensor_copy(
                self.tsc2[:, 0:wN], self.stream[dc][:, sl]),
                reads=[self.b(f"stream{dc}")], writes=[self.b("tsc2")])
            p.add("vector", lambda e, dc=dc, sl=sl, wN=wN: e.tensor_mul(
                self.tsc2[:, 2048:2048 + wN], self.stream[dc][:, sl],
                self.stream[dc][:, sl]),
                reads=[self.b(f"stream{dc}")], writes=[self.b("tsc2")])
            p.add("tensor", lambda e, dc=dc, wN=wN: e.matmul(
                self.psA[0][0:1, 0:wN], self.ones1b[:], self.tsc2[:, 0:wN],
                start=(dc == 0), stop=(dc == NCH - 1)),
                reads=[self.b("tsc2"), self.b("ones1b")], writes=[self.b("psA0")])
            p.add("tensor", lambda e, dc=dc, wN=wN: e.matmul(
                self.psA[0][0:1, 512:512 + wN], self.ones1b[:],
                self.tsc2[:, 2048:2048 + wN],
                start=(dc == 0), stop=(dc == NCH - 1)),
                reads=[self.b("tsc2"), self.b("ones1b")], writes=[self.b("psA0")])
        p.add("vector", lambda e, toks0=toks0, wN=wN: e.tensor_copy(
            self.rowsSS[0:1, toks0:toks0 + wN], self.psA[0][0:1, 0:wN]),
            reads=[self.b("psA0")], writes=[self.b("rowsSS")])
        p.add("vector", lambda e, toks0=toks0, wN=wN: e.tensor_copy(
            self.rowsSS[0:1, TOK + toks0:TOK + toks0 + wN],
            self.psA[0][0:1, 512:512 + wN]),
            reads=[self.b("psA0")], writes=[self.b("rowsSS")])
        if T == 0 and layer == 0:
            self.tap("rh0", self.hs[:, 0:512], self.b("hs"), BF16)
            self.tap("res0", self.stream[0][:, 0:512], self.b("stream0"))

    def emit_halo_fix(self, layer, jcol2, w2T, bw2T):
        nc, p = self.nc, self.p
        hin, hout = self.hin[layer], self.hout[layer]
        bhin, bhout = self.b(f"hin{layer}"), self.b(f"hout{layer}")
        bpv = self.bpay[:].rearrange("p (c t) -> p c t", t=2)
        bhL = self.bhalo[:, 0:NFH].rearrange("p (c one) -> p c one", one=1)
        bhR = self.bhalo[:, NFH:2 * NFH].rearrange("p (c one) -> p c one", one=1)
        cwv = self.cwsb[:].rearrange("p (c t) -> p c t", t=4)
        h0v = self.h0sv[:].rearrange("p (c t) -> p c t", t=2)
        sv2 = self.saved2[:].rearrange("p (c t) -> p c t", t=2)
        t32 = self.tsc[:, 0:2 * NFH].rearrange("p (a c) -> p a c", a=2)
        p.add("vector", lambda e, t32=t32, bpv=bpv: e.tensor_scalar(
            out=t32[:, 0:1, :], in0=bpv[:, :, 0:1].rearrange("p c t -> p t c"),
            scalar1=self.mrow[:, 0:1], scalar2=None, op0=ALU.mult),
            reads=[self.b("bpay"), self.b("mrow")], writes=[self.b("tsc")])
        p.add("vector", lambda e, t32=t32, bpv=bpv: e.tensor_scalar(
            out=t32[:, 1:2, :], in0=bpv[:, :, 1:2].rearrange("p c t -> p t c"),
            scalar1=self.mrow[:, 1:2], scalar2=None, op0=ALU.mult),
            reads=[self.b("bpay"), self.b("mrow")], writes=[self.b("tsc")])
        p.add("stw", lambda e, hin=hin, t32=t32: e.dma_start(
            out=hin.rearrange("r (p c) -> p r c", p=128), in_=t32),
            reads=[self.b("tsc")], writes=[bhin])
        p.add("gpsimd", lambda e, hin=hin, hout=hout: e.collective_compute(
            "AllReduce", ALU.add, replica_groups=GROUPS, ins=[hin], outs=[hout]),
            reads=[bhin], writes=[bhout])
        p.add("sbb", lambda e, hout=hout: e.dma_start(
            out=self.bhalo[:].rearrange("p (t c) -> p t c", c=NFH),
            in_=hout.rearrange("r (p c) -> p r c", p=128)),
            reads=[bhout], writes=[self.b("bhalo")])
        p.add("vector", lambda e: e.tensor_scalar(
            out=self.bhalo[:, 0:NFH], in0=self.bhalo[:, 0:NFH],
            scalar1=self.mrow[:, 1:2], scalar2=None, op0=ALU.mult),
            reads=[self.b("mrow")], writes=[self.b("bhalo")])
        p.add("vector", lambda e: e.tensor_scalar(
            out=self.bhalo[:, NFH:2 * NFH], in0=self.bhalo[:, NFH:2 * NFH],
            scalar1=self.mrow[:, 0:1], scalar2=None, op0=ALU.mult),
            reads=[self.b("mrow")], writes=[self.b("bhalo")])
        y2 = self.ysc[:, 0:2 * NFH].rearrange("p (c t) -> p c t", t=2)
        scr = self.tsc[:, 64:64 + NFH].rearrange("p (c one) -> p c one", one=1)
        p.add("vector", lambda e, y2=y2, bhL=bhL, cwv=cwv: e.tensor_mul(
            y2[:, :, 0:1], bhL, cwv[:, :, 0:1]),
            reads=[self.b("bhalo"), self.b("cwsb")], writes=[self.b("ysc")])
        p.add("vector", lambda e, y2=y2, cwv=cwv: e.tensor_add(
            y2[:, :, 0:1], y2[:, :, 0:1], cwv[:, :, 3:4]),
            reads=[self.b("cwsb")], writes=[self.b("ysc")])
        p.add("vector", lambda e, scr=scr, h0v=h0v, cwv=cwv: e.tensor_mul(
            scr, h0v[:, :, 0:1], cwv[:, :, 1:2]),
            reads=[self.b("h0sv"), self.b("cwsb")], writes=[self.b("tsc")])
        p.add("vector", lambda e, y2=y2, scr=scr: e.tensor_add(
            y2[:, :, 0:1], y2[:, :, 0:1], scr),
            reads=[self.b("tsc")], writes=[self.b("ysc")])
        p.add("vector", lambda e, scr=scr, h0v=h0v, cwv=cwv: e.tensor_mul(
            scr, h0v[:, :, 1:2], cwv[:, :, 2:3]),
            reads=[self.b("h0sv"), self.b("cwsb")], writes=[self.b("tsc")])
        p.add("vector", lambda e, y2=y2, scr=scr: e.tensor_add(
            y2[:, :, 0:1], y2[:, :, 0:1], scr),
            reads=[self.b("tsc")], writes=[self.b("ysc")])
        p.add("vector", lambda e, y2=y2, sv2=sv2, cwv=cwv: e.tensor_mul(
            y2[:, :, 1:2], sv2[:, :, 0:1], cwv[:, :, 0:1]),
            reads=[self.b("saved2"), self.b("cwsb")], writes=[self.b("ysc")])
        p.add("vector", lambda e, y2=y2, cwv=cwv: e.tensor_add(
            y2[:, :, 1:2], y2[:, :, 1:2], cwv[:, :, 3:4]),
            reads=[self.b("cwsb")], writes=[self.b("ysc")])
        p.add("vector", lambda e, scr=scr, sv2=sv2, cwv=cwv: e.tensor_mul(
            scr, sv2[:, :, 1:2], cwv[:, :, 1:2]),
            reads=[self.b("saved2"), self.b("cwsb")], writes=[self.b("tsc")])
        p.add("vector", lambda e, y2=y2, scr=scr: e.tensor_add(
            y2[:, :, 1:2], y2[:, :, 1:2], scr),
            reads=[self.b("tsc")], writes=[self.b("ysc")])
        p.add("vector", lambda e, scr=scr, bhR=bhR, cwv=cwv: e.tensor_mul(
            scr, bhR, cwv[:, :, 2:3]),
            reads=[self.b("bhalo"), self.b("cwsb")], writes=[self.b("tsc")])
        p.add("vector", lambda e, y2=y2, scr=scr: e.tensor_add(
            y2[:, :, 1:2], y2[:, :, 1:2], scr),
            reads=[self.b("tsc")], writes=[self.b("ysc")])
        p.add("scalar", lambda e: e.activation(self.bhalo[:, 0:2 * NFH],
                                               self.ysc[:, 0:2 * NFH], AF.Silu),
              reads=[self.b("ysc")], writes=[self.b("bhalo")])
        hq2 = self.bhalo[:, 0:2 * NFH].rearrange("p (c t) -> p c t", t=2)
        ymx = self.ysc[:, 0:2 * NFH].rearrange("p (c t) -> p c t", t=2)
        ymn = self.hx[0][:, 0:2 * NFH].rearrange("p (c t) -> p c t", t=2)
        p.add("vector", lambda e: e.tensor_copy(self.ysc[:, 0:2 * NFH],
                                                self.bhalo[:, 0:2 * NFH]),
              reads=[self.b("bhalo")], writes=[self.b("ysc")])
        p.add("vector", lambda e: e.tensor_copy(self.hx[0][:, 0:2 * NFH],
                                                self.bhalo[:, 0:2 * NFH]),
              reads=[self.b("bhalo")], writes=[self.b("hx0")])
        n = NFH // 2
        while n >= 1:
            p.add("vector", lambda e, n=n, ymx=ymx: e.tensor_max(
                ymx[:, 0:n, :], ymx[:, 0:n, :], ymx[:, n:2 * n, :]),
                reads=[self.b("ysc")], writes=[self.b("ysc")])
            p.add("vector", lambda e, n=n, ymn=ymn: e.tensor_tensor(
                ymn[:, 0:n, :], ymn[:, 0:n, :], ymn[:, n:2 * n, :], ALU.min),
                reads=[self.b("hx0")], writes=[self.b("hx0")])
            n //= 2
        p.add("tensor", lambda e: _tr(e, self.psD[0:2, 0:128], self.ysc[:, 0:2],
                                              self.ident[:]),
              reads=[self.b("ysc"), self.b("ident")], writes=[self.b("psD")])
        p.add("tensor", lambda e: _tr(e, self.psD[0:2, 512:640],
                                      self.hx[0][:, 0:2],
                                              self.ident[:]),
              reads=[self.b("hx0"), self.b("ident")], writes=[self.b("psD")])
        g = self.tguard()
        p.add("vector", lambda e: e.tensor_copy(self.tsc2[0:2, 2560:2688],
                                                self.psD[0:2, 0:128]),
              reads=[self.b("psD")], writes=[self.b("tsc2")], deps=[g])
        p.add("vector", lambda e: e.tensor_copy(self.tsc2[0:2, 2688:2816],
                                                self.psD[0:2, 512:640]),
              reads=[self.b("psD")], writes=[self.b("tsc2")])
        p.add("vector", lambda e: e.reduce_max(self.tsc[0:2, 8:9],
                                               self.tsc2[0:2, 2560:2688], axis=AXX),
              reads=[self.b("tsc2")], writes=[self.b("tsc")])
        p.add("vector", lambda e: e.tensor_reduce(self.tsc[0:2, 9:10],
                                                  self.tsc2[0:2, 2688:2816],
                                                  axis=AXX, op=ALU.min),
              reads=[self.b("tsc2")], writes=[self.b("tsc")])
        p.add("vector", lambda e: e.tensor_scalar(
            out=self.tsc[0:2, 9:10], in0=self.tsc[0:2, 9:10], scalar1=-1.0,
            scalar2=None, op0=ALU.mult), reads=[self.b("tsc")], writes=[self.b("tsc")])
        p.add("vector", lambda e: e.tensor_max(self.tsc[0:2, 8:9], self.tsc[0:2, 8:9],
                                               self.tsc[0:2, 9:10]),
              reads=[self.b("tsc")], writes=[self.b("tsc")])
        p.add("vector", lambda e: e.tensor_scalar(
            out=self.tsc[0:2, 8:9], in0=self.tsc[0:2, 8:9], scalar1=1e-5,
            scalar2=None, op0=ALU.max), reads=[self.b("tsc")], writes=[self.b("tsc")])
        p.add("vector", lambda e: e.reciprocal(self.tsc[0:2, 10:11], self.tsc[0:2, 8:9]),
              reads=[self.b("tsc")], writes=[self.b("tsc")])
        p.add("vector", lambda e: e.tensor_scalar(
            out=self.tsc[0:2, 10:11], in0=self.tsc[0:2, 10:11], scalar1=127.0,
            scalar2=None, op0=ALU.mult), reads=[self.b("tsc")], writes=[self.b("tsc")])
        p.add("vector", lambda e, jcol2=jcol2: e.tensor_scalar(
            out=self.tsc[0:2, 8:9], in0=self.tsc[0:2, 8:9],
            scalar1=self.mrow[0:2, 12 + jcol2:13 + jcol2], scalar2=None, op0=ALU.mult),
            reads=[self.b("tsc"), self.b("mrow")], writes=[self.b("tsc")])
        p.add("stw", lambda e: e.dma_start(
            out=self.triph[0:1, 0:2].rearrange("one n -> n one"),
            in_=self.tsc[0:2, 10:11]),
            reads=[self.b("tsc")], writes=[self.b("triph")])
        p.add("stw", lambda e: e.dma_start(
            out=self.triph[0:1, 2:4].rearrange("one n -> n one"),
            in_=self.tsc[0:2, 8:9]),
            reads=[self.b("tsc")], writes=[self.b("triph")])
        p.add("sbc", lambda e: e.dma_start(
            out=self.bcsb[:, 0:4], in_=self.triph[0:1, 0:4].broadcast_to([128, 4])),
            reads=[self.b("triph")], writes=[self.b("bcsb")])
        p.add("vector", lambda e, hq2=hq2: e.tensor_tensor(
            self.ysc[:, 0:2 * NFH].rearrange("p (c t) -> p c t", t=2), hq2,
            self.bcsb[:, 0:2].rearrange("p (one t) -> p one t", one=1).broadcast_to([128, NFH, 2]),
            ALU.mult),
            reads=[self.b("bhalo"), self.b("bcsb")], writes=[self.b("ysc")])
        p.add("vector", lambda e: e.tensor_scalar(
            out=self.tsc2[:, 0:2 * NFH], in0=self.ysc[:, 0:2 * NFH], scalar1=MAGIC,
            scalar2=MAGIC, op0=ALU.add, op1=ALU.subtract),
            reads=[self.b("ysc")], writes=[self.b("tsc2")])
        for dc in range(NCH):
            bwb = self.b("w2buf")
            p.add("sync", lambda e, w2T=w2T, dc=dc: e.dma_start(
                out=self.w2buf[:].rearrange("p (hc d) -> p hc d", d=128),
                in_=w2T.rearrange("(hc p) d -> p hc d", p=128)[
                    :, :, 128 * dc:128 * (dc + 1)]),
                reads=[bw2T], writes=[bwb])
            for hc in range(NFH):
                p.add("tensor", lambda e, hc=hc: e.matmul(
                    self.psE[:, 0:2], self.w2buf[:, 128 * hc:128 * (hc + 1)],
                    self.tsc2[:, 2 * hc:2 * hc + 2], start=(hc == 0),
                    stop=(hc == NFH - 1)),
                    reads=[bwb, self.b("tsc2")], writes=[self.b("psE")])
            p.add("vector", lambda e: e.tensor_tensor(
                self.tsc[:, 4:6], self.psE[:, 0:2], self.bcsb[:, 2:4], ALU.mult),
                reads=[self.b("psE"), self.b("bcsb")], writes=[self.b("tsc")])
            p.add("vector", lambda e, dc=dc: e.tensor_add(
                self.stream[dc][:, 0:1], self.stream[dc][:, 0:1], self.tsc[:, 4:5]),
                reads=[self.b("tsc")], writes=[self.b(f"stream{dc}")])
            p.add("vector", lambda e, dc=dc: e.tensor_add(
                self.stream[dc][:, TOK - 1:TOK], self.stream[dc][:, TOK - 1:TOK],
                self.tsc[:, 5:6]),
                reads=[self.b("tsc")], writes=[self.b(f"stream{dc}")])
            p.add("vector", lambda e, dc=dc: e.tensor_copy(
                self.tsc2[:, 3000:3001], self.stream[dc][:, 0:1]),
                reads=[self.b(f"stream{dc}")], writes=[self.b("tsc2")])
            p.add("vector", lambda e, dc=dc: e.tensor_copy(
                self.tsc2[:, 3001:3002], self.stream[dc][:, TOK - 1:TOK]),
                reads=[self.b(f"stream{dc}")], writes=[self.b("tsc2")])
            p.add("vector", lambda e: e.tensor_mul(
                self.tsc2[:, 3002:3004], self.tsc2[:, 3000:3002],
                self.tsc2[:, 3000:3002]),
                reads=[self.b("tsc2")], writes=[self.b("tsc2")])
            p.add("tensor", lambda e, dc=dc: e.matmul(
                self.psA[1][0:1, 0:4], self.ones1b[:], self.tsc2[:, 3000:3004],
                start=(dc == 0), stop=(dc == NCH - 1)),
                reads=[self.b("tsc2"), self.b("ones1b")], writes=[self.b("psA1")])
        p.add("vector", lambda e: e.tensor_copy(self.rowsSS[0:1, 0:1],
                                                self.psA[1][0:1, 0:1]),
              reads=[self.b("psA1")], writes=[self.b("rowsSS")])
        p.add("vector", lambda e: e.tensor_copy(self.rowsSS[0:1, TOK - 1:TOK],
                                                self.psA[1][0:1, 1:2]),
              reads=[self.b("psA1")], writes=[self.b("rowsSS")])
        p.add("vector", lambda e: e.tensor_copy(self.rowsSS[0:1, TOK:TOK + 1],
                                                self.psA[1][0:1, 2:3]),
              reads=[self.b("psA1")], writes=[self.b("rowsSS")])
        p.add("vector", lambda e: e.tensor_copy(
            self.rowsSS[0:1, 2 * TOK - 1:2 * TOK], self.psA[1][0:1, 3:4]),
            reads=[self.b("psA1")], writes=[self.b("rowsSS")])

    def emit_output(self):
        nc, p = self.nc, self.p
        sts = []
        for fc in range(NCH):
            sts.append(p.add("sync", lambda e, fc=fc: e.dma_start(
                out=self.out[128 * fc:128 * (fc + 1), :], in_=self.stream[fc][:]),
                reads=[self.b(f"stream{fc}")]))
        p.add("sync", lambda e: e.nop(), deps=sts)


def build_program(stage=8):
    return K(stage=stage).build()


def kernel(**inputs):
    stage = int(os.environ.get("STAGE", "8"))
    nc = build_program(stage)
    x = np.ascontiguousarray(np.asarray(inputs["L0_attn_x"], np.float32))
    in_maps = []
    for core in range(NCORE):
        b_, h_ = core // 2, core % 2
        m = {
            "x_own": np.ascontiguousarray(x[b_, TOK * h_:TOK * (h_ + 1), :]),
            "x_oth": np.ascontiguousarray(x[b_, TOK * (1 - h_):TOK * (2 - h_), :]),
            "hmask": np.array([[1.0 - h_, float(h_)]], np.float32),
            "L0_attn_Wq": np.asarray(inputs["L0_attn_Wq"], np.float32),
            "L0_attn_Wkv": np.asarray(inputs["L0_attn_Wkv"], np.float32),
            "L0_attn_k_scale": np.ascontiguousarray(
                np.asarray(inputs["L0_attn_k_scale"], np.float32).reshape(-1)),
            "L0_ffn_w1": np.asarray(inputs["L0_ffn_w1"], np.float32),
            "L0_ffn_conv_w": np.ascontiguousarray(
                np.asarray(inputs["L0_ffn_conv_w"], np.float32).reshape(FF, 3)),
            "L0_ffn_conv_b": np.asarray(inputs["L0_ffn_conv_b"], np.float32),
            "L0_ffn_w2": np.asarray(inputs["L0_ffn_w2"], np.float32),
            "L1_attn_Wq": np.asarray(inputs["L1_attn_Wq"], np.float32),
            "L1_attn_Wk": np.asarray(inputs["L1_attn_Wk"], np.float32),
            "L1_attn_Wv": np.asarray(inputs["L1_attn_Wv"], np.float32),
            "L1_attn_Wo": np.asarray(inputs["L1_attn_Wo"], np.float32),
            "L1_attn_quadrant_weights": np.ascontiguousarray(
                np.asarray(inputs["L1_attn_quadrant_weights"],
                           np.float32).reshape(1, 4)),
            "L1_ffn_w1": np.asarray(inputs["L1_ffn_w1"], np.float32),
            "L1_ffn_conv_w": np.ascontiguousarray(
                np.asarray(inputs["L1_ffn_conv_w"], np.float32).reshape(FF, 3)),
            "L1_ffn_conv_b": np.asarray(inputs["L1_ffn_conv_b"], np.float32),
            "L1_ffn_w2": np.asarray(inputs["L1_ffn_w2"], np.float32),
        }
        in_maps.append(m)
    res = run_bass_kernel_spmd(nc, in_maps, core_ids=list(range(NCORE)))
    out = np.empty((4, SEQ, D), np.float32)
    for core in range(NCORE):
        b_, h_ = core // 2, core % 2
        out[b_, TOK * h_:TOK * (h_ + 1), :] = res.results[core]["out"].T
    return out


# version 12
# speedup vs baseline: 1.1506x; 1.1506x over previous
import os
import numpy as np
from contextlib import ExitStack

import concourse.bass as bass
import concourse.mybir as mybir
from concourse.bass_utils import run_bass_kernel_spmd

F32 = mybir.dt.float32
BF16 = mybir.dt.bfloat16
AF = mybir.ActivationFunctionType
ALU = mybir.AluOpType
AXX = mybir.AxisListType.X

MAGIC = float(np.float32(1.5 * 2 ** 23))
NCORE = 8
TOK = 2048
SEQ = 4096
D = 1024
NCH = 8
FF = 4096
NFH = 32
NH = 16
DH = 64
NJC = 32
BLK = 512
NBLK = TOK // BLK
FT = 512
NFT = TOK // FT
EXPSHIFT = 70.0
GROUPS = [[0, 1], [2, 3], [4, 5], [6, 7]]

DMA_LANES = {"sync", "sdma2", "gdma", "sbc", "stw", "svt", "sbb", "swl",
             "swt"}
SERIAL_LANES = {"sbc", "stw", "sbb"}
DELAY_SRC = set(os.environ.get("DELAY_SRC", "tensor").split(",")) - {""}
DELAY_CYC = int(os.environ.get("DELAY_CYC", "96"))
LANES = ["tensor", "vector", "scalar", "gpsimd", "sync", "sdma2", "gdma",
         "sbc", "stw", "svt", "sbb", "swl", "swt"]
LANE_STREAM = {"tensor": "tensor", "vector": "vector", "scalar": "scalar",
               "gpsimd": "gpsimd", "sync": "sync", "sdma2": "scalar",
               "gdma": "gpsimd", "sbc": "sync", "stw": "scalar",
               "svt": "sync", "sbb": "sync", "swl": "sync", "swt": "scalar"}


class Step:
    __slots__ = ("lane", "fn", "val", "deps", "label")

    def __init__(self, lane, fn, val, deps, label):
        self.lane, self.fn, self.val, self.deps, self.label = \
            lane, fn, val, deps, label


class Buf:
    __slots__ = ("name", "w", "readers")

    def __init__(self, name):
        self.name = name
        self.w = None
        self.readers = []


class Plan:
    def __init__(self):
        self.steps = []
        self.count = {e: 0 for e in LANES}

    def add(self, lane, fn, reads=(), writes=(), deps=(), label=""):
        d = [x for x in deps if x is not None]
        for b in reads:
            if b.w is not None:
                d.append(b.w)
        for b in writes:
            if b.w is not None:
                d.append(b.w)
            d.extend(b.readers)
        inc = 16 if lane in DMA_LANES else 1
        self.count[lane] += inc
        st = Step(lane, fn, self.count[lane], d, label)
        self.steps.append(st)
        for b in reads:
            b.readers.append(st)
        for b in writes:
            b.w = st
            b.readers = []
        return st


def _skip_dep(lane, dep):
    if dep.lane == lane and lane == "tensor":
        return True
    if dep.lane == lane and lane in DMA_LANES:
        return True
    return False


def emit_all(nc, block, plan):
    sems = {}
    for e in LANES:
        if any(st.lane == e for st in plan.steps):
            sems[e] = nc.semaphore(f"sem_{e}").__enter__()

    def make_body(lanes):
        steps = [st for st in plan.steps if st.lane in lanes]

        def body(eng):
            seen = {e: 0 for e in LANES}
            for st in steps:
                need = {}
                for dep in st.deps:
                    if _skip_dep(st.lane, dep):
                        continue
                    if dep.val > need.get(dep.lane, 0):
                        need[dep.lane] = dep.val
                fresh_wait_lanes = set()
                for lane_, val in need.items():
                    if seen[lane_] >= val:
                        continue
                    eng.wait_ge(sems[lane_], val)
                    seen[lane_] = val
                    fresh_wait_lanes.add(lane_)
                if st.lane in ("vector", "scalar", "gpsimd") and \
                        fresh_wait_lanes & DELAY_SRC:
                    eng.nop()
                    eng.nop()
                    eng.nop()
                if st.lane in SERIAL_LANES and st.val > 16 and \
                        seen[st.lane] < st.val - 16:
                    eng.wait_ge(sems[st.lane], st.val - 16)
                    seen[st.lane] = st.val - 16
                inst = st.fn(eng)
                if inst is None:
                    inst = eng.nop()
                inst.then_inc(sems[st.lane], 16 if st.lane in DMA_LANES else 1)
            for lane_ in LANES:
                if lane_ in sems and plan.count[lane_] > seen[lane_]:
                    eng.wait_ge(sems[lane_], plan.count[lane_])
        return body

    streams = {}
    for lane in LANES:
        if any(st.lane == lane for st in plan.steps):
            streams.setdefault(LANE_STREAM[lane], []).append(lane)
    for phys, fn in (("tensor", block.tensor), ("vector", block.vector),
                     ("scalar", block.scalar), ("gpsimd", block.gpsimd),
                     ("sync", block.sync)):
        if phys in streams:
            fn(make_body(tuple(streams[phys])))
    return sems



def _tr(e, out, in_, ident):
    return e.transpose(out, in_, ident)

def make_identity(nc, identity, base=0):
    nc.gpsimd.memset(identity, 0.0)
    nc.gpsimd.nop()
    nc.gpsimd.affine_select(
        out=identity, in_=identity, compare_op=ALU.not_equal,
        fill=1.0, base=base, pattern=[[-1, identity.shape[1]]],
        channel_multiplier=1)


class K:
    def __init__(self, stage=8):
        self.stage = stage
        self.want_taps = set(os.environ.get("TAPS", "").split(",")) - {""}
        nc = bass.Bass("TRN2", target_bir_lowering=False)
        self.nc = nc
        self.p = Plan()
        self.stk = ExitStack()
        self.bufs = {}

        def inp(name, shape):
            return nc.dram_tensor(name, shape, F32, kind="ExternalInput").ap()

        self.x_own = inp("x_own", [TOK, D])
        self.x_oth = inp("x_oth", [TOK, D])
        self.hmask = inp("hmask", [1, 2])
        self.wq0 = inp("L0_attn_Wq", [D, D])
        self.wkv0 = inp("L0_attn_Wkv", [D, 512])
        self.kscale = inp("L0_attn_k_scale", [NH * DH])
        self.w1_0 = inp("L0_ffn_w1", [FF, D])
        self.cw0 = inp("L0_ffn_conv_w", [FF, 3])
        self.cb0 = inp("L0_ffn_conv_b", [FF])
        self.w2_0 = inp("L0_ffn_w2", [D, FF])
        self.wq1 = inp("L1_attn_Wq", [D, D])
        self.wk1 = inp("L1_attn_Wk", [D, D])
        self.wv1 = inp("L1_attn_Wv", [D, D])
        self.wo1 = inp("L1_attn_Wo", [D, D])
        self.qw1 = inp("L1_attn_quadrant_weights", [1, 4])
        self.w1_1 = inp("L1_ffn_w1", [FF, D])
        self.cw1 = inp("L1_ffn_conv_w", [FF, 3])
        self.cb1 = inp("L1_ffn_conv_b", [FF])
        self.w2_1 = inp("L1_ffn_w2", [D, FF])
        self.out = nc.dram_tensor("out", [D, TOK], F32, kind="ExternalOutput").ap()

        def scr(name, shape, dt=BF16):
            t = nc.dram_tensor(name, shape, dt).ap()
            self.bufs[name] = Buf(name)
            return t

        self.xb16 = scr("xb16", [SEQ, D])
        self.wqnat = [[scr(f"w{j}qnat_l{l}", [FF, D] if j == 1 else [D, FF])
                       for j in (1, 2)] for l in (0, 1)]
        self.wqT = [[scr(f"w{j}qT_l{l}", [D, FF] if j == 1 else [FF, D])
                     for j in (1, 2)] for l in (0, 1)]
        self.wqTp = [scr(f"w1qTp_l{l}", [128, NFH * NCH * 128]) for l in (0, 1)]
        self.k0scr = scr("k0scr", [256, SEQ])
        self.v0scr = scr("v0scr", [SEQ, 256])
        self.kin = scr("kin", [D + 1, TOK])
        self.kout = scr("kout", [2 * (D + 1), TOK])
        self.vin = scr("vin", [TOK, D])
        self.vout = scr("vout", [SEQ, D])
        self.hin = [scr(f"hin{l}", [2, FF], F32) for l in (0, 1)]
        self.hout = [scr(f"hout{l}", [2, FF], F32) for l in (0, 1)]
        self.trip = scr("trip", [16, 16], F32)
        self.tripq = scr("tripq", [2, TOK], F32)
        self.qscr = scr("qscr", [128, 1024])
        self.tripa = scr("tripa", [1, 1024], F32)
        self.triph = scr("triph", [1, 8], F32)

        def sb(name, shape, dt):
            t = self.stk.enter_context(nc.sbuf_tensor(name, shape, dt))
            self.bufs[name] = Buf(name)
            return t

        self.stream = [sb(f"stream{c}", [128, TOK], F32) for c in range(NCH)]
        self.xrow = sb("xrow", [128, D], F32)
        self.xrow2 = sb("xrow2", [128, D], F32)
        self.xt = [sb(f"xt{i}", [128, BLK], BF16) for i in range(NCH)]
        self.qp = [sb(f"qp{i}", [128, BLK], BF16) for i in range(NCH)]
        self.kpair = sb("kpair", [128, SEQ], BF16)
        self.vt = [sb(f"vt{i}", [128, NJC * 65], BF16) for i in range(2)]
        self.pt = [sb(f"pt{i}", [128, 1024], BF16) for i in range(2)]
        self.nrmp = sb("nrmp", [128, 1024], BF16)
        self.bcsb = sb("bcsb", [128, 1024], F32)
        self.ao = [sb(f"ao{i}", [128, BLK], BF16) for i in range(NCH)]
        self.hs = sb("hs", [128, NFH * FT], BF16)
        self.w1buf = [sb(f"w1buf{i}", [128, NCH * 128], BF16) for i in range(2)]
        self.w2buf = sb("w2buf", [128, NFH * 128], BF16)
        self.hx = [sb("hx0", [128, 514], F32), sb("hx1", [128, 514], F32)]
        self.cst = [sb("cst0", [128, 512], F32), sb("cst1", [128, 512], F32)]
        self.ysc = sb("ysc", [128, FT], F32)
        self.tsc = sb("tsc", [128, FT], F32)
        self.tsc2 = sb("tsc2", [128, 3584], BF16)
        self.saved2 = sb("saved2", [128, NFH * 2], F32)
        self.h0sv = sb("h0sv", [128, NFH * 2], F32)
        self.bpay = sb("bpay", [128, NFH * 2], F32)
        self.bhalo = sb("bhalo", [128, NFH * 2], F32)
        self.runmax = sb("runmax", [128, FT], BF16)
        self.runmin = sb("runmin", [128, FT], BF16)
        self.cwsb = sb("cwsb", [128, NFH * 4], F32)
        self.kssb = sb("kssb", [128, NCH], F32)
        self.mrow = sb("mrow", [128, 16], F32)
        self.rowsSS = sb("rowsSS", [1, 2 * TOK], F32)
        self.rowA = self.cst[0]
        self.bufs["rowA"] = self.bufs["cst0"]
        self.srow = sb("srow", [1, 64], F32)
        self.nacc = sb("nacc", [2, 16], F32)
        self.expb = sb("expb", [128, 8], F32)
        self.onesF = sb("onesF", [128, 128], F32)
        self.ones2b = sb("ones2b", [128, 2], BF16)
        self.ones1b = sb("ones1b", [128, 1], BF16)
        self.ident = sb("ident", [128, 128], F32)
        self.identb = sb("identb", [128, 128], BF16)
        self.ishif = sb("ishif", [128, 128], BF16)
        self.wsrow = sb("wsrow", [1, 16], F32)

        def ps(name, shape):
            t = self.stk.enter_context(nc.psum_tensor(name, shape, F32))
            self.bufs[name] = Buf(name)
            return t
        self.psA = [ps(f"psA{i}", [128, 1024]) for i in range(2)]
        self.psE = ps("psE", [128, 512])
        self.psO = ps("psO", [128, 512])
        self.psD = ps("psD", [128, 1024])

    def b(self, name):
        return self.bufs[name]

    def tap(self, name, ap, buf, dt=F32):
        if name not in self.want_taps:
            return
        sh = list(ap.shape)
        t = self.nc.dram_tensor("tap_" + name, sh, dt, kind="ExternalOutput").ap()
        self.p.add("sync", lambda e, t=t, ap=ap: e.dma_start(out=t, in_=ap),
                   reads=[buf], label="tap_" + name)



    def tguard(self):
        return self.p.add("tensor", lambda e: e.matmul(
            self.psD[0:1, 1022:1024], self.ones1b[0:1, 0:1],
            self.identb[0:1, 0:2], start=True, stop=True),
            reads=[self.b("ones1b"), self.b("identb")], writes=[])

    def bcast_row(self, src_ap, src_buf, dst_ap, dst_buf, n, lane="vector"):
        p = self.p
        p.add("tensor", lambda e: e.matmul(
            self.psD[:, 0:n] if n <= 512 else None,
            self.onesF[0:1, :], src_ap, start=True, stop=True),
            reads=[src_buf, self.b("onesF")], writes=[self.b("psD")])
        p.add(lane, lambda e: e.tensor_copy(dst_ap, self.psD[0:dst_ap.shape[0], 0:n]),
              reads=[self.b("psD")], writes=[dst_buf])

    def wpv(self, dc):
        return self.hs[:, 1024 * dc:1024 * (dc + 1)]

    def wov(self, dc):
        return self.hs[:, 8192 + 1024 * dc:8192 + 1024 * (dc + 1)]

    def build(self):
        nc, p = self.nc, self.p
        with nc.Block() as block:
            self.emit_consts()
            self.emit_input_stage()
            self.tap("st0", self.stream[0][:, 0:512], self.b("stream0"))
            self.emit_weight_prep(0)
            self.tap("wsrow", self.wsrow[:], self.b("wsrow"))
            self.emit_l0_kv()
            self.tap("nacc0", self.nacc[0:2, 0:4], self.b("nacc"))
            self.emit_attention(0)
            if self.stage >= 4:
                self.emit_weight_prep(1)
                self.emit_ffn(0)
            if self.stage >= 6:
                self.emit_l1_kv()
                self.emit_attention(1)
            if self.stage >= 8:
                self.emit_ffn(1)
            self.emit_output()
            emit_all(nc, block, p)
        self.stk.close()
        return nc

    def emit_consts(self):
        nc, p = self.nc, self.p
        for nm, base in (("ident", 0), ("identb", 0), ("ishif", 64)):
            t_ = getattr(self, nm)
            p.add("gpsimd", lambda e, t_=t_: e.memset(t_[:], 0.0),
                  writes=[self.b(nm)])
            p.add("gpsimd", lambda e, t_=t_, base=base: e.affine_select(
                out=t_[:], in_=t_[:], compare_op=ALU.not_equal, fill=1.0,
                base=base, pattern=[[-1, t_.shape[1]]], channel_multiplier=1),
                reads=[self.b(nm)], writes=[self.b(nm)])
        p.add("vector", lambda e: e.memset(self.onesF[:], 1.0), writes=[self.b("onesF")])
        p.add("vector", lambda e: e.memset(self.ones1b[:], 1.0), writes=[self.b("ones1b")])
        p.add("vector", lambda e: e.memset(self.ones2b[:], 0.0), writes=[self.b("ones2b")])
        p.add("vector", lambda e: e.memset(self.ones2b[0:64, 0:1], 1.0),
              writes=[self.b("ones2b")])
        p.add("vector", lambda e: e.memset(self.ones2b[64:128, 1:2], 1.0),
              writes=[self.b("ones2b")])
        p.add("vector", lambda e: e.memset(self.srow[0:1, 30:31], 1e-5),
              writes=[self.b("srow")])
        p.add("sbc", lambda e: e.dma_start(out=self.mrow[:, 0:2],
                                            in_=self.hmask.broadcast_to([128, 2])),
              writes=[self.b("mrow")])
        p.add("sbc", lambda e: e.dma_start(
            out=self.kssb[:], in_=self.kscale.rearrange("(c p) -> p c", p=128),
            allow_slow_non_contiguous=True),
            writes=[self.b("kssb")])
        for i in range(2):
            p.add("vector", lambda e, i=i: e.memset(
                self.vt[i][:].rearrange("p (c d) -> p c d", d=65)[:, :, 64:65], 1.0),
                writes=[self.b(f"vt{i}")])

    def emit_input_stage(self):
        nc, p = self.nc, self.p
        bxr = self.b("xrow")
        for si, (src, roff) in enumerate([(self.x_own, 0), (self.x_oth, TOK)]):
            for t in range(TOK // 128):
                p.add("sync", lambda e, src=src, t=t: e.dma_start(
                    out=self.xrow[:], in_=src[128 * t:128 * (t + 1), :]), writes=[bxr])
                p.add("vector", lambda e: e.tensor_copy(self.tsc2[:, 0:D], self.xrow[:]),
                      reads=[bxr], writes=[self.b("tsc2")])
                p.add("sdma2", lambda e, roff=roff, t=t: e.dma_start(
                    out=self.xb16[roff + 128 * t:roff + 128 * (t + 1), :],
                    in_=self.tsc2[:, 0:D]), reads=[self.b("tsc2")],
                    writes=[self.b("xb16")])
                if si == 0:
                    for c in range(NCH):
                        psv = self.psA[c % 2][:, 0:128]
                        bps = self.b(f"psA{c % 2}")
                        p.add("tensor", lambda e, c=c, psv=psv: _tr(e, 
                            psv, self.xrow[:, 128 * c:128 * (c + 1)], self.ident[:]),
                            reads=[bxr, self.b("ident")], writes=[bps])
                        p.add("vector", lambda e, c=c, t=t, psv=psv: e.tensor_copy(
                            self.stream[c][:, 128 * t:128 * (t + 1)], psv),
                            reads=[bps], writes=[self.b(f"stream{c}")])

    def emit_weight_prep(self, layer):
        nc, p = self.nc, self.p
        ws = [(self.w1_0, self.w1_1)[layer], (self.w2_0, self.w2_1)[layer]]
        for j, w in enumerate(ws):
            R, C = (FF, D) if j == 0 else (D, FF)
            nat, wt = self.wqnat[layer][j], self.wqT[layer][j]
            if layer == 0 and j == 0:
                self.tap_w1 = True
            bnat = self.b(f"w{j + 1}qnat_l{layer}")
            bwt = self.b(f"w{j + 1}qT_l{layer}")
            col = 2 * layer + j
            bxr = self.b("xrow")
            ncol = C // D
            first = True
            for r in range(R // 128):
                for cc in range(ncol):
                    p.add("swl", lambda e, w=w, r=r, cc=cc: e.dma_start(
                        out=self.xrow[:], in_=w[128 * r:128 * (r + 1),
                                                D * cc:D * (cc + 1)]), writes=[bxr])
                    p.add("vector", lambda e: e.reduce_sum(
                        self.tsc[:, 0:1], self.xrow[:], axis=AXX,
                        apply_absolute_value=True),
                        reads=[bxr], writes=[self.b("tsc")])
                    if first:
                        p.add("vector", lambda e: e.tensor_copy(
                            self.tsc[:, 1:2], self.tsc[:, 0:1]),
                            reads=[self.b("tsc")], writes=[self.b("tsc")])
                        first = False
                    else:
                        p.add("vector", lambda e: e.tensor_add(
                            self.tsc[:, 1:2], self.tsc[:, 1:2], self.tsc[:, 0:1]),
                            reads=[self.b("tsc")], writes=[self.b("tsc")])
            p.add("tensor", lambda e: _tr(e, self.psD[0:1, 0:128],
                                          self.tsc[:, 1:2], self.ident[:]),
                  reads=[self.b("tsc"), self.b("ident")], writes=[self.b("psD")])
            g = self.tguard()
            p.add("vector", lambda e: e.tensor_copy(self.ysc[0:1, 0:128],
                                                    self.psD[0:1, 0:128]),
                  reads=[self.b("psD")], writes=[self.b("ysc")], deps=[g])
            p.add("vector", lambda e: e.reduce_sum(self.tsc[0:1, 2:3],
                                                   self.ysc[0:1, 0:128], axis=AXX),
                  reads=[self.b("ysc")], writes=[self.b("tsc")])
            p.add("vector", lambda e, R=R, C=C, col=col: e.tensor_scalar(
                out=self.wsrow[0:1, 4 * col:4 * col + 1], in0=self.tsc[0:1, 2:3],
                scalar1=1.0 / (R * C), scalar2=1e-5, op0=ALU.mult, op1=ALU.max),
                reads=[self.b("tsc")], writes=[self.b("wsrow")])
            p.add("vector", lambda e, col=col: e.reciprocal(
                self.wsrow[0:1, 4 * col + 1:4 * col + 2],
                self.wsrow[0:1, 4 * col:4 * col + 1]),
                reads=[self.b("wsrow")], writes=[self.b("wsrow")])
            p.add("vector", lambda e, col=col: e.tensor_scalar(
                out=self.wsrow[0:1, 4 * col + 2:4 * col + 3],
                in0=self.wsrow[0:1, 4 * col:4 * col + 1],
                scalar1=1.0 / 127.0, scalar2=None, op0=ALU.mult),
                reads=[self.b("wsrow")], writes=[self.b("wsrow")])
            p.add("tensor", lambda e, col=col: e.matmul(
                self.psD[:, 0:2], self.onesF[0:1, :],
                self.wsrow[0:1, 4 * col + 1:4 * col + 3], start=True, stop=True),
                reads=[self.b("wsrow"), self.b("onesF")], writes=[self.b("psD")])
            p.add("vector", lambda e, col=col: e.tensor_copy(
                self.mrow[:, 8 + col:9 + col], self.psD[:, 0:1]),
                reads=[self.b("psD")], writes=[self.b("mrow")])
            p.add("vector", lambda e, col=col: e.tensor_copy(
                self.mrow[:, 12 + col:13 + col], self.psD[:, 1:2]),
                reads=[self.b("psD")], writes=[self.b("mrow")])
            for r in range(R // 128):
                for cc in range(ncol):
                    idx = r * ncol + cc
                    xr = self.xrow if idx % 2 == 0 else self.xrow2
                    bx = bxr if idx % 2 == 0 else self.b("xrow2")
                    p.add("swl", lambda e, w=w, r=r, cc=cc, xr=xr: e.dma_start(
                        out=xr[:], in_=w[128 * r:128 * (r + 1),
                                         D * cc:D * (cc + 1)]), writes=[bx])
                    p.add("vector", lambda e, col=col, xr=xr: e.tensor_scalar(
                        out=xr[:], in0=xr[:],
                        scalar1=self.mrow[:, 8 + col:9 + col], scalar2=1.25,
                        op0=ALU.mult, op1=ALU.min),
                        reads=[bx, self.b("mrow")], writes=[bx])
                    p.add("vector", lambda e, xr=xr: e.tensor_scalar(
                        out=xr[:], in0=xr[:], scalar1=-1.25,
                        scalar2=MAGIC, op0=ALU.max, op1=ALU.add),
                        reads=[bx], writes=[bx])
                    p.add("vector", lambda e, cc=cc, xr=xr: e.tensor_scalar(
                        out=self.tsc2[:, D * (cc % 2):D * (cc % 2 + 1)], in0=xr[:],
                        scalar1=MAGIC, scalar2=None, op0=ALU.subtract),
                        reads=[bx], writes=[self.b("tsc2")])
                    p.add("sdma2", lambda e, nat=nat, r=r, cc=cc: e.dma_start(
                        out=nat[128 * r:128 * (r + 1), D * cc:D * (cc + 1)],
                        in_=self.tsc2[:, D * (cc % 2):D * (cc % 2 + 1)]),
                        reads=[self.b("tsc2")], writes=[bnat])
            wtp = self.wqTp[layer]
            bwtp = self.b(f"w1qTp_l{layer}")
            wtpv = wtp.rearrange("p (hc dc h) -> p hc dc h", hc=NFH, dc=NCH)
            for cc in range(C // 128):
                for rh in range(0, R, 2048):
                    rw = min(2048, R - rh)
                    p.add("swt", lambda e, nat=nat, cc=cc, rh=rh, rw=rw:
                          e.dma_start(
                        out=self.tsc2[:, 0:rw],
                        in_=nat[rh:rh + rw, 128 * cc:128 * (cc + 1)],
                        transpose=True), reads=[bnat], writes=[self.b("tsc2")])
                    if j == 0:
                        hc0 = rh // 128
                        p.add("sdma2", lambda e, wtpv=wtpv, cc=cc, hc0=hc0,
                              rw=rw: e.dma_start(
                            out=wtpv[:, hc0:hc0 + rw // 128, cc:cc + 1, :],
                            in_=self.tsc2[:, 0:rw].rearrange(
                                "p (hc one h) -> p hc one h", one=1, h=128)),
                            reads=[self.b("tsc2")], writes=[bwtp])
                    else:
                        p.add("sdma2", lambda e, wt=wt, cc=cc, rh=rh, rw=rw:
                              e.dma_start(
                            out=wt[128 * cc:128 * (cc + 1), rh:rh + rw],
                            in_=self.tsc2[:, 0:rw]),
                            reads=[self.b("tsc2")], writes=[bwt])

    def load_proj_weight(self, w, into="wp"):
        p = self.p
        Fout = w.shape[1]
        view = self.wpv if into == "wp" else self.wov
        for dc in range(NCH):
            p.add("gdma", lambda e, w=w, dc=dc, Fout=Fout, view=view: e.dma_start(
                out=view(dc)[:, 0:Fout], in_=w[128 * dc:128 * (dc + 1), :]),
                writes=[self.b("hs")])

    def kv_norm_acc(self, ncol, first):
        p = self.p
        p.add("vector", lambda e: e.tensor_mul(
            self.tsc2[:, 2048:2560], self.tsc2[:, 0:512], self.tsc2[:, 0:512]),
            reads=[self.b("tsc2")], writes=[self.b("tsc2")])
        p.add("tensor", lambda e: e.matmul(self.psD[0:2, 0:512], self.ones2b[:],
                                           self.tsc2[:, 2048:2560],
                                           start=True, stop=True),
              reads=[self.b("tsc2"), self.b("ones2b")], writes=[self.b("psD")])
        p.add("vector", lambda e: e.tensor_copy(self.tsc2[0:2, 2560:3072],
                                                self.psD[0:2, 0:512]),
              reads=[self.b("psD")], writes=[self.b("tsc2")])
        p.add("vector", lambda e: e.reduce_max(self.tsc[0:2, 3:4],
                                               self.tsc2[0:2, 2560:3072], axis=AXX),
              reads=[self.b("tsc2")], writes=[self.b("tsc")])
        if first:
            p.add("vector", lambda e, ncol=ncol: e.tensor_copy(
                self.nacc[0:2, ncol:ncol + 1], self.tsc[0:2, 3:4]),
                reads=[self.b("tsc")], writes=[self.b("nacc")])
        else:
            p.add("vector", lambda e, ncol=ncol: e.tensor_max(
                self.nacc[0:2, ncol:ncol + 1], self.nacc[0:2, ncol:ncol + 1],
                self.tsc[0:2, 3:4]),
                reads=[self.b("tsc")], writes=[self.b("nacc")])

    def flatten2_max(self, src_ap, src_buf, dst_ap, trip_slot):
        p = self.p
        p.add("tensor", lambda e, src_ap=src_ap: _tr(
            e, self.psD[0:1, 0:2], src_ap, self.ident[0:2, 0:2]),
            reads=[src_buf, self.b("ident")], writes=[self.b("psD")])
        g = self.tguard()
        p.add("vector", lambda e: e.tensor_copy(self.srow[0:1, 8:10],
                                                self.psD[0:1, 0:2]),
            reads=[self.b("psD")], writes=[self.b("srow")], deps=[g])
        p.add("vector", lambda e, dst_ap=dst_ap: e.reduce_max(
            dst_ap, self.srow[0:1, 8:10], axis=AXX),
            reads=[self.b("srow")], writes=[self.b("srow")])

    def emit_l0_kv(self):
        nc, p = self.nc, self.p
        self.load_proj_weight(self.wkv0)
        first_n = True
        for t in range(SEQ // BLK):
            for dc in range(NCH):
                p.add("sync", lambda e, t=t, dc=dc: e.dma_start(
                    out=self.xt[dc][:], in_=self.xb16[BLK * t:BLK * (t + 1),
                                                      128 * dc:128 * (dc + 1)],
                    transpose=True), reads=[self.b("xb16")], writes=[self.b(f"xt{dc}")])
            for kc in range(2):
                psv = self.psA[0][:, 512 * kc:512 * (kc + 1)]
                bps = self.b("psA0")
                for dc in range(NCH):
                    p.add("tensor", lambda e, kc=kc, dc=dc, psv=psv: e.matmul(
                        psv, self.wpv(dc)[:, 128 * kc:128 * (kc + 1)], self.xt[dc][:],
                        start=(dc == 0), stop=(dc == NCH - 1)),
                        reads=[self.b("hs"), self.b(f"xt{dc}")], writes=[bps])
                p.add("vector", lambda e, psv=psv: e.tensor_copy(
                    self.tsc2[:, 0:512], psv), reads=[bps], writes=[self.b("tsc2")])
                p.add("sdma2", lambda e, kc=kc, t=t: e.dma_start(
                    out=self.k0scr[128 * kc:128 * (kc + 1), BLK * t:BLK * (t + 1)],
                    in_=self.tsc2[:, 0:512]), reads=[self.b("tsc2")],
                    writes=[self.b("k0scr")])
                self.kv_norm_acc(0, first_n)
                first_n = False
            for s in range(BLK // 128):
                psv = self.psA[1][:, 0:256]
                bps = self.b("psA1")
                for dc in range(NCH):
                    p.add("tensor", lambda e, s=s, dc=dc, psv=psv: e.matmul(
                        psv, self.xt[dc][:, 128 * s:128 * (s + 1)],
                        self.wpv(dc)[:, 256:512], start=(dc == 0),
                        stop=(dc == NCH - 1)),
                        reads=[self.b("hs"), self.b(f"xt{dc}")], writes=[bps])
                p.add("vector", lambda e, psv=psv: e.tensor_copy(
                    self.tsc2[:, 1024:1280], psv), reads=[bps], writes=[self.b("tsc2")])
                p.add("sdma2", lambda e, t=t, s=s: e.dma_start(
                    out=self.v0scr[BLK * t + 128 * s:BLK * t + 128 * (s + 1), :],
                    in_=self.tsc2[:, 1024:1280]), reads=[self.b("tsc2")],
                    writes=[self.b("v0scr")])
        self.flatten2_max(self.nacc[0:2, 0:1], self.b("nacc"), self.srow[0:1, 1:2], 0)

    def emit_l1_kv(self):
        nc, p = self.nc, self.p
        p.add("sbc", lambda e: e.dma_start(out=self.srow[0:1, 16:20], in_=self.qw1),
              writes=[self.b("srow")])
        p.add("scalar", lambda e: e.activation(self.srow[0:1, 20:24],
                                               self.srow[0:1, 16:20], AF.Exp),
              reads=[self.b("srow")], writes=[self.b("srow")])
        p.add("vector", lambda e: e.reduce_sum(self.srow[0:1, 24:25],
                                               self.srow[0:1, 20:24], axis=AXX),
              reads=[self.b("srow")], writes=[self.b("srow")])
        p.add("vector", lambda e: e.reciprocal(self.srow[0:1, 25:26],
                                               self.srow[0:1, 24:25]),
              reads=[self.b("srow")], writes=[self.b("srow")])
        p.add("vector", lambda e: e.tensor_tensor(
            self.srow[0:1, 26:30], self.srow[0:1, 20:24],
            self.srow[0:1, 25:26].broadcast_to([1, 4]), ALU.mult),
            reads=[self.b("srow")], writes=[self.b("srow")])
        p.add("tensor", lambda e: e.matmul(
            self.psD[:, 0:4], self.onesF[0:1, :], self.srow[0:1, 26:30],
            start=True, stop=True),
            reads=[self.b("srow"), self.b("onesF")], writes=[self.b("psD")])
        p.add("vector", lambda e: e.tensor_copy(self.mrow[:, 4:8],
                                                self.psD[:, 0:4]),
            reads=[self.b("psD")], writes=[self.b("mrow")])
        for dc in range(NCH):
            bxr = self.b("xrow")
            p.add("sync", lambda e, dc=dc: e.dma_start(
                out=self.xrow[:], in_=self.wk1[128 * dc:128 * (dc + 1), :]),
                writes=[bxr])
            A = self.xrow[:].rearrange("p (h two c) -> p h two c", two=2, c=32)
            W = self.wpv(dc).rearrange("p (h two c) -> p h two c", two=2, c=32)
            T1 = self.tsc[:, 0:512].rearrange("p (h one c) -> p h one c",
                                              one=1, c=32)
            p.add("vector", lambda e, A=A, T1=T1: e.tensor_scalar(
                out=T1, in0=A[:, :, 0:1, :], scalar1=self.mrow[:, 4:5], scalar2=None,
                op0=ALU.mult), reads=[bxr, self.b("mrow")], writes=[self.b("tsc")])
            p.add("vector", lambda e, A=A, W=W, T1=T1: e.scalar_tensor_tensor(
                out=W[:, :, 0:1, :], in0=A[:, :, 1:2, :], scalar=self.mrow[:, 5:6],
                in1=T1, op0=ALU.mult, op1=ALU.add),
                reads=[bxr, self.b("mrow"), self.b("tsc")], writes=[self.b("hs")])
            p.add("vector", lambda e, A=A, T1=T1: e.tensor_scalar(
                out=T1, in0=A[:, :, 0:1, :], scalar1=self.mrow[:, 6:7], scalar2=None,
                op0=ALU.mult), reads=[bxr, self.b("mrow")], writes=[self.b("tsc")])
            p.add("vector", lambda e, A=A, W=W, T1=T1: e.scalar_tensor_tensor(
                out=W[:, :, 1:2, :], in0=A[:, :, 1:2, :], scalar=self.mrow[:, 7:8],
                in1=T1, op0=ALU.mult, op1=ALU.add),
                reads=[bxr, self.b("mrow"), self.b("tsc")], writes=[self.b("hs")])
        first_n = True
        for t in range(NBLK):
            for dc in range(NCH):
                p.add("vector", lambda e, t=t, dc=dc: e.tensor_copy(
                    self.xt[dc][:], self.stream[dc][:, BLK * t:BLK * (t + 1)]),
                    reads=[self.b(f"stream{dc}")], writes=[self.b(f"xt{dc}")])
            for fc in range(NCH):
                psv = self.psA[0][:, 0:512]
                bps = self.b("psA0")
                for dc in range(NCH):
                    p.add("tensor", lambda e, fc=fc, dc=dc, psv=psv: e.matmul(
                        psv, self.wpv(dc)[:, 128 * fc:128 * (fc + 1)], self.xt[dc][:],
                        start=(dc == 0), stop=(dc == NCH - 1)),
                        reads=[self.b("hs"), self.b(f"xt{dc}")], writes=[bps])
                p.add("vector", lambda e, psv=psv: e.tensor_copy(self.tsc2[:, 0:512], psv),
                      reads=[bps], writes=[self.b("tsc2")])
                p.add("sdma2", lambda e, fc=fc, t=t: e.dma_start(
                    out=self.kin[128 * fc:128 * (fc + 1), BLK * t:BLK * (t + 1)],
                    in_=self.tsc2[:, 0:512]), reads=[self.b("tsc2")],
                    writes=[self.b("kin")])
                self.kv_norm_acc(1, first_n)
                first_n = False
        self.flatten2_max(self.nacc[0:2, 1:2], self.b("nacc"), self.srow[0:1, 2:3], 1)
        p.add("vector", lambda e: e.tensor_copy(self.tsc2[0:1, 0:1], self.srow[0:1, 2:3]),
              reads=[self.b("srow")], writes=[self.b("tsc2")])
        p.add("stw", lambda e: e.dma_start(out=self.kin[D:D + 1, 0:1],
                                             in_=self.tsc2[0:1, 0:1]),
              reads=[self.b("tsc2")], writes=[self.b("kin")])
        self.load_proj_weight(self.wv1)
        for t in range(NBLK):
            for dc in range(NCH):
                p.add("vector", lambda e, t=t, dc=dc: e.tensor_copy(
                    self.xt[dc][:], self.stream[dc][:, BLK * t:BLK * (t + 1)]),
                    reads=[self.b(f"stream{dc}")], writes=[self.b(f"xt{dc}")])
            for s in range(BLK // 128):
                for half in range(2):
                    psv = self.psA[1][:, 512 * half:512 * (half + 1)]
                    bps = self.b("psA1")
                    for dc in range(NCH):
                        p.add("tensor", lambda e, s=s, dc=dc, half=half, psv=psv:
                              e.matmul(
                            psv, self.xt[dc][:, 128 * s:128 * (s + 1)],
                            self.wpv(dc)[:, 512 * half:512 * (half + 1)],
                            start=(dc == 0), stop=(dc == NCH - 1)),
                            reads=[self.b("hs"), self.b(f"xt{dc}")], writes=[bps])
                p.add("vector", lambda e: e.tensor_copy(self.tsc2[:, 0:1024],
                                                        self.psA[1][:]),
                      reads=[self.b("psA1")], writes=[self.b("tsc2")])
                p.add("sdma2", lambda e, t=t, s=s: e.dma_start(
                    out=self.vin[BLK * t + 128 * s:BLK * t + 128 * (s + 1), :],
                    in_=self.tsc2[:, 0:1024]), reads=[self.b("tsc2")],
                    writes=[self.b("vin")])
        p.add("gpsimd", lambda e: e.collective_compute(
            "AllGather", ALU.bypass, replica_groups=GROUPS,
            ins=[self.kin], outs=[self.kout]),
            reads=[self.b("kin")], writes=[self.b("kout")])
        p.add("gpsimd", lambda e: e.collective_compute(
            "AllGather", ALU.bypass, replica_groups=GROUPS,
            ins=[self.vin], outs=[self.vout]),
            reads=[self.b("vin")], writes=[self.b("vout")])
        p.add("sbc", lambda e: e.dma_start(out=self.tsc2[0:1, 4:5],
                                            in_=self.kout[D:D + 1, 0:1]),
              reads=[self.b("kout")], writes=[self.b("tsc2")])
        p.add("sbc", lambda e: e.dma_start(out=self.tsc2[0:1, 5:6],
                                            in_=self.kout[2 * D + 1:2 * D + 2, 0:1]),
              reads=[self.b("kout")], writes=[self.b("tsc2")])
        p.add("vector", lambda e: e.tensor_copy(self.srow[0:1, 8:10],
                                                self.tsc2[0:1, 4:6]),
              reads=[self.b("tsc2")], writes=[self.b("srow")])
        p.add("vector", lambda e: e.reduce_max(self.srow[0:1, 2:3],
                                               self.srow[0:1, 8:10], axis=AXX),
              reads=[self.b("srow")], writes=[self.b("srow")])

    def emit_attention(self, layer):
        nc, p = self.nc, self.p
        scale = 0.125 if layer == 0 else float(np.float32(32.0 ** -0.5))
        wq = self.wq0 if layer == 0 else self.wq1
        self.load_proj_weight(wq, into="wp")
        if layer == 1:
            self.load_proj_weight(self.wo1, into="wo")
        for b_ in range(NBLK):
            self.emit_qproj(layer, b_, scale)
            if layer == 0 and b_ == 0:
                self.tap("srowA", self.srow[0:1, 0:16], self.b("srow"))
                self.tap("expb0", self.expb[:, 0:4], self.b("expb"))
                self.tap("qpA", self.qp[0][:], self.b("qp0"), BF16)
            pending = None
            for P in range(NH // 2):
                pending = self.emit_pair(layer, b_, P, scale, pending_tail=pending)
                if layer == 0 and b_ == 0 and P == 0:
                    self.tap("ao0", self.ao[0][:], self.b("ao0"), BF16)
            pending()
            if layer == 1:
                self.emit_oproj(b_)
            else:
                for fc in range(NCH):
                    p.add("vector", lambda e, fc=fc, b_=b_: e.tensor_tensor(
                        self.stream[fc][:, BLK * b_:BLK * (b_ + 1)],
                        self.stream[fc][:, BLK * b_:BLK * (b_ + 1)],
                        self.ao[fc][:], ALU.add),
                        reads=[self.b(f"ao{fc}")], writes=[self.b(f"stream{fc}")])
        self.emit_ln_full()

    def emit_qproj(self, layer, b_, scale):
        nc, p = self.nc, self.p
        for dc in range(NCH):
            if layer == 0:
                p.add("sync", lambda e, b_=b_, dc=dc: e.dma_start(
                    out=self.xt[dc][:], in_=self.xb16[BLK * b_:BLK * (b_ + 1),
                                                      128 * dc:128 * (dc + 1)],
                    transpose=True), reads=[self.b("xb16")], writes=[self.b(f"xt{dc}")])
            else:
                p.add("vector", lambda e, b_=b_, dc=dc: e.tensor_copy(
                    self.xt[dc][:], self.stream[dc][:, BLK * b_:BLK * (b_ + 1)]),
                    reads=[self.b(f"stream{dc}")], writes=[self.b(f"xt{dc}")])
        first = True
        for fc in range(NCH):
            psv = self.psA[0][:, 0:512]
            bps = self.b("psA0")
            for dc in range(NCH):
                p.add("tensor", lambda e, fc=fc, dc=dc, psv=psv: e.matmul(
                    psv, self.wpv(dc)[:, 128 * fc:128 * (fc + 1)], self.xt[dc][:],
                    start=(dc == 0), stop=(dc == NCH - 1)),
                    reads=[self.b("hs"), self.b(f"xt{dc}")], writes=[bps])
            if layer == 0:
                p.add("vector", lambda e, fc=fc, psv=psv: e.tensor_scalar(
                    out=self.qp[fc][:], in0=psv, scalar1=self.kssb[:, fc:fc + 1],
                    scalar2=None, op0=ALU.mult),
                    reads=[bps, self.b("kssb")], writes=[self.b(f"qp{fc}")])
            else:
                p.add("vector", lambda e, fc=fc, psv=psv: e.tensor_copy(
                    self.qp[fc][:], psv), reads=[bps], writes=[self.b(f"qp{fc}")])
            p.add("vector", lambda e, fc=fc: e.tensor_mul(
                self.tsc2[:, 2048:2560], self.qp[fc][:], self.qp[fc][:]),
                reads=[self.b(f"qp{fc}")], writes=[self.b("tsc2")])
            p.add("tensor", lambda e: e.matmul(self.psD[0:2, 512:1024], self.ones2b[:],
                                               self.tsc2[:, 2048:2560],
                                               start=True, stop=True),
                  reads=[self.b("tsc2"), self.b("ones2b")], writes=[self.b("psD")])
            p.add("vector", lambda e: e.tensor_copy(self.tsc2[0:2, 2560:3072],
                                                    self.psD[0:2, 512:1024]),
                  reads=[self.b("psD")], writes=[self.b("tsc2")])
            p.add("vector", lambda e: e.reduce_max(self.tsc[0:2, 2:3],
                                                   self.tsc2[0:2, 2560:3072], axis=AXX),
                  reads=[self.b("tsc2")], writes=[self.b("tsc")])
            if layer == 0 and b_ == 0:
                self.tap(f"cp{fc}", self.tsc2[0:2, 2560:3072], self.b("tsc2"), BF16)
                self.tap(f"ts{fc}", self.tsc[0:2, 2:3], self.b("tsc"))
            if first:
                p.add("vector", lambda e: e.tensor_copy(self.nacc[0:2, 2:3],
                                                        self.tsc[0:2, 2:3]),
                      reads=[self.b("tsc")], writes=[self.b("nacc")])
                first = False
            else:
                p.add("vector", lambda e: e.tensor_max(
                    self.nacc[0:2, 2:3], self.nacc[0:2, 2:3], self.tsc[0:2, 2:3]),
                    reads=[self.b("tsc")], writes=[self.b("nacc")])
        if layer == 0 and b_ == 0:
            self.tap("naccq", self.nacc[0:2, 2:3], self.b("nacc"))
        self.flatten2_max(self.nacc[0:2, 2:3], self.b("nacc"), self.srow[0:1, 3:4], 2)
        kcol = 1 if layer == 0 else 2
        p.add("vector", lambda e, kcol=kcol: e.tensor_mul(
            self.srow[0:1, 4:5], self.srow[0:1, 3:4], self.srow[0:1, kcol:kcol + 1]),
            reads=[self.b("srow")], writes=[self.b("srow")])
        p.add("scalar", lambda e: e.activation(self.srow[0:1, 5:6], self.srow[0:1, 4:5],
                                               AF.Ln),
              reads=[self.b("srow")], writes=[self.b("srow")])
        p.add("vector", lambda e, scale=scale: e.memset(self.srow[0:1, 6:7],
                                                        float(np.log(scale))),
              writes=[self.b("srow")])
        p.add("vector", lambda e: e.tensor_scalar(
            out=self.srow[0:1, 7:8], in0=self.srow[0:1, 5:6], scalar1=0.5,
            scalar2=self.srow[0:1, 6:7], op0=ALU.mult, op1=ALU.add),
            reads=[self.b("srow")], writes=[self.b("srow")])
        p.add("scalar", lambda e: e.activation(self.srow[0:1, 12:13],
                                               self.srow[0:1, 7:8], AF.Exp),
              reads=[self.b("srow")], writes=[self.b("srow")])
        p.add("vector", lambda e: e.tensor_scalar(
            out=self.srow[0:1, 13:14], in0=self.srow[0:1, 12:13], scalar1=-1.0,
            scalar2=EXPSHIFT, op0=ALU.mult, op1=ALU.add),
            reads=[self.b("srow")], writes=[self.b("srow")])
        col = 4 * layer + b_
        p.add("tensor", lambda e: e.matmul(
            self.psD[:, 0:1], self.onesF[0:1, :], self.srow[0:1, 13:14],
            start=True, stop=True),
            reads=[self.b("srow"), self.b("onesF")], writes=[self.b("psD")])
        p.add("vector", lambda e, col=col: e.tensor_copy(
            self.expb[:, col:col + 1], self.psD[:, 0:1]),
            reads=[self.b("psD")], writes=[self.b("expb")])

    def emit_pair(self, layer, b_, P, scale, pending_tail=None):
        nc, p = self.nc, self.p
        h0, h1 = 2 * P, 2 * P + 1
        bk = self.b("kpair")
        bv0, bv1 = self.b("vt0"), self.b("vt1")
        vv0 = self.vt[0][:].rearrange("p (c d) -> p c d", d=65)
        vv1 = self.vt[1][:].rearrange("p (c d) -> p c d", d=65)
        if layer == 0:
            kv = 64 * (h0 // 4)
            if P % 2 == 0:
                p.add("svt", lambda e, kv=kv: e.dma_start(
                    out=self.kpair[0:64, :], in_=self.k0scr[kv:kv + 64, :]),
                    reads=[self.b("k0scr")], writes=[bk])
                p.add("svt", lambda e, kv=kv: e.dma_start(
                    out=self.kpair[64:128, :], in_=self.k0scr[kv:kv + 64, :]),
                    reads=[self.b("k0scr")], writes=[bk])
                p.add("svt", lambda e, kv=kv, vv0=vv0: e.dma_start(
                    out=vv0[:, :, 0:64],
                    in_=self.v0scr.rearrange("(c p) f -> p c f", p=128)[
                        :, :, kv:kv + 64]),
                    reads=[self.b("v0scr")], writes=[bv0])
                p.add("svt", lambda e, kv=kv, vv1=vv1: e.dma_start(
                    out=vv1[:, :, 0:64],
                    in_=self.v0scr.rearrange("(c p) f -> p c f", p=128)[
                        :, :, kv:kv + 64]),
                    reads=[self.b("v0scr")], writes=[bv1])
        else:
            for half, h in ((0, h0), (1, h1)):
                prows = slice(64 * half, 64 * half + 64)
                p.add("sync", lambda e, h=h, prows=prows: e.dma_start(
                    out=self.kpair[prows, 0:TOK],
                    in_=self.kout[64 * h:64 * h + 64, :]),
                    reads=[self.b("kout")], writes=[bk])
                p.add("svt", lambda e, h=h, prows=prows: e.dma_start(
                    out=self.kpair[prows, TOK:SEQ],
                    in_=self.kout[D + 1 + 64 * h:D + 1 + 64 * h + 64, :]),
                    reads=[self.b("kout")], writes=[bk])
            p.add("svt", lambda e, h0=h0, vv0=vv0: e.dma_start(
                out=vv0[:, :, 0:64],
                in_=self.vout.rearrange("(c p) f -> p c f", p=128)[
                    :, :, 64 * h0:64 * h0 + 64]),
                reads=[self.b("vout")], writes=[bv0])
            p.add("svt", lambda e, h1=h1, vv1=vv1: e.dma_start(
                out=vv1[:, :, 0:64],
                in_=self.vout.rearrange("(c p) f -> p c f", p=128)[
                    :, :, 64 * h1:64 * h1 + 64]),
                reads=[self.b("vout")], writes=[bv1])
        col = 4 * layer + b_
        qbuf = self.qp[P]
        bq = self.b(f"qp{P}")
        bexp = self.b("expb")
        bpsE, bpsO = self.b("psE"), self.b("psO")
        def emit_score(c):
            psA = self.psA[c % 2]
            bpsA = self.b(f"psA{c % 2}")
            p.add("tensor", lambda e, c=c, psA=psA, qbuf=qbuf: e.matmul(
                psA[:, 0:512], self.kpair[0:64, 128 * c:128 * (c + 1)], qbuf[0:64, :],
                start=True, stop=True, tile_position=(0, 0)),
                reads=[bk, bq], writes=[bpsA])
            p.add("tensor", lambda e, c=c, psA=psA, qbuf=qbuf: e.matmul(
                psA[:, 512:1024], self.kpair[64:128, 128 * c:128 * (c + 1)],
                qbuf[64:128, :], start=True, stop=True, tile_position=(64, 0)),
                reads=[bk, bq], writes=[bpsA])

        emit_score(0)
        for c in range(NJC):
            if c + 1 < NJC:
                emit_score(c + 1)
            if c == 0 and pending_tail is not None:
                pending_tail()
                pending_tail = None
            psA = self.psA[c % 2]
            bpsA = self.b(f"psA{c % 2}")
            ptb = self.pt[c % 2]
            bpt = self.b(f"pt{c % 2}")
            p.add("scalar", lambda e, psA=psA, ptb=ptb, col=col, scale=scale:
                  e.activation(
                ptb[:], psA[:], AF.Exp, bias=self.expb[:, col:col + 1], scale=scale),
                reads=[bpsA, bexp], writes=[bpt])
            p.add("tensor", lambda e, c=c, ptb=ptb, vv0=vv0: e.matmul(
                self.psE[0:65, :], vv0[:, c, :], ptb[:, 0:512],
                start=(c == 0), stop=(c == NJC - 1)),
                reads=[bpt, bv0], writes=[bpsE])
            p.add("tensor", lambda e, c=c, ptb=ptb, vv1=vv1: e.matmul(
                self.psO[0:65, :], vv1[:, c, :], ptb[:, 512:1024],
                start=(c == 0), stop=(c == NJC - 1)),
                reads=[bpt, bv1], writes=[bpsO])
        if pending_tail is not None:
            pending_tail()

        def tail():
            p.add("vector", lambda e: e.reciprocal(self.bcsb[64:65, 0:512],
                                                   self.psE[64:65, :]),
                  reads=[bpsE], writes=[self.b("bcsb")])
            p.add("vector", lambda e: e.reciprocal(self.bcsb[64:65, 512:1024],
                                                   self.psO[64:65, :]),
                  reads=[bpsO], writes=[self.b("bcsb")])
            p.add("tensor", lambda e: e.matmul(
                self.psD[0:64, 0:512], self.onesF[64:65, 0:64],
                self.bcsb[64:65, 0:512],
                start=True, stop=True, tile_position=(64, 0)),
                reads=[self.b("bcsb"), self.b("onesF")], writes=[self.b("psD")])
            p.add("tensor", lambda e: e.matmul(
                self.psD[0:64, 512:1024], self.onesF[64:65, 0:64],
                self.bcsb[64:65, 512:1024],
                start=True, stop=True, tile_position=(64, 0)),
                reads=[self.b("bcsb"), self.b("onesF")], writes=[self.b("psD")])
            p.add("vector", lambda e: e.tensor_copy(self.bcsb[0:64, 0:1024],
                                                    self.psD[0:64, 0:1024]),
                  reads=[self.b("psD")], writes=[self.b("bcsb")])
            p.add("vector", lambda e: e.tensor_tensor(
                self.nrmp[0:64, 0:512], self.psE[0:64, :], self.bcsb[0:64, 0:512],
                ALU.mult),
                reads=[bpsE, self.b("bcsb")], writes=[self.b("nrmp")])
            p.add("vector", lambda e: e.tensor_tensor(
                self.nrmp[0:64, 512:1024], self.psO[0:64, :],
                self.bcsb[0:64, 512:1024],
                ALU.mult),
                reads=[bpsO, self.b("bcsb")], writes=[self.b("nrmp")])
            p.add("tensor", lambda e: e.matmul(self.psD[:, 0:512],
                                               self.identb[0:64, :],
                                               self.nrmp[0:64, 0:512],
                                               start=True, stop=False),
                  reads=[self.b("nrmp"), self.b("identb")], writes=[self.b("psD")])
            p.add("tensor", lambda e: e.matmul(self.psD[:, 0:512],
                                               self.ishif[0:64, :],
                                               self.nrmp[0:64, 512:1024],
                                               start=False, stop=True),
                  reads=[self.b("nrmp"), self.b("ishif")], writes=[self.b("psD")])
            p.add("vector", lambda e, P=P: e.tensor_copy(self.ao[P][:],
                                                         self.psD[:, 0:512]),
                  reads=[self.b("psD")], writes=[self.b(f"ao{P}")])
        return tail

    def emit_oproj(self, b_):
        nc, p = self.nc, self.p
        for fc in range(NCH):
            psv = self.psA[0][:, 0:512]
            bps = self.b("psA0")
            for dc in range(NCH):
                p.add("tensor", lambda e, fc=fc, dc=dc, psv=psv: e.matmul(
                    psv, self.wov(dc)[:, 128 * fc:128 * (fc + 1)], self.ao[dc][:],
                    start=(dc == 0), stop=(dc == NCH - 1)),
                    reads=[self.b("hs"), self.b(f"ao{dc}")], writes=[bps])
            p.add("vector", lambda e, fc=fc, b_=b_, psv=psv: e.tensor_tensor(
                self.stream[fc][:, BLK * b_:BLK * (b_ + 1)],
                self.stream[fc][:, BLK * b_:BLK * (b_ + 1)], psv, ALU.add),
                reads=[bps], writes=[self.b(f"stream{fc}")])

    def emit_ln_full(self):
        nc, p = self.nc, self.p
        for tt in range(NBLK):
            sl = slice(BLK * tt, BLK * (tt + 1))
            for fc in range(NCH):
                p.add("vector", lambda e, fc=fc, sl=sl: e.tensor_copy(
                    self.tsc2[:, 0:512], self.stream[fc][:, sl]),
                    reads=[self.b(f"stream{fc}")], writes=[self.b("tsc2")])
                p.add("vector", lambda e, fc=fc, sl=sl: e.tensor_mul(
                    self.tsc2[:, 2048:2560], self.stream[fc][:, sl],
                    self.stream[fc][:, sl]),
                    reads=[self.b(f"stream{fc}")], writes=[self.b("tsc2")])
                p.add("tensor", lambda e, fc=fc: e.matmul(
                    self.psD[0:1, 0:512], self.ones1b[:], self.tsc2[:, 0:512],
                    start=(fc == 0), stop=(fc == NCH - 1)),
                    reads=[self.b("tsc2"), self.b("ones1b")], writes=[self.b("psD")])
                p.add("tensor", lambda e, fc=fc: e.matmul(
                    self.psD[0:1, 512:1024], self.ones1b[:], self.tsc2[:, 2048:2560],
                    start=(fc == 0), stop=(fc == NCH - 1)),
                    reads=[self.b("tsc2"), self.b("ones1b")], writes=[self.b("psD")])
            p.add("vector", lambda e, sl=sl: e.tensor_copy(
                self.rowsSS[0:1, sl], self.psD[0:1, 0:512]),
                reads=[self.b("psD")], writes=[self.b("rowsSS")])
            p.add("vector", lambda e, tt=tt: e.tensor_copy(
                self.rowsSS[0:1, TOK + BLK * tt:TOK + BLK * (tt + 1)],
                self.psD[0:1, 512:1024]),
                reads=[self.b("psD")], writes=[self.b("rowsSS")])
        self.tap("rows0", self.rowsSS[0:1, :], self.b("rowsSS"))
        self.emit_ln_apply()

    def emit_ln_apply(self):
        nc, p = self.nc, self.p
        p.add("vector", lambda e: e.tensor_scalar(
            out=self.rowsSS[0:1, 0:TOK], in0=self.rowsSS[0:1, 0:TOK],
            scalar1=1.0 / D, scalar2=None, op0=ALU.mult),
            reads=[self.b("rowsSS")], writes=[self.b("rowsSS")])
        p.add("vector", lambda e: e.tensor_scalar(
            out=self.rowsSS[0:1, TOK:2 * TOK], in0=self.rowsSS[0:1, TOK:2 * TOK],
            scalar1=1.0 / D, scalar2=None, op0=ALU.mult),
            reads=[self.b("rowsSS")], writes=[self.b("rowsSS")])
        for tt in range(NBLK):
            sl = slice(BLK * tt, BLK * (tt + 1))
            sl2 = slice(TOK + BLK * tt, TOK + BLK * (tt + 1))
            p.add("vector", lambda e, sl=sl: e.tensor_mul(
                self.rowA[0:1, :], self.rowsSS[0:1, sl], self.rowsSS[0:1, sl]),
                reads=[self.b("rowsSS")], writes=[self.b("rowA")])
            p.add("vector", lambda e, sl2=sl2: e.tensor_sub(
                self.rowsSS[0:1, sl2], self.rowsSS[0:1, sl2], self.rowA[0:1, :]),
                reads=[self.b("rowA")], writes=[self.b("rowsSS")])
            p.add("scalar", lambda e, sl2=sl2: e.activation(
                self.rowsSS[0:1, sl2], self.rowsSS[0:1, sl2], AF.Ln,
                bias=self.srow[0:1, 30:31]),
                reads=[self.b("srow")], writes=[self.b("rowsSS")])
            p.add("scalar", lambda e, sl2=sl2: e.activation(
                self.rowsSS[0:1, sl2], self.rowsSS[0:1, sl2], AF.Exp, scale=-0.5),
                writes=[self.b("rowsSS")])
            p.add("vector", lambda e, sl=sl, sl2=sl2: e.tensor_mul(
                self.rowA[0:1, :], self.rowsSS[0:1, sl], self.rowsSS[0:1, sl2]),
                reads=[self.b("rowsSS")], writes=[self.b("rowA")])
            p.add("vector", lambda e: e.tensor_scalar(
                out=self.rowA[0:1, :], in0=self.rowA[0:1, :], scalar1=-1.0,
                scalar2=None, op0=ALU.mult),
                reads=[self.b("rowA")], writes=[self.b("rowA")])
            p.add("tensor", lambda e, sl2=sl2: e.matmul(
                self.psD[:, 0:512], self.onesF[0:1, :], self.rowsSS[0:1, sl2],
                start=True, stop=True),
                reads=[self.b("rowsSS"), self.b("onesF")], writes=[self.b("psD")])
            p.add("tensor", lambda e: e.matmul(
                self.psD[:, 512:1024], self.onesF[0:1, :], self.rowA[0:1, :],
                start=True, stop=True),
                reads=[self.b("rowA"), self.b("onesF")], writes=[self.b("psD")])
            for fc in range(NCH):
                p.add("vector", lambda e, fc=fc, sl=sl: e.tensor_tensor(
                    self.tsc[:, 0:512], self.stream[fc][:, sl], self.psD[:, 0:512],
                    ALU.mult),
                    reads=[self.b(f"stream{fc}"), self.b("psD")],
                    writes=[self.b("tsc")])
                p.add("vector", lambda e, fc=fc, sl=sl: e.tensor_tensor(
                    self.stream[fc][:, sl], self.tsc[:, 0:512], self.psD[:, 512:1024],
                    ALU.add),
                    reads=[self.b("tsc"), self.b("psD")],
                    writes=[self.b(f"stream{fc}")])

    def quant_scale(self, maxbuf, bmaxb, minbuf, bminb, W, toks0, wscol, use_bf16):
        nc, p = self.nc, self.p
        Wpad = ((W + 127) // 128) * 128
        ns = Wpad // 128
        if use_bf16:
            p.add("sdma2", lambda e: e.dma_start(out=self.qscr[:, 0:Wpad],
                                                 in_=maxbuf[:, 0:Wpad]),
                  reads=[bmaxb], writes=[self.b("qscr")])
            p.add("sdma2", lambda e: e.dma_start(out=self.qscr[:, 512:512 + Wpad],
                                                 in_=minbuf[:, 0:Wpad]),
                  reads=[bminb], writes=[self.b("qscr")])
        else:
            p.add("vector", lambda e: e.tensor_copy(self.tsc2[:, 3072:3072 + Wpad],
                                                    maxbuf[:, 0:Wpad]),
                  reads=[bmaxb], writes=[self.b("tsc2")])
            p.add("sdma2", lambda e: e.dma_start(out=self.qscr[:, 0:Wpad],
                                                 in_=self.tsc2[:, 3072:3072 + Wpad]),
                  reads=[self.b("tsc2")], writes=[self.b("qscr")])
            p.add("vector", lambda e: e.tensor_copy(self.tsc2[:, 3072:3072 + Wpad],
                                                    minbuf[:, 0:Wpad]),
                  reads=[bminb], writes=[self.b("tsc2")])
            p.add("sdma2", lambda e: e.dma_start(out=self.qscr[:, 512:512 + Wpad],
                                                 in_=self.tsc2[:, 3072:3072 + Wpad]),
                  reads=[self.b("tsc2")], writes=[self.b("qscr")])
        for s_ in range(ns):
            p.add("sbb", lambda e, s_=s_: e.dma_start(
                out=self.tsc2[:, 128 * s_:128 * (s_ + 1)],
                in_=self.qscr[:, 128 * s_:128 * (s_ + 1)], transpose=True),
                reads=[self.b("qscr")], writes=[self.b("tsc2")])
            p.add("sbb", lambda e, s_=s_: e.dma_start(
                out=self.tsc2[:, 512 + 128 * s_:512 + 128 * (s_ + 1)],
                in_=self.qscr[:, 512 + 128 * s_:512 + 128 * (s_ + 1)], transpose=True),
                reads=[self.b("qscr")], writes=[self.b("tsc2")])
            p.add("vector", lambda e, s_=s_: e.reduce_max(
                self.tsc[:, 16 + s_:17 + s_], self.tsc2[:, 128 * s_:128 * (s_ + 1)],
                axis=AXX),
                reads=[self.b("tsc2")], writes=[self.b("tsc")])
            p.add("vector", lambda e, s_=s_: e.tensor_reduce(
                self.tsc[:, 20 + s_:21 + s_],
                self.tsc2[:, 512 + 128 * s_:512 + 128 * (s_ + 1)], axis=AXX,
                op=ALU.min),
                reads=[self.b("tsc2")], writes=[self.b("tsc")])
        p.add("vector", lambda e, ns=ns: e.tensor_scalar(
            out=self.tsc[:, 20:20 + ns], in0=self.tsc[:, 20:20 + ns], scalar1=-1.0,
            scalar2=None, op0=ALU.mult), reads=[self.b("tsc")], writes=[self.b("tsc")])
        p.add("vector", lambda e, ns=ns: e.tensor_max(
            self.tsc[:, 16:16 + ns], self.tsc[:, 16:16 + ns], self.tsc[:, 20:20 + ns]),
            reads=[self.b("tsc")], writes=[self.b("tsc")])
        p.add("vector", lambda e, ns=ns: e.tensor_scalar(
            out=self.tsc[:, 16:16 + ns], in0=self.tsc[:, 16:16 + ns], scalar1=1e-5,
            scalar2=None, op0=ALU.max), reads=[self.b("tsc")], writes=[self.b("tsc")])
        p.add("vector", lambda e, ns=ns: e.reciprocal(
            self.tsc[:, 20:20 + ns], self.tsc[:, 16:16 + ns]),
            reads=[self.b("tsc")], writes=[self.b("tsc")])
        p.add("vector", lambda e, ns=ns: e.tensor_scalar(
            out=self.tsc[:, 20:20 + ns], in0=self.tsc[:, 20:20 + ns], scalar1=127.0,
            scalar2=None, op0=ALU.mult), reads=[self.b("tsc")], writes=[self.b("tsc")])
        p.add("vector", lambda e, ns=ns, wscol=wscol: e.tensor_scalar(
            out=self.tsc[:, 16:16 + ns], in0=self.tsc[:, 16:16 + ns],
            scalar1=self.mrow[:, 12 + wscol:13 + wscol], scalar2=None, op0=ALU.mult),
            reads=[self.b("tsc"), self.b("mrow")], writes=[self.b("tsc")])
        slots = [(self.psA[0], "psA0", 0), (self.psA[0], "psA0", 512),
                 (self.psA[1], "psA1", 0), (self.psA[1], "psA1", 512),
                 (self.psD, "psD", 0), (self.psD, "psD", 512),
                 (self.psE, "psE", 0), (self.psO, "psO", 0)]
        for s_ in range(ns):
            for k_, srccol in ((0, 20 + s_), (4, 16 + s_)):
                pst, pnm, off = slots[k_ + s_]
                p.add("tensor", lambda e, pst=pst, off=off, srccol=srccol: _tr(
                    e, pst[0:1, off:off + 128], self.tsc[:, srccol:srccol + 1],
                    self.ident[:]),
                    reads=[self.b("tsc"), self.b("ident")], writes=[self.b(pnm)])
        g = self.tguard()
        for s_ in range(ns):
            for k_ in (0, 4):
                pst, pnm, off = slots[k_ + s_]
                dst = self.cst[0] if k_ == 0 else self.cst[1]
                dnm = "cst0" if k_ == 0 else "cst1"
                p.add("vector", lambda e, pst=pst, off=off, s_=s_, dst=dst:
                      e.tensor_copy(dst[0:1, 128 * s_:128 * (s_ + 1)],
                                    pst[0:1, off:off + 128]),
                    reads=[self.b(pnm)], writes=[self.b(dnm)], deps=[g])
        p.add("tensor", lambda e: e.matmul(
            self.psD[:, 0:512], self.onesF[0:1, :], self.cst[0][0:1, 0:512],
            start=True, stop=True),
            reads=[self.b("cst0"), self.b("onesF")], writes=[self.b("psD")])
        p.add("tensor", lambda e: e.matmul(
            self.psD[:, 512:1024], self.onesF[0:1, :], self.cst[1][0:1, 0:512],
            start=True, stop=True),
            reads=[self.b("cst1"), self.b("onesF")], writes=[self.b("psD")])
        p.add("vector", lambda e: e.tensor_copy(self.bcsb[:, 0:512],
                                                self.psD[:, 0:512]),
              reads=[self.b("psD")], writes=[self.b("bcsb")])
        p.add("vector", lambda e: e.tensor_copy(self.bcsb[:, 512:1024],
                                                self.psD[:, 512:1024]),
              reads=[self.b("psD")], writes=[self.b("bcsb")])

    def emit_ffn(self, layer):
        nc, p = self.nc, self.p
        w1T = self.wqTp[layer]
        w2T = self.wqT[layer][1]
        bw1T = self.b(f"w1qTp_l{layer}")
        bw2T = self.b(f"w2qT_l{layer}")
        cw = (self.cw0, self.cw1)[layer]
        cb = (self.cb0, self.cb1)[layer]
        jcol1 = 2 * layer + 0
        jcol2 = 2 * layer + 1
        cwv = self.cwsb[:].rearrange("p (c t) -> p c t", t=4)
        p.add("sync", lambda e, cw=cw, cwv=cwv: e.dma_start(
            out=cwv[:, :, 0:3], in_=cw.rearrange("(c p) t -> p c t", p=128),
            allow_slow_non_contiguous=True), writes=[self.b("cwsb")])
        p.add("sync", lambda e, cb=cb, cwv=cwv: e.dma_start(
            out=cwv[:, :, 3:4], in_=cb.rearrange("(c p one) -> p c one", p=128, one=1),
            allow_slow_non_contiguous=True), writes=[self.b("cwsb")])
        p.add("vector", lambda e: e.memset(self.saved2[:], 0.0),
              writes=[self.b("saved2")])
        hsv = self.hs[:].rearrange("p (c t) -> p c t", t=FT)
        sv2 = self.saved2[:].rearrange("p (c t) -> p c t", t=2)
        h0v = self.h0sv[:].rearrange("p (c t) -> p c t", t=2)
        bpv = self.bpay[:].rearrange("p (c t) -> p c t", t=2)
        for T in range(NFT):
            self.emit_quant_x(layer, T, jcol1)
            if T == 0 and layer == 0:
                self.tap("ser1", self.qp[7][:], self.b("qp7"), BF16)
            wN = 510 if T == 0 else 512
            w0_ = 2 if T == 0 else 0
            for hc in range(NFH):
                wb = self.w1buf[hc % 2]
                bwb = self.b(f"w1buf{hc % 2}")
                p.add("sync", lambda e, w1T=w1T, hc=hc, wb=wb: e.dma_start(
                    out=wb[:], in_=w1T[:, 1024 * hc:1024 * (hc + 1)]),
                    reads=[bw1T], writes=[bwb])
                psv = self.psE if hc % 2 == 0 else self.psO
                bps = self.b("psE") if hc % 2 == 0 else self.b("psO")
                for dc in range(NCH):
                    p.add("tensor", lambda e, dc=dc, wb=wb, psv=psv: e.matmul(
                        psv[:], wb[:, 128 * dc:128 * (dc + 1)], self.qp[dc][:],
                        start=(dc == 0), stop=(dc == NCH - 1)),
                        reads=[bwb, self.b(f"qp{dc}")], writes=[bps])
                hx = self.hx[hc % 2]
                bhx = self.b(f"hx{hc % 2}")
                cst = self.cst[hc % 2]
                bcst = self.b(f"cst{hc % 2}")
                p.add("vector", lambda e, hc=hc, sv2=sv2, hx=hx: e.tensor_copy(
                    hx[:, 0:2], sv2[:, hc, :]),
                    reads=[self.b("saved2")], writes=[bhx])
                p.add("scalar", lambda e, psv=psv, hx=hx: e.copy(hx[:, 2:514],
                                                                 psv[:]),
                      reads=[bps], writes=[bhx])
                if T == 0 and layer == 0 and hc == 0:
                    self.tap("h0", hx[:], bhx)
                p.add("vector", lambda e, hc=hc, sv2=sv2, hx=hx: e.tensor_copy(
                    sv2[:, hc, :], hx[:, 512:514]),
                    reads=[bhx], writes=[self.b("saved2")])
                if T == 0:
                    p.add("vector", lambda e, hc=hc, h0v=h0v, hx=hx: e.tensor_copy(
                        h0v[:, hc, :], hx[:, 2:4]),
                        reads=[bhx], writes=[self.b("h0sv")])
                    p.add("vector", lambda e, hc=hc, bpv=bpv, hx=hx: e.tensor_copy(
                        bpv[:, hc, 1:2], hx[:, 2:3]),
                        reads=[bhx], writes=[self.b("bpay")])
                if T == NFT - 1:
                    p.add("vector", lambda e, hc=hc, bpv=bpv, hx=hx: e.tensor_copy(
                        bpv[:, hc, 0:1], hx[:, 513:514]),
                        reads=[bhx], writes=[self.b("bpay")])
                p.add("vector", lambda e, hc=hc, cwv=cwv, w0_=w0_, wN=wN, hx=hx,
                      cst=cst: e.tensor_scalar(
                    out=cst[:, 0:wN], in0=hx[:, w0_:w0_ + wN],
                    scalar1=cwv[:, hc, 0:1], scalar2=cwv[:, hc, 3:4],
                    op0=ALU.mult, op1=ALU.add),
                    reads=[bhx, self.b("cwsb")], writes=[bcst])
                p.add("vector", lambda e, hc=hc, cwv=cwv, w0_=w0_, wN=wN, hx=hx,
                      cst=cst: e.scalar_tensor_tensor(
                    out=cst[:, 0:wN], in0=hx[:, w0_ + 1:w0_ + 1 + wN],
                    scalar=cwv[:, hc, 1:2], in1=cst[:, 0:wN],
                    op0=ALU.mult, op1=ALU.add),
                    reads=[bhx, self.b("cwsb")], writes=[bcst])
                p.add("vector", lambda e, hc=hc, cwv=cwv, w0_=w0_, wN=wN, hx=hx,
                      cst=cst: e.scalar_tensor_tensor(
                    out=cst[:, 0:wN], in0=hx[:, w0_ + 2:w0_ + 2 + wN],
                    scalar=cwv[:, hc, 2:3], in1=cst[:, 0:wN],
                    op0=ALU.mult, op1=ALU.add),
                    reads=[bhx, self.b("cwsb")], writes=[bcst])
                p.add("scalar", lambda e, hc=hc, hsv=hsv, wN=wN, cst=cst:
                      e.activation(
                    hsv[:, hc, 0:wN], cst[:, 0:wN], AF.Silu),
                    reads=[bcst], writes=[self.b("hs")])
                if T == 0 and layer == 0 and hc == 0:
                    self.tap("y0", self.cst[0][:], self.b("cst0"))
                    self.tap("hs0", self.hs[:, 0:512], self.b("hs"), BF16)
                if T == 0 and layer == 0 and hc == 15:
                    self.tap("ser2", self.hs[:, 7680:8192], self.b("hs"), BF16)
                if hc == 0:
                    p.add("vector", lambda e, hsv=hsv, wN=wN: e.tensor_copy(
                        self.runmax[:, 0:wN], hsv[:, 0, 0:wN]),
                        reads=[self.b("hs")], writes=[self.b("runmax")])
                    p.add("vector", lambda e, hsv=hsv, wN=wN: e.tensor_copy(
                        self.runmin[:, 0:wN], hsv[:, 0, 0:wN]),
                        reads=[self.b("hs")], writes=[self.b("runmin")])
                else:
                    p.add("vector", lambda e, hc=hc, hsv=hsv, wN=wN: e.tensor_max(
                        self.runmax[:, 0:wN], self.runmax[:, 0:wN], hsv[:, hc, 0:wN]),
                        reads=[self.b("hs")], writes=[self.b("runmax")])
                    p.add("vector", lambda e, hc=hc, hsv=hsv, wN=wN: e.tensor_tensor(
                        self.runmin[:, 0:wN], self.runmin[:, 0:wN], hsv[:, hc, 0:wN],
                        ALU.min),
                        reads=[self.b("hs")], writes=[self.b("runmin")])
            self.emit_tile_tail(layer, T, jcol2, w2T, bw2T)
        self.emit_halo_fix(layer, jcol2, w2T, bw2T)
        self.emit_ln_apply()

    def emit_quant_x(self, layer, T, jcol1):
        nc, p = self.nc, self.p
        sl = slice(FT * T, FT * (T + 1))
        for fc in range(NCH):
            if fc == 0:
                p.add("vector", lambda e, sl=sl: e.tensor_copy(
                    self.tsc[:, 0:512], self.stream[0][:, sl]),
                    reads=[self.b("stream0")], writes=[self.b("tsc")])
                p.add("vector", lambda e, sl=sl: e.tensor_copy(
                    self.ysc[:, 0:512], self.stream[0][:, sl]),
                    reads=[self.b("stream0")], writes=[self.b("ysc")])
            else:
                p.add("vector", lambda e, fc=fc, sl=sl: e.tensor_max(
                    self.tsc[:, 0:512], self.tsc[:, 0:512], self.stream[fc][:, sl]),
                    reads=[self.b(f"stream{fc}")], writes=[self.b("tsc")])
                p.add("vector", lambda e, fc=fc, sl=sl: e.tensor_tensor(
                    self.ysc[:, 0:512], self.ysc[:, 0:512], self.stream[fc][:, sl],
                    ALU.min),
                    reads=[self.b(f"stream{fc}")], writes=[self.b("ysc")])
        self.quant_scale(self.tsc, self.b("tsc"), self.ysc, self.b("ysc"),
                         512, FT * T, jcol1, False)
        if T == 0 and layer == 0:
            self.tap("xasinv", self.bcsb[:], self.b("bcsb"))
            self.tap("mrowt", self.mrow[:], self.b("mrow"))
            self.tap("tsct", self.tsc[0:128, 14:26], self.b("tsc"))
        if T == 0 and layer == 1:
            self.tap("xasinv1", self.bcsb[:], self.b("bcsb"))
            self.tap("tsct1", self.tsc[0:128, 14:26], self.b("tsc"))
        for fc in range(NCH):
            p.add("vector", lambda e, fc=fc, sl=sl: e.tensor_tensor(
                self.ysc[:, 0:512], self.stream[fc][:, sl], self.bcsb[:, 0:512],
                ALU.mult),
                reads=[self.b(f"stream{fc}"), self.b("bcsb")], writes=[self.b("ysc")])
            p.add("vector", lambda e: e.tensor_scalar(
                out=self.ysc[:, 0:512], in0=self.ysc[:, 0:512], scalar1=MAGIC,
                scalar2=MAGIC, op0=ALU.add, op1=ALU.subtract),
                reads=[self.b("ysc")], writes=[self.b("ysc")])
            p.add("vector", lambda e, fc=fc: e.tensor_tensor(
                self.qp[fc][:], self.ysc[:, 0:512], self.bcsb[:, 512:1024], ALU.mult),
                reads=[self.b("ysc"), self.b("bcsb")], writes=[self.b(f"qp{fc}")])
            if T == 0 and layer == 0 and fc == 0:
                self.tap("xq0", self.qp[0][:], self.b("qp0"), BF16)

    def emit_tile_tail(self, layer, T, jcol2, w2T, bw2T):
        nc, p = self.nc, self.p
        wN = 510 if T == 0 else 512
        toks0 = FT * T - 1 if T > 0 else 1
        hsv = self.hs[:].rearrange("p (c t) -> p c t", t=FT)
        if T == 0 and layer == 0:
            self.tap("ser3", self.runmax[:], self.b("runmax"), BF16)
            self.tap("hsall", self.hs[:], self.b("hs"), BF16)
        self.quant_scale(self.runmax, self.b("runmax"), self.runmin, self.b("runmin"),
                         wN, toks0, jcol2, True)
        if T == 0 and layer == 0:
            self.tap("hasinv", self.bcsb[:], self.b("bcsb"))
        if T == 0 and layer == 1:
            self.tap("hasinv1", self.bcsb[:], self.b("bcsb"))
            self.tap("hsall1", self.hs[:], self.b("hs"), BF16)
        for hc in range(NFH):
            p.add("vector", lambda e, hc=hc, hsv=hsv, wN=wN: e.tensor_tensor(
                self.ysc[:, 0:wN], hsv[:, hc, 0:wN], self.bcsb[:, 0:wN], ALU.mult),
                reads=[self.b("hs"), self.b("bcsb")], writes=[self.b("ysc")])
            p.add("vector", lambda e, hc=hc, hsv=hsv, wN=wN: e.tensor_scalar(
                out=hsv[:, hc, 0:wN], in0=self.ysc[:, 0:wN], scalar1=MAGIC,
                scalar2=MAGIC, op0=ALU.add, op1=ALU.subtract),
                reads=[self.b("ysc")], writes=[self.b("hs")])
        for dc in range(NCH):
            bwb = self.b("w2buf")
            p.add("sync", lambda e, w2T=w2T, dc=dc: e.dma_start(
                out=self.w2buf[:].rearrange("p (hc d) -> p hc d", d=128),
                in_=w2T.rearrange("(hc p) d -> p hc d", p=128)[
                    :, :, 128 * dc:128 * (dc + 1)]),
                reads=[bw2T], writes=[bwb])
            psv = self.psE if dc % 2 == 0 else self.psO
            bps = self.b("psE") if dc % 2 == 0 else self.b("psO")
            for hc in range(NFH):
                p.add("tensor", lambda e, hc=hc, psv=psv, wN=wN, hsv=hsv: e.matmul(
                    psv[:, 0:wN], self.w2buf[:, 128 * hc:128 * (hc + 1)],
                    hsv[:, hc, 0:wN], start=(hc == 0), stop=(hc == NFH - 1)),
                    reads=[bwb, self.b("hs")], writes=[bps])
            sl = slice(toks0, toks0 + wN)
            p.add("vector", lambda e, psv=psv, wN=wN: e.tensor_tensor(
                self.tsc[:, 0:wN], psv[:, 0:wN], self.bcsb[:, 512:512 + wN], ALU.mult),
                reads=[bps, self.b("bcsb")], writes=[self.b("tsc")])
            p.add("vector", lambda e, dc=dc, sl=sl, wN=wN: e.tensor_tensor(
                self.stream[dc][:, sl], self.stream[dc][:, sl], self.tsc[:, 0:wN],
                ALU.add),
                reads=[self.b("tsc")], writes=[self.b(f"stream{dc}")])
            p.add("vector", lambda e, dc=dc, sl=sl, wN=wN: e.tensor_copy(
                self.tsc2[:, 0:wN], self.stream[dc][:, sl]),
                reads=[self.b(f"stream{dc}")], writes=[self.b("tsc2")])
            p.add("vector", lambda e, dc=dc, sl=sl, wN=wN: e.tensor_mul(
                self.tsc2[:, 2048:2048 + wN], self.stream[dc][:, sl],
                self.stream[dc][:, sl]),
                reads=[self.b(f"stream{dc}")], writes=[self.b("tsc2")])
            p.add("tensor", lambda e, dc=dc, wN=wN: e.matmul(
                self.psA[0][0:1, 0:wN], self.ones1b[:], self.tsc2[:, 0:wN],
                start=(dc == 0), stop=(dc == NCH - 1)),
                reads=[self.b("tsc2"), self.b("ones1b")], writes=[self.b("psA0")])
            p.add("tensor", lambda e, dc=dc, wN=wN: e.matmul(
                self.psA[0][0:1, 512:512 + wN], self.ones1b[:],
                self.tsc2[:, 2048:2048 + wN],
                start=(dc == 0), stop=(dc == NCH - 1)),
                reads=[self.b("tsc2"), self.b("ones1b")], writes=[self.b("psA0")])
        p.add("vector", lambda e, toks0=toks0, wN=wN: e.tensor_copy(
            self.rowsSS[0:1, toks0:toks0 + wN], self.psA[0][0:1, 0:wN]),
            reads=[self.b("psA0")], writes=[self.b("rowsSS")])
        p.add("vector", lambda e, toks0=toks0, wN=wN: e.tensor_copy(
            self.rowsSS[0:1, TOK + toks0:TOK + toks0 + wN],
            self.psA[0][0:1, 512:512 + wN]),
            reads=[self.b("psA0")], writes=[self.b("rowsSS")])
        if T == 0 and layer == 0:
            self.tap("rh0", self.hs[:, 0:512], self.b("hs"), BF16)
            self.tap("res0", self.stream[0][:, 0:512], self.b("stream0"))

    def emit_halo_fix(self, layer, jcol2, w2T, bw2T):
        nc, p = self.nc, self.p
        hin, hout = self.hin[layer], self.hout[layer]
        bhin, bhout = self.b(f"hin{layer}"), self.b(f"hout{layer}")
        bpv = self.bpay[:].rearrange("p (c t) -> p c t", t=2)
        bhL = self.bhalo[:, 0:NFH].rearrange("p (c one) -> p c one", one=1)
        bhR = self.bhalo[:, NFH:2 * NFH].rearrange("p (c one) -> p c one", one=1)
        cwv = self.cwsb[:].rearrange("p (c t) -> p c t", t=4)
        h0v = self.h0sv[:].rearrange("p (c t) -> p c t", t=2)
        sv2 = self.saved2[:].rearrange("p (c t) -> p c t", t=2)
        t32 = self.tsc[:, 0:2 * NFH].rearrange("p (a c) -> p a c", a=2)
        p.add("vector", lambda e, t32=t32, bpv=bpv: e.tensor_scalar(
            out=t32[:, 0:1, :], in0=bpv[:, :, 0:1].rearrange("p c t -> p t c"),
            scalar1=self.mrow[:, 0:1], scalar2=None, op0=ALU.mult),
            reads=[self.b("bpay"), self.b("mrow")], writes=[self.b("tsc")])
        p.add("vector", lambda e, t32=t32, bpv=bpv: e.tensor_scalar(
            out=t32[:, 1:2, :], in0=bpv[:, :, 1:2].rearrange("p c t -> p t c"),
            scalar1=self.mrow[:, 1:2], scalar2=None, op0=ALU.mult),
            reads=[self.b("bpay"), self.b("mrow")], writes=[self.b("tsc")])
        p.add("stw", lambda e, hin=hin, t32=t32: e.dma_start(
            out=hin.rearrange("r (p c) -> p r c", p=128), in_=t32),
            reads=[self.b("tsc")], writes=[bhin])
        p.add("gpsimd", lambda e, hin=hin, hout=hout: e.collective_compute(
            "AllReduce", ALU.add, replica_groups=GROUPS, ins=[hin], outs=[hout]),
            reads=[bhin], writes=[bhout])
        p.add("sbb", lambda e, hout=hout: e.dma_start(
            out=self.bhalo[:].rearrange("p (t c) -> p t c", c=NFH),
            in_=hout.rearrange("r (p c) -> p r c", p=128)),
            reads=[bhout], writes=[self.b("bhalo")])
        p.add("vector", lambda e: e.tensor_scalar(
            out=self.bhalo[:, 0:NFH], in0=self.bhalo[:, 0:NFH],
            scalar1=self.mrow[:, 1:2], scalar2=None, op0=ALU.mult),
            reads=[self.b("mrow")], writes=[self.b("bhalo")])
        p.add("vector", lambda e: e.tensor_scalar(
            out=self.bhalo[:, NFH:2 * NFH], in0=self.bhalo[:, NFH:2 * NFH],
            scalar1=self.mrow[:, 0:1], scalar2=None, op0=ALU.mult),
            reads=[self.b("mrow")], writes=[self.b("bhalo")])
        y2 = self.ysc[:, 0:2 * NFH].rearrange("p (c t) -> p c t", t=2)
        scr = self.tsc[:, 64:64 + NFH].rearrange("p (c one) -> p c one", one=1)
        p.add("vector", lambda e, y2=y2, bhL=bhL, cwv=cwv: e.tensor_mul(
            y2[:, :, 0:1], bhL, cwv[:, :, 0:1]),
            reads=[self.b("bhalo"), self.b("cwsb")], writes=[self.b("ysc")])
        p.add("vector", lambda e, y2=y2, cwv=cwv: e.tensor_add(
            y2[:, :, 0:1], y2[:, :, 0:1], cwv[:, :, 3:4]),
            reads=[self.b("cwsb")], writes=[self.b("ysc")])
        p.add("vector", lambda e, scr=scr, h0v=h0v, cwv=cwv: e.tensor_mul(
            scr, h0v[:, :, 0:1], cwv[:, :, 1:2]),
            reads=[self.b("h0sv"), self.b("cwsb")], writes=[self.b("tsc")])
        p.add("vector", lambda e, y2=y2, scr=scr: e.tensor_add(
            y2[:, :, 0:1], y2[:, :, 0:1], scr),
            reads=[self.b("tsc")], writes=[self.b("ysc")])
        p.add("vector", lambda e, scr=scr, h0v=h0v, cwv=cwv: e.tensor_mul(
            scr, h0v[:, :, 1:2], cwv[:, :, 2:3]),
            reads=[self.b("h0sv"), self.b("cwsb")], writes=[self.b("tsc")])
        p.add("vector", lambda e, y2=y2, scr=scr: e.tensor_add(
            y2[:, :, 0:1], y2[:, :, 0:1], scr),
            reads=[self.b("tsc")], writes=[self.b("ysc")])
        p.add("vector", lambda e, y2=y2, sv2=sv2, cwv=cwv: e.tensor_mul(
            y2[:, :, 1:2], sv2[:, :, 0:1], cwv[:, :, 0:1]),
            reads=[self.b("saved2"), self.b("cwsb")], writes=[self.b("ysc")])
        p.add("vector", lambda e, y2=y2, cwv=cwv: e.tensor_add(
            y2[:, :, 1:2], y2[:, :, 1:2], cwv[:, :, 3:4]),
            reads=[self.b("cwsb")], writes=[self.b("ysc")])
        p.add("vector", lambda e, scr=scr, sv2=sv2, cwv=cwv: e.tensor_mul(
            scr, sv2[:, :, 1:2], cwv[:, :, 1:2]),
            reads=[self.b("saved2"), self.b("cwsb")], writes=[self.b("tsc")])
        p.add("vector", lambda e, y2=y2, scr=scr: e.tensor_add(
            y2[:, :, 1:2], y2[:, :, 1:2], scr),
            reads=[self.b("tsc")], writes=[self.b("ysc")])
        p.add("vector", lambda e, scr=scr, bhR=bhR, cwv=cwv: e.tensor_mul(
            scr, bhR, cwv[:, :, 2:3]),
            reads=[self.b("bhalo"), self.b("cwsb")], writes=[self.b("tsc")])
        p.add("vector", lambda e, y2=y2, scr=scr: e.tensor_add(
            y2[:, :, 1:2], y2[:, :, 1:2], scr),
            reads=[self.b("tsc")], writes=[self.b("ysc")])
        p.add("scalar", lambda e: e.activation(self.bhalo[:, 0:2 * NFH],
                                               self.ysc[:, 0:2 * NFH], AF.Silu),
              reads=[self.b("ysc")], writes=[self.b("bhalo")])
        hq2 = self.bhalo[:, 0:2 * NFH].rearrange("p (c t) -> p c t", t=2)
        ymx = self.ysc[:, 0:2 * NFH].rearrange("p (c t) -> p c t", t=2)
        ymn = self.hx[0][:, 0:2 * NFH].rearrange("p (c t) -> p c t", t=2)
        p.add("vector", lambda e: e.tensor_copy(self.ysc[:, 0:2 * NFH],
                                                self.bhalo[:, 0:2 * NFH]),
              reads=[self.b("bhalo")], writes=[self.b("ysc")])
        p.add("vector", lambda e: e.tensor_copy(self.hx[0][:, 0:2 * NFH],
                                                self.bhalo[:, 0:2 * NFH]),
              reads=[self.b("bhalo")], writes=[self.b("hx0")])
        n = NFH // 2
        while n >= 1:
            p.add("vector", lambda e, n=n, ymx=ymx: e.tensor_max(
                ymx[:, 0:n, :], ymx[:, 0:n, :], ymx[:, n:2 * n, :]),
                reads=[self.b("ysc")], writes=[self.b("ysc")])
            p.add("vector", lambda e, n=n, ymn=ymn: e.tensor_tensor(
                ymn[:, 0:n, :], ymn[:, 0:n, :], ymn[:, n:2 * n, :], ALU.min),
                reads=[self.b("hx0")], writes=[self.b("hx0")])
            n //= 2
        p.add("tensor", lambda e: _tr(e, self.psD[0:2, 0:128], self.ysc[:, 0:2],
                                              self.ident[:]),
              reads=[self.b("ysc"), self.b("ident")], writes=[self.b("psD")])
        p.add("tensor", lambda e: _tr(e, self.psD[0:2, 512:640],
                                      self.hx[0][:, 0:2],
                                              self.ident[:]),
              reads=[self.b("hx0"), self.b("ident")], writes=[self.b("psD")])
        g = self.tguard()
        p.add("vector", lambda e: e.tensor_copy(self.tsc2[0:2, 2560:2688],
                                                self.psD[0:2, 0:128]),
              reads=[self.b("psD")], writes=[self.b("tsc2")], deps=[g])
        p.add("vector", lambda e: e.tensor_copy(self.tsc2[0:2, 2688:2816],
                                                self.psD[0:2, 512:640]),
              reads=[self.b("psD")], writes=[self.b("tsc2")])
        p.add("vector", lambda e: e.reduce_max(self.tsc[0:2, 8:9],
                                               self.tsc2[0:2, 2560:2688], axis=AXX),
              reads=[self.b("tsc2")], writes=[self.b("tsc")])
        p.add("vector", lambda e: e.tensor_reduce(self.tsc[0:2, 9:10],
                                                  self.tsc2[0:2, 2688:2816],
                                                  axis=AXX, op=ALU.min),
              reads=[self.b("tsc2")], writes=[self.b("tsc")])
        p.add("vector", lambda e: e.tensor_scalar(
            out=self.tsc[0:2, 9:10], in0=self.tsc[0:2, 9:10], scalar1=-1.0,
            scalar2=None, op0=ALU.mult), reads=[self.b("tsc")], writes=[self.b("tsc")])
        p.add("vector", lambda e: e.tensor_max(self.tsc[0:2, 8:9], self.tsc[0:2, 8:9],
                                               self.tsc[0:2, 9:10]),
              reads=[self.b("tsc")], writes=[self.b("tsc")])
        p.add("vector", lambda e: e.tensor_scalar(
            out=self.tsc[0:2, 8:9], in0=self.tsc[0:2, 8:9], scalar1=1e-5,
            scalar2=None, op0=ALU.max), reads=[self.b("tsc")], writes=[self.b("tsc")])
        p.add("vector", lambda e: e.reciprocal(self.tsc[0:2, 10:11], self.tsc[0:2, 8:9]),
              reads=[self.b("tsc")], writes=[self.b("tsc")])
        p.add("vector", lambda e: e.tensor_scalar(
            out=self.tsc[0:2, 10:11], in0=self.tsc[0:2, 10:11], scalar1=127.0,
            scalar2=None, op0=ALU.mult), reads=[self.b("tsc")], writes=[self.b("tsc")])
        p.add("vector", lambda e, jcol2=jcol2: e.tensor_scalar(
            out=self.tsc[0:2, 8:9], in0=self.tsc[0:2, 8:9],
            scalar1=self.mrow[0:2, 12 + jcol2:13 + jcol2], scalar2=None, op0=ALU.mult),
            reads=[self.b("tsc"), self.b("mrow")], writes=[self.b("tsc")])
        p.add("stw", lambda e: e.dma_start(
            out=self.triph[0:1, 0:2].rearrange("one n -> n one"),
            in_=self.tsc[0:2, 10:11]),
            reads=[self.b("tsc")], writes=[self.b("triph")])
        p.add("stw", lambda e: e.dma_start(
            out=self.triph[0:1, 2:4].rearrange("one n -> n one"),
            in_=self.tsc[0:2, 8:9]),
            reads=[self.b("tsc")], writes=[self.b("triph")])
        p.add("sbc", lambda e: e.dma_start(
            out=self.bcsb[:, 0:4], in_=self.triph[0:1, 0:4].broadcast_to([128, 4])),
            reads=[self.b("triph")], writes=[self.b("bcsb")])
        p.add("vector", lambda e, hq2=hq2: e.tensor_tensor(
            self.ysc[:, 0:2 * NFH].rearrange("p (c t) -> p c t", t=2), hq2,
            self.bcsb[:, 0:2].rearrange("p (one t) -> p one t", one=1).broadcast_to([128, NFH, 2]),
            ALU.mult),
            reads=[self.b("bhalo"), self.b("bcsb")], writes=[self.b("ysc")])
        p.add("vector", lambda e: e.tensor_scalar(
            out=self.tsc2[:, 0:2 * NFH], in0=self.ysc[:, 0:2 * NFH], scalar1=MAGIC,
            scalar2=MAGIC, op0=ALU.add, op1=ALU.subtract),
            reads=[self.b("ysc")], writes=[self.b("tsc2")])
        for dc in range(NCH):
            bwb = self.b("w2buf")
            p.add("sync", lambda e, w2T=w2T, dc=dc: e.dma_start(
                out=self.w2buf[:].rearrange("p (hc d) -> p hc d", d=128),
                in_=w2T.rearrange("(hc p) d -> p hc d", p=128)[
                    :, :, 128 * dc:128 * (dc + 1)]),
                reads=[bw2T], writes=[bwb])
            for hc in range(NFH):
                p.add("tensor", lambda e, hc=hc: e.matmul(
                    self.psE[:, 0:2], self.w2buf[:, 128 * hc:128 * (hc + 1)],
                    self.tsc2[:, 2 * hc:2 * hc + 2], start=(hc == 0),
                    stop=(hc == NFH - 1)),
                    reads=[bwb, self.b("tsc2")], writes=[self.b("psE")])
            p.add("vector", lambda e: e.tensor_tensor(
                self.tsc[:, 4:6], self.psE[:, 0:2], self.bcsb[:, 2:4], ALU.mult),
                reads=[self.b("psE"), self.b("bcsb")], writes=[self.b("tsc")])
            p.add("vector", lambda e, dc=dc: e.tensor_add(
                self.stream[dc][:, 0:1], self.stream[dc][:, 0:1], self.tsc[:, 4:5]),
                reads=[self.b("tsc")], writes=[self.b(f"stream{dc}")])
            p.add("vector", lambda e, dc=dc: e.tensor_add(
                self.stream[dc][:, TOK - 1:TOK], self.stream[dc][:, TOK - 1:TOK],
                self.tsc[:, 5:6]),
                reads=[self.b("tsc")], writes=[self.b(f"stream{dc}")])
            p.add("vector", lambda e, dc=dc: e.tensor_copy(
                self.tsc2[:, 3000:3001], self.stream[dc][:, 0:1]),
                reads=[self.b(f"stream{dc}")], writes=[self.b("tsc2")])
            p.add("vector", lambda e, dc=dc: e.tensor_copy(
                self.tsc2[:, 3001:3002], self.stream[dc][:, TOK - 1:TOK]),
                reads=[self.b(f"stream{dc}")], writes=[self.b("tsc2")])
            p.add("vector", lambda e: e.tensor_mul(
                self.tsc2[:, 3002:3004], self.tsc2[:, 3000:3002],
                self.tsc2[:, 3000:3002]),
                reads=[self.b("tsc2")], writes=[self.b("tsc2")])
            p.add("tensor", lambda e, dc=dc: e.matmul(
                self.psA[1][0:1, 0:4], self.ones1b[:], self.tsc2[:, 3000:3004],
                start=(dc == 0), stop=(dc == NCH - 1)),
                reads=[self.b("tsc2"), self.b("ones1b")], writes=[self.b("psA1")])
        p.add("vector", lambda e: e.tensor_copy(self.rowsSS[0:1, 0:1],
                                                self.psA[1][0:1, 0:1]),
              reads=[self.b("psA1")], writes=[self.b("rowsSS")])
        p.add("vector", lambda e: e.tensor_copy(self.rowsSS[0:1, TOK - 1:TOK],
                                                self.psA[1][0:1, 1:2]),
              reads=[self.b("psA1")], writes=[self.b("rowsSS")])
        p.add("vector", lambda e: e.tensor_copy(self.rowsSS[0:1, TOK:TOK + 1],
                                                self.psA[1][0:1, 2:3]),
              reads=[self.b("psA1")], writes=[self.b("rowsSS")])
        p.add("vector", lambda e: e.tensor_copy(
            self.rowsSS[0:1, 2 * TOK - 1:2 * TOK], self.psA[1][0:1, 3:4]),
            reads=[self.b("psA1")], writes=[self.b("rowsSS")])

    def emit_output(self):
        nc, p = self.nc, self.p
        sts = []
        for fc in range(NCH):
            sts.append(p.add("sync", lambda e, fc=fc: e.dma_start(
                out=self.out[128 * fc:128 * (fc + 1), :], in_=self.stream[fc][:]),
                reads=[self.b(f"stream{fc}")]))
        p.add("sync", lambda e: e.nop(), deps=sts)


def build_program(stage=8):
    return K(stage=stage).build()


def kernel(**inputs):
    stage = int(os.environ.get("STAGE", "8"))
    nc = build_program(stage)
    x = np.ascontiguousarray(np.asarray(inputs["L0_attn_x"], np.float32))
    in_maps = []
    for core in range(NCORE):
        b_, h_ = core // 2, core % 2
        m = {
            "x_own": np.ascontiguousarray(x[b_, TOK * h_:TOK * (h_ + 1), :]),
            "x_oth": np.ascontiguousarray(x[b_, TOK * (1 - h_):TOK * (2 - h_), :]),
            "hmask": np.array([[1.0 - h_, float(h_)]], np.float32),
            "L0_attn_Wq": np.asarray(inputs["L0_attn_Wq"], np.float32),
            "L0_attn_Wkv": np.asarray(inputs["L0_attn_Wkv"], np.float32),
            "L0_attn_k_scale": np.ascontiguousarray(
                np.asarray(inputs["L0_attn_k_scale"], np.float32).reshape(-1)),
            "L0_ffn_w1": np.asarray(inputs["L0_ffn_w1"], np.float32),
            "L0_ffn_conv_w": np.ascontiguousarray(
                np.asarray(inputs["L0_ffn_conv_w"], np.float32).reshape(FF, 3)),
            "L0_ffn_conv_b": np.asarray(inputs["L0_ffn_conv_b"], np.float32),
            "L0_ffn_w2": np.asarray(inputs["L0_ffn_w2"], np.float32),
            "L1_attn_Wq": np.asarray(inputs["L1_attn_Wq"], np.float32),
            "L1_attn_Wk": np.asarray(inputs["L1_attn_Wk"], np.float32),
            "L1_attn_Wv": np.asarray(inputs["L1_attn_Wv"], np.float32),
            "L1_attn_Wo": np.asarray(inputs["L1_attn_Wo"], np.float32),
            "L1_attn_quadrant_weights": np.ascontiguousarray(
                np.asarray(inputs["L1_attn_quadrant_weights"],
                           np.float32).reshape(1, 4)),
            "L1_ffn_w1": np.asarray(inputs["L1_ffn_w1"], np.float32),
            "L1_ffn_conv_w": np.ascontiguousarray(
                np.asarray(inputs["L1_ffn_conv_w"], np.float32).reshape(FF, 3)),
            "L1_ffn_conv_b": np.asarray(inputs["L1_ffn_conv_b"], np.float32),
            "L1_ffn_w2": np.asarray(inputs["L1_ffn_w2"], np.float32),
        }
        in_maps.append(m)
    res = run_bass_kernel_spmd(nc, in_maps, core_ids=list(range(NCORE)))
    out = np.empty((4, SEQ, D), np.float32)
    for core in range(NCORE):
        b_, h_ = core // 2, core % 2
        out[b_, TOK * h_:TOK * (h_ + 1), :] = res.results[core]["out"].T
    return out


# version 13 (final)
# speedup vs baseline: 1.1647x; 1.0122x over previous
import os
import numpy as np
from contextlib import ExitStack

import concourse.bass as bass
import concourse.mybir as mybir
from concourse.bass_utils import run_bass_kernel_spmd

F32 = mybir.dt.float32
BF16 = mybir.dt.bfloat16
AF = mybir.ActivationFunctionType
ALU = mybir.AluOpType
AXX = mybir.AxisListType.X

MAGIC = float(np.float32(1.5 * 2 ** 23))
NCORE = 8
TOK = 2048
SEQ = 4096
D = 1024
NCH = 8
FF = 4096
NFH = 32
NH = 16
DH = 64
NJC = 32
BLK = 512
NBLK = TOK // BLK
FT = 512
NFT = TOK // FT
EXPSHIFT = 70.0
GROUPS = [[0, 1], [2, 3], [4, 5], [6, 7]]

DMA_LANES = {"sync", "sdma2", "gdma", "sbc", "stw", "svt", "sbb", "swl",
             "swt"}
SERIAL_LANES = {"sbc", "stw", "sbb"}
DELAY_SRC = set(os.environ.get("DELAY_SRC", "tensor").split(",")) - {""}
DELAY_CYC = int(os.environ.get("DELAY_CYC", "96"))
LANES = ["tensor", "vector", "scalar", "gpsimd", "sync", "sdma2", "gdma",
         "sbc", "stw", "svt", "sbb", "swl", "swt"]
LANE_STREAM = {"tensor": "tensor", "vector": "vector", "scalar": "scalar",
               "gpsimd": "gpsimd", "sync": "sync", "sdma2": "scalar",
               "gdma": "gpsimd", "sbc": "sync", "stw": "scalar",
               "svt": "sync", "sbb": "sync", "swl": "sync", "swt": "scalar"}


class Step:
    __slots__ = ("lane", "fn", "val", "deps", "label")

    def __init__(self, lane, fn, val, deps, label):
        self.lane, self.fn, self.val, self.deps, self.label = \
            lane, fn, val, deps, label


class Buf:
    __slots__ = ("name", "w", "readers")

    def __init__(self, name):
        self.name = name
        self.w = None
        self.readers = []


class Plan:
    def __init__(self):
        self.steps = []
        self.count = {e: 0 for e in LANES}

    def add(self, lane, fn, reads=(), writes=(), deps=(), label=""):
        d = [x for x in deps if x is not None]
        for b in reads:
            if b.w is not None:
                d.append(b.w)
        for b in writes:
            if b.w is not None:
                d.append(b.w)
            d.extend(b.readers)
        inc = 16 if lane in DMA_LANES else 1
        self.count[lane] += inc
        st = Step(lane, fn, self.count[lane], d, label)
        self.steps.append(st)
        for b in reads:
            b.readers.append(st)
        for b in writes:
            b.w = st
            b.readers = []
        return st


def _skip_dep(lane, dep):
    if dep.lane == lane and lane == "tensor":
        return True
    if dep.lane == lane and lane in DMA_LANES:
        return True
    return False


def emit_all(nc, block, plan):
    sems = {}
    for e in LANES:
        if any(st.lane == e for st in plan.steps):
            sems[e] = nc.semaphore(f"sem_{e}").__enter__()

    def make_body(lanes):
        steps = [st for st in plan.steps if st.lane in lanes]

        def body(eng):
            seen = {e: 0 for e in LANES}
            for st in steps:
                need = {}
                for dep in st.deps:
                    if _skip_dep(st.lane, dep):
                        continue
                    if dep.val > need.get(dep.lane, 0):
                        need[dep.lane] = dep.val
                fresh_wait_lanes = set()
                for lane_, val in need.items():
                    if seen[lane_] >= val:
                        continue
                    eng.wait_ge(sems[lane_], val)
                    seen[lane_] = val
                    fresh_wait_lanes.add(lane_)
                if st.lane in ("vector", "scalar", "gpsimd") and \
                        fresh_wait_lanes & DELAY_SRC:
                    eng.nop()
                    eng.nop()
                    eng.nop()
                if st.lane in SERIAL_LANES and st.val > 16 and \
                        seen[st.lane] < st.val - 16:
                    eng.wait_ge(sems[st.lane], st.val - 16)
                    seen[st.lane] = st.val - 16
                inst = st.fn(eng)
                if inst is None:
                    inst = eng.nop()
                inst.then_inc(sems[st.lane], 16 if st.lane in DMA_LANES else 1)
            for lane_ in LANES:
                if lane_ in sems and plan.count[lane_] > seen[lane_]:
                    eng.wait_ge(sems[lane_], plan.count[lane_])
        return body

    streams = {}
    for lane in LANES:
        if any(st.lane == lane for st in plan.steps):
            streams.setdefault(LANE_STREAM[lane], []).append(lane)
    for phys, fn in (("tensor", block.tensor), ("vector", block.vector),
                     ("scalar", block.scalar), ("gpsimd", block.gpsimd),
                     ("sync", block.sync)):
        if phys in streams:
            fn(make_body(tuple(streams[phys])))
    return sems



def _tr(e, out, in_, ident):
    return e.transpose(out, in_, ident)

def make_identity(nc, identity, base=0):
    nc.gpsimd.memset(identity, 0.0)
    nc.gpsimd.nop()
    nc.gpsimd.affine_select(
        out=identity, in_=identity, compare_op=ALU.not_equal,
        fill=1.0, base=base, pattern=[[-1, identity.shape[1]]],
        channel_multiplier=1)


class K:
    def __init__(self, stage=8):
        self.stage = stage
        self.want_taps = set(os.environ.get("TAPS", "").split(",")) - {""}
        nc = bass.Bass("TRN2", target_bir_lowering=False)
        self.nc = nc
        self.p = Plan()
        self.stk = ExitStack()
        self.bufs = {}

        def inp(name, shape):
            return nc.dram_tensor(name, shape, F32, kind="ExternalInput").ap()

        self.x_own = inp("x_own", [TOK, D])
        self.x_oth = inp("x_oth", [TOK, D])
        self.hmask = inp("hmask", [1, 2])
        self.wq0 = inp("L0_attn_Wq", [D, D])
        self.wkv0 = inp("L0_attn_Wkv", [D, 512])
        self.kscale = inp("L0_attn_k_scale", [NH * DH])
        self.w1_0 = inp("L0_ffn_w1", [FF, D])
        self.cw0 = inp("L0_ffn_conv_w", [FF, 3])
        self.cb0 = inp("L0_ffn_conv_b", [FF])
        self.w2_0 = inp("L0_ffn_w2", [D, FF])
        self.wq1 = inp("L1_attn_Wq", [D, D])
        self.wk1 = inp("L1_attn_Wk", [D, D])
        self.wv1 = inp("L1_attn_Wv", [D, D])
        self.wo1 = inp("L1_attn_Wo", [D, D])
        self.qw1 = inp("L1_attn_quadrant_weights", [1, 4])
        self.w1_1 = inp("L1_ffn_w1", [FF, D])
        self.cw1 = inp("L1_ffn_conv_w", [FF, 3])
        self.cb1 = inp("L1_ffn_conv_b", [FF])
        self.w2_1 = inp("L1_ffn_w2", [D, FF])
        self.out = nc.dram_tensor("out", [D, TOK], F32, kind="ExternalOutput").ap()

        def scr(name, shape, dt=BF16):
            t = nc.dram_tensor(name, shape, dt).ap()
            self.bufs[name] = Buf(name)
            return t

        self.xb16 = scr("xb16", [SEQ, D])
        self.wqnat = [[scr(f"w{j}qnat_l{l}", [FF, D] if j == 1 else [D, FF])
                       for j in (1, 2)] for l in (0, 1)]
        self.wqT = [[scr(f"w{j}qT_l{l}", [D, FF] if j == 1 else [FF, D])
                     for j in (1, 2)] for l in (0, 1)]
        self.wqTp = [scr(f"w1qTp_l{l}", [128, NFH * NCH * 128]) for l in (0, 1)]
        self.wqTp2 = [scr(f"w2qTp_l{l}", [128, NCH * NFH * 128]) for l in (0, 1)]
        self.k0scr = scr("k0scr", [256, SEQ])
        self.v0scr = scr("v0scr", [SEQ, 256])
        self.kin = scr("kin", [D + 1, TOK])
        self.kout = scr("kout", [2 * (D + 1), TOK])
        self.vin = scr("vin", [TOK, D])
        self.vout = scr("vout", [SEQ, D])
        self.hin = [scr(f"hin{l}", [2, FF], F32) for l in (0, 1)]
        self.hout = [scr(f"hout{l}", [2, FF], F32) for l in (0, 1)]
        self.trip = scr("trip", [16, 16], F32)
        self.tripq = scr("tripq", [2, TOK], F32)
        self.qscr = scr("qscr", [128, 1024])
        self.tripa = scr("tripa", [1, 1024], F32)
        self.triph = scr("triph", [1, 8], F32)

        def sb(name, shape, dt):
            t = self.stk.enter_context(nc.sbuf_tensor(name, shape, dt))
            self.bufs[name] = Buf(name)
            return t

        self.stream = [sb(f"stream{c}", [128, TOK], F32) for c in range(NCH)]
        self.xrow = sb("xrow", [128, D], F32)
        self.xrow2 = sb("xrow2", [128, D], F32)
        self.xt = [sb(f"xt{i}", [128, BLK], BF16) for i in range(NCH)]
        self.qp = [sb(f"qp{i}", [128, BLK], BF16) for i in range(NCH)]
        self.kpair = sb("kpair", [128, SEQ], BF16)
        self.vt = [sb(f"vt{i}", [128, NJC * 65], BF16) for i in range(2)]
        self.pt = [sb(f"pt{i}", [128, 1024], BF16) for i in range(2)]
        self.nrmp = sb("nrmp", [128, 1024], BF16)
        self.bcsb = sb("bcsb", [128, 1024], F32)
        self.ao = [sb(f"ao{i}", [128, BLK], BF16) for i in range(NCH)]
        self.hs = sb("hs", [128, NFH * FT], BF16)
        self.w1buf = [sb(f"w1buf{i}", [128, NCH * 128], BF16) for i in range(2)]
        self.w2buf = sb("w2buf", [128, NFH * 128], BF16)
        self.hx = [sb("hx0", [128, 514], F32), sb("hx1", [128, 514], F32)]
        self.cst = [sb("cst0", [128, 512], F32), sb("cst1", [128, 512], F32)]
        self.ysc = sb("ysc", [128, FT], F32)
        self.tsc = sb("tsc", [128, FT], F32)
        self.tsc2 = sb("tsc2", [128, 3584], BF16)
        self.saved2 = sb("saved2", [128, NFH * 2], F32)
        self.h0sv = sb("h0sv", [128, NFH * 2], F32)
        self.bpay = sb("bpay", [128, NFH * 2], F32)
        self.bhalo = sb("bhalo", [128, NFH * 2], F32)
        self.runmax = sb("runmax", [128, FT], BF16)
        self.runmin = sb("runmin", [128, FT], BF16)
        self.cwsb = sb("cwsb", [128, NFH * 4], F32)
        self.kssb = sb("kssb", [128, NCH], F32)
        self.mrow = sb("mrow", [128, 16], F32)
        self.rowsSS = sb("rowsSS", [1, 2 * TOK], F32)
        self.rowA = self.cst[0]
        self.bufs["rowA"] = self.bufs["cst0"]
        self.srow = sb("srow", [1, 64], F32)
        self.nacc = sb("nacc", [2, 16], F32)
        self.expb = sb("expb", [128, 8], F32)
        self.onesF = sb("onesF", [128, 128], F32)
        self.ones2b = sb("ones2b", [128, 2], BF16)
        self.ones1b = sb("ones1b", [128, 1], BF16)
        self.ident = sb("ident", [128, 128], F32)
        self.identb = sb("identb", [128, 128], BF16)
        self.ishif = sb("ishif", [128, 128], BF16)
        self.wsrow = sb("wsrow", [1, 16], F32)

        def ps(name, shape):
            t = self.stk.enter_context(nc.psum_tensor(name, shape, F32))
            self.bufs[name] = Buf(name)
            return t
        self.psA = [ps(f"psA{i}", [128, 1024]) for i in range(2)]
        self.psE = ps("psE", [128, 512])
        self.psO = ps("psO", [128, 512])
        self.psD = ps("psD", [128, 1024])

    def b(self, name):
        return self.bufs[name]

    def tap(self, name, ap, buf, dt=F32):
        if name not in self.want_taps:
            return
        sh = list(ap.shape)
        t = self.nc.dram_tensor("tap_" + name, sh, dt, kind="ExternalOutput").ap()
        self.p.add("sync", lambda e, t=t, ap=ap: e.dma_start(out=t, in_=ap),
                   reads=[buf], label="tap_" + name)



    def tguard(self):
        return self.p.add("tensor", lambda e: e.matmul(
            self.psD[0:1, 1022:1024], self.ones1b[0:1, 0:1],
            self.identb[0:1, 0:2], start=True, stop=True),
            reads=[self.b("ones1b"), self.b("identb")], writes=[])

    def bcast_row(self, src_ap, src_buf, dst_ap, dst_buf, n, lane="vector"):
        p = self.p
        p.add("tensor", lambda e: e.matmul(
            self.psD[:, 0:n] if n <= 512 else None,
            self.onesF[0:1, :], src_ap, start=True, stop=True),
            reads=[src_buf, self.b("onesF")], writes=[self.b("psD")])
        p.add(lane, lambda e: e.tensor_copy(dst_ap, self.psD[0:dst_ap.shape[0], 0:n]),
              reads=[self.b("psD")], writes=[dst_buf])

    def wpv(self, dc):
        return self.hs[:, 1024 * dc:1024 * (dc + 1)]

    def wov(self, dc):
        return self.hs[:, 8192 + 1024 * dc:8192 + 1024 * (dc + 1)]

    def build(self):
        nc, p = self.nc, self.p
        with nc.Block() as block:
            self.emit_consts()
            self.emit_input_stage()
            self.tap("st0", self.stream[0][:, 0:512], self.b("stream0"))
            self.emit_weight_prep(0)
            self.tap("wsrow", self.wsrow[:], self.b("wsrow"))
            self.emit_l0_kv()
            self.tap("nacc0", self.nacc[0:2, 0:4], self.b("nacc"))
            self.emit_attention(0)
            if self.stage >= 4:
                self.emit_weight_prep(1)
                self.emit_ffn(0)
            if self.stage >= 6:
                self.emit_l1_kv()
                self.emit_attention(1)
            if self.stage >= 8:
                self.emit_ffn(1)
            self.emit_output()
            emit_all(nc, block, p)
        self.stk.close()
        return nc

    def emit_consts(self):
        nc, p = self.nc, self.p
        for nm, base in (("ident", 0), ("identb", 0), ("ishif", 64)):
            t_ = getattr(self, nm)
            p.add("gpsimd", lambda e, t_=t_: e.memset(t_[:], 0.0),
                  writes=[self.b(nm)])
            p.add("gpsimd", lambda e, t_=t_, base=base: e.affine_select(
                out=t_[:], in_=t_[:], compare_op=ALU.not_equal, fill=1.0,
                base=base, pattern=[[-1, t_.shape[1]]], channel_multiplier=1),
                reads=[self.b(nm)], writes=[self.b(nm)])
        p.add("vector", lambda e: e.memset(self.onesF[:], 1.0), writes=[self.b("onesF")])
        p.add("vector", lambda e: e.memset(self.ones1b[:], 1.0), writes=[self.b("ones1b")])
        p.add("vector", lambda e: e.memset(self.ones2b[:], 0.0), writes=[self.b("ones2b")])
        p.add("vector", lambda e: e.memset(self.ones2b[0:64, 0:1], 1.0),
              writes=[self.b("ones2b")])
        p.add("vector", lambda e: e.memset(self.ones2b[64:128, 1:2], 1.0),
              writes=[self.b("ones2b")])
        p.add("vector", lambda e: e.memset(self.srow[0:1, 30:31], 1e-5),
              writes=[self.b("srow")])
        p.add("sbc", lambda e: e.dma_start(out=self.mrow[:, 0:2],
                                            in_=self.hmask.broadcast_to([128, 2])),
              writes=[self.b("mrow")])
        p.add("sbc", lambda e: e.dma_start(
            out=self.kssb[:], in_=self.kscale.rearrange("(c p) -> p c", p=128),
            allow_slow_non_contiguous=True),
            writes=[self.b("kssb")])
        for i in range(2):
            p.add("vector", lambda e, i=i: e.memset(
                self.vt[i][:].rearrange("p (c d) -> p c d", d=65)[:, :, 64:65], 1.0),
                writes=[self.b(f"vt{i}")])

    def emit_input_stage(self):
        nc, p = self.nc, self.p
        bxr = self.b("xrow")
        for si, (src, roff) in enumerate([(self.x_own, 0), (self.x_oth, TOK)]):
            for t in range(TOK // 128):
                p.add("sync", lambda e, src=src, t=t: e.dma_start(
                    out=self.xrow[:], in_=src[128 * t:128 * (t + 1), :]), writes=[bxr])
                p.add("vector", lambda e: e.tensor_copy(self.tsc2[:, 0:D], self.xrow[:]),
                      reads=[bxr], writes=[self.b("tsc2")])
                p.add("sdma2", lambda e, roff=roff, t=t: e.dma_start(
                    out=self.xb16[roff + 128 * t:roff + 128 * (t + 1), :],
                    in_=self.tsc2[:, 0:D]), reads=[self.b("tsc2")],
                    writes=[self.b("xb16")])
                if si == 0:
                    for c in range(NCH):
                        psv = self.psA[c % 2][:, 0:128]
                        bps = self.b(f"psA{c % 2}")
                        p.add("tensor", lambda e, c=c, psv=psv: _tr(e, 
                            psv, self.xrow[:, 128 * c:128 * (c + 1)], self.ident[:]),
                            reads=[bxr, self.b("ident")], writes=[bps])
                        p.add("vector", lambda e, c=c, t=t, psv=psv: e.tensor_copy(
                            self.stream[c][:, 128 * t:128 * (t + 1)], psv),
                            reads=[bps], writes=[self.b(f"stream{c}")])

    def emit_weight_prep(self, layer):
        nc, p = self.nc, self.p
        ws = [(self.w1_0, self.w1_1)[layer], (self.w2_0, self.w2_1)[layer]]
        for j, w in enumerate(ws):
            R, C = (FF, D) if j == 0 else (D, FF)
            nat, wt = self.wqnat[layer][j], self.wqT[layer][j]
            if layer == 0 and j == 0:
                self.tap_w1 = True
            bnat = self.b(f"w{j + 1}qnat_l{layer}")
            bwt = self.b(f"w{j + 1}qT_l{layer}")
            col = 2 * layer + j
            bxr = self.b("xrow")
            ncol = C // D
            first = True
            for r in range(R // 128):
                for cc in range(ncol):
                    p.add("swl", lambda e, w=w, r=r, cc=cc: e.dma_start(
                        out=self.xrow[:], in_=w[128 * r:128 * (r + 1),
                                                D * cc:D * (cc + 1)]), writes=[bxr])
                    p.add("vector", lambda e: e.reduce_sum(
                        self.tsc[:, 0:1], self.xrow[:], axis=AXX,
                        apply_absolute_value=True),
                        reads=[bxr], writes=[self.b("tsc")])
                    if first:
                        p.add("vector", lambda e: e.tensor_copy(
                            self.tsc[:, 1:2], self.tsc[:, 0:1]),
                            reads=[self.b("tsc")], writes=[self.b("tsc")])
                        first = False
                    else:
                        p.add("vector", lambda e: e.tensor_add(
                            self.tsc[:, 1:2], self.tsc[:, 1:2], self.tsc[:, 0:1]),
                            reads=[self.b("tsc")], writes=[self.b("tsc")])
            p.add("tensor", lambda e: _tr(e, self.psD[0:1, 0:128],
                                          self.tsc[:, 1:2], self.ident[:]),
                  reads=[self.b("tsc"), self.b("ident")], writes=[self.b("psD")])
            g = self.tguard()
            p.add("vector", lambda e: e.tensor_copy(self.ysc[0:1, 0:128],
                                                    self.psD[0:1, 0:128]),
                  reads=[self.b("psD")], writes=[self.b("ysc")], deps=[g])
            p.add("vector", lambda e: e.reduce_sum(self.tsc[0:1, 2:3],
                                                   self.ysc[0:1, 0:128], axis=AXX),
                  reads=[self.b("ysc")], writes=[self.b("tsc")])
            p.add("vector", lambda e, R=R, C=C, col=col: e.tensor_scalar(
                out=self.wsrow[0:1, 4 * col:4 * col + 1], in0=self.tsc[0:1, 2:3],
                scalar1=1.0 / (R * C), scalar2=1e-5, op0=ALU.mult, op1=ALU.max),
                reads=[self.b("tsc")], writes=[self.b("wsrow")])
            p.add("vector", lambda e, col=col: e.reciprocal(
                self.wsrow[0:1, 4 * col + 1:4 * col + 2],
                self.wsrow[0:1, 4 * col:4 * col + 1]),
                reads=[self.b("wsrow")], writes=[self.b("wsrow")])
            p.add("vector", lambda e, col=col: e.tensor_scalar(
                out=self.wsrow[0:1, 4 * col + 2:4 * col + 3],
                in0=self.wsrow[0:1, 4 * col:4 * col + 1],
                scalar1=1.0 / 127.0, scalar2=None, op0=ALU.mult),
                reads=[self.b("wsrow")], writes=[self.b("wsrow")])
            p.add("tensor", lambda e, col=col: e.matmul(
                self.psD[:, 0:2], self.onesF[0:1, :],
                self.wsrow[0:1, 4 * col + 1:4 * col + 3], start=True, stop=True),
                reads=[self.b("wsrow"), self.b("onesF")], writes=[self.b("psD")])
            p.add("vector", lambda e, col=col: e.tensor_copy(
                self.mrow[:, 8 + col:9 + col], self.psD[:, 0:1]),
                reads=[self.b("psD")], writes=[self.b("mrow")])
            p.add("vector", lambda e, col=col: e.tensor_copy(
                self.mrow[:, 12 + col:13 + col], self.psD[:, 1:2]),
                reads=[self.b("psD")], writes=[self.b("mrow")])
            for r in range(R // 128):
                for cc in range(ncol):
                    idx = r * ncol + cc
                    xr = self.xrow if idx % 2 == 0 else self.xrow2
                    bx = bxr if idx % 2 == 0 else self.b("xrow2")
                    p.add("swl", lambda e, w=w, r=r, cc=cc, xr=xr: e.dma_start(
                        out=xr[:], in_=w[128 * r:128 * (r + 1),
                                         D * cc:D * (cc + 1)]), writes=[bx])
                    p.add("vector", lambda e, col=col, xr=xr: e.tensor_scalar(
                        out=xr[:], in0=xr[:],
                        scalar1=self.mrow[:, 8 + col:9 + col], scalar2=1.25,
                        op0=ALU.mult, op1=ALU.min),
                        reads=[bx, self.b("mrow")], writes=[bx])
                    p.add("vector", lambda e, xr=xr: e.tensor_scalar(
                        out=xr[:], in0=xr[:], scalar1=-1.25,
                        scalar2=MAGIC, op0=ALU.max, op1=ALU.add),
                        reads=[bx], writes=[bx])
                    p.add("vector", lambda e, cc=cc, xr=xr: e.tensor_scalar(
                        out=self.tsc2[:, D * (cc % 2):D * (cc % 2 + 1)], in0=xr[:],
                        scalar1=MAGIC, scalar2=None, op0=ALU.subtract),
                        reads=[bx], writes=[self.b("tsc2")])
                    p.add("sdma2", lambda e, nat=nat, r=r, cc=cc: e.dma_start(
                        out=nat[128 * r:128 * (r + 1), D * cc:D * (cc + 1)],
                        in_=self.tsc2[:, D * (cc % 2):D * (cc % 2 + 1)]),
                        reads=[self.b("tsc2")], writes=[bnat])
            wtp = self.wqTp[layer]
            bwtp = self.b(f"w1qTp_l{layer}")
            wtpv = wtp.rearrange("p (hc dc h) -> p hc dc h", hc=NFH, dc=NCH)
            for cc in range(C // 128):
                for rh in range(0, R, 2048):
                    rw = min(2048, R - rh)
                    p.add("swt", lambda e, nat=nat, cc=cc, rh=rh, rw=rw:
                          e.dma_start(
                        out=self.tsc2[:, 0:rw],
                        in_=nat[rh:rh + rw, 128 * cc:128 * (cc + 1)],
                        transpose=True), reads=[bnat], writes=[self.b("tsc2")])
                    if j == 0:
                        hc0 = rh // 128
                        p.add("sdma2", lambda e, wtpv=wtpv, cc=cc, hc0=hc0,
                              rw=rw: e.dma_start(
                            out=wtpv[:, hc0:hc0 + rw // 128, cc:cc + 1, :],
                            in_=self.tsc2[:, 0:rw].rearrange(
                                "p (hc one h) -> p hc one h", one=1, h=128)),
                            reads=[self.b("tsc2")], writes=[bwtp])
                    else:
                        wtp2 = self.wqTp2[layer]
                        bwtp2 = self.b(f"w2qTp_l{layer}")
                        wtp2v = wtp2.rearrange("p (dc hc d) -> p dc hc d",
                                               dc=NCH, hc=NFH)
                        d0 = rh // 128
                        p.add("sdma2", lambda e, wtp2v=wtp2v, cc=cc, d0=d0,
                              rw=rw: e.dma_start(
                            out=wtp2v[:, d0:d0 + rw // 128, cc:cc + 1, :],
                            in_=self.tsc2[:, 0:rw].rearrange(
                                "p (dc one d) -> p dc one d", one=1, d=128)),
                            reads=[self.b("tsc2")], writes=[bwtp2])

    def load_proj_weight(self, w, into="wp"):
        p = self.p
        Fout = w.shape[1]
        view = self.wpv if into == "wp" else self.wov
        for dc in range(NCH):
            p.add("gdma", lambda e, w=w, dc=dc, Fout=Fout, view=view: e.dma_start(
                out=view(dc)[:, 0:Fout], in_=w[128 * dc:128 * (dc + 1), :]),
                writes=[self.b("hs")])

    def kv_norm_acc(self, ncol, first):
        p = self.p
        p.add("vector", lambda e: e.tensor_mul(
            self.tsc2[:, 2048:2560], self.tsc2[:, 0:512], self.tsc2[:, 0:512]),
            reads=[self.b("tsc2")], writes=[self.b("tsc2")])
        p.add("tensor", lambda e: e.matmul(self.psD[0:2, 0:512], self.ones2b[:],
                                           self.tsc2[:, 2048:2560],
                                           start=True, stop=True),
              reads=[self.b("tsc2"), self.b("ones2b")], writes=[self.b("psD")])
        p.add("vector", lambda e: e.tensor_copy(self.tsc2[0:2, 2560:3072],
                                                self.psD[0:2, 0:512]),
              reads=[self.b("psD")], writes=[self.b("tsc2")])
        p.add("vector", lambda e: e.reduce_max(self.tsc[0:2, 3:4],
                                               self.tsc2[0:2, 2560:3072], axis=AXX),
              reads=[self.b("tsc2")], writes=[self.b("tsc")])
        if first:
            p.add("vector", lambda e, ncol=ncol: e.tensor_copy(
                self.nacc[0:2, ncol:ncol + 1], self.tsc[0:2, 3:4]),
                reads=[self.b("tsc")], writes=[self.b("nacc")])
        else:
            p.add("vector", lambda e, ncol=ncol: e.tensor_max(
                self.nacc[0:2, ncol:ncol + 1], self.nacc[0:2, ncol:ncol + 1],
                self.tsc[0:2, 3:4]),
                reads=[self.b("tsc")], writes=[self.b("nacc")])

    def flatten2_max(self, src_ap, src_buf, dst_ap, trip_slot):
        p = self.p
        p.add("tensor", lambda e, src_ap=src_ap: _tr(
            e, self.psD[0:1, 0:2], src_ap, self.ident[0:2, 0:2]),
            reads=[src_buf, self.b("ident")], writes=[self.b("psD")])
        g = self.tguard()
        p.add("vector", lambda e: e.tensor_copy(self.srow[0:1, 8:10],
                                                self.psD[0:1, 0:2]),
            reads=[self.b("psD")], writes=[self.b("srow")], deps=[g])
        p.add("vector", lambda e, dst_ap=dst_ap: e.reduce_max(
            dst_ap, self.srow[0:1, 8:10], axis=AXX),
            reads=[self.b("srow")], writes=[self.b("srow")])

    def emit_l0_kv(self):
        nc, p = self.nc, self.p
        self.load_proj_weight(self.wkv0)
        first_n = True
        for t in range(SEQ // BLK):
            for dc in range(NCH):
                p.add("sync", lambda e, t=t, dc=dc: e.dma_start(
                    out=self.xt[dc][:], in_=self.xb16[BLK * t:BLK * (t + 1),
                                                      128 * dc:128 * (dc + 1)],
                    transpose=True), reads=[self.b("xb16")], writes=[self.b(f"xt{dc}")])
            for kc in range(2):
                psv = self.psA[0][:, 512 * kc:512 * (kc + 1)]
                bps = self.b("psA0")
                for dc in range(NCH):
                    p.add("tensor", lambda e, kc=kc, dc=dc, psv=psv: e.matmul(
                        psv, self.wpv(dc)[:, 128 * kc:128 * (kc + 1)], self.xt[dc][:],
                        start=(dc == 0), stop=(dc == NCH - 1)),
                        reads=[self.b("hs"), self.b(f"xt{dc}")], writes=[bps])
                p.add("vector", lambda e, psv=psv: e.tensor_copy(
                    self.tsc2[:, 0:512], psv), reads=[bps], writes=[self.b("tsc2")])
                p.add("sdma2", lambda e, kc=kc, t=t: e.dma_start(
                    out=self.k0scr[128 * kc:128 * (kc + 1), BLK * t:BLK * (t + 1)],
                    in_=self.tsc2[:, 0:512]), reads=[self.b("tsc2")],
                    writes=[self.b("k0scr")])
                self.kv_norm_acc(0, first_n)
                first_n = False
            for s in range(BLK // 128):
                psv = self.psA[1][:, 0:256]
                bps = self.b("psA1")
                for dc in range(NCH):
                    p.add("tensor", lambda e, s=s, dc=dc, psv=psv: e.matmul(
                        psv, self.xt[dc][:, 128 * s:128 * (s + 1)],
                        self.wpv(dc)[:, 256:512], start=(dc == 0),
                        stop=(dc == NCH - 1)),
                        reads=[self.b("hs"), self.b(f"xt{dc}")], writes=[bps])
                p.add("vector", lambda e, psv=psv: e.tensor_copy(
                    self.tsc2[:, 1024:1280], psv), reads=[bps], writes=[self.b("tsc2")])
                p.add("sdma2", lambda e, t=t, s=s: e.dma_start(
                    out=self.v0scr[BLK * t + 128 * s:BLK * t + 128 * (s + 1), :],
                    in_=self.tsc2[:, 1024:1280]), reads=[self.b("tsc2")],
                    writes=[self.b("v0scr")])
        self.flatten2_max(self.nacc[0:2, 0:1], self.b("nacc"), self.srow[0:1, 1:2], 0)

    def emit_l1_kv(self):
        nc, p = self.nc, self.p
        p.add("sbc", lambda e: e.dma_start(out=self.srow[0:1, 16:20], in_=self.qw1),
              writes=[self.b("srow")])
        p.add("scalar", lambda e: e.activation(self.srow[0:1, 20:24],
                                               self.srow[0:1, 16:20], AF.Exp),
              reads=[self.b("srow")], writes=[self.b("srow")])
        p.add("vector", lambda e: e.reduce_sum(self.srow[0:1, 24:25],
                                               self.srow[0:1, 20:24], axis=AXX),
              reads=[self.b("srow")], writes=[self.b("srow")])
        p.add("vector", lambda e: e.reciprocal(self.srow[0:1, 25:26],
                                               self.srow[0:1, 24:25]),
              reads=[self.b("srow")], writes=[self.b("srow")])
        p.add("vector", lambda e: e.tensor_tensor(
            self.srow[0:1, 26:30], self.srow[0:1, 20:24],
            self.srow[0:1, 25:26].broadcast_to([1, 4]), ALU.mult),
            reads=[self.b("srow")], writes=[self.b("srow")])
        p.add("tensor", lambda e: e.matmul(
            self.psD[:, 0:4], self.onesF[0:1, :], self.srow[0:1, 26:30],
            start=True, stop=True),
            reads=[self.b("srow"), self.b("onesF")], writes=[self.b("psD")])
        p.add("vector", lambda e: e.tensor_copy(self.mrow[:, 4:8],
                                                self.psD[:, 0:4]),
            reads=[self.b("psD")], writes=[self.b("mrow")])
        for dc in range(NCH):
            bxr = self.b("xrow")
            p.add("sync", lambda e, dc=dc: e.dma_start(
                out=self.xrow[:], in_=self.wk1[128 * dc:128 * (dc + 1), :]),
                writes=[bxr])
            A = self.xrow[:].rearrange("p (h two c) -> p h two c", two=2, c=32)
            W = self.wpv(dc).rearrange("p (h two c) -> p h two c", two=2, c=32)
            T1 = self.tsc[:, 0:512].rearrange("p (h one c) -> p h one c",
                                              one=1, c=32)
            p.add("vector", lambda e, A=A, T1=T1: e.tensor_scalar(
                out=T1, in0=A[:, :, 0:1, :], scalar1=self.mrow[:, 4:5], scalar2=None,
                op0=ALU.mult), reads=[bxr, self.b("mrow")], writes=[self.b("tsc")])
            p.add("vector", lambda e, A=A, W=W, T1=T1: e.scalar_tensor_tensor(
                out=W[:, :, 0:1, :], in0=A[:, :, 1:2, :], scalar=self.mrow[:, 5:6],
                in1=T1, op0=ALU.mult, op1=ALU.add),
                reads=[bxr, self.b("mrow"), self.b("tsc")], writes=[self.b("hs")])
            p.add("vector", lambda e, A=A, T1=T1: e.tensor_scalar(
                out=T1, in0=A[:, :, 0:1, :], scalar1=self.mrow[:, 6:7], scalar2=None,
                op0=ALU.mult), reads=[bxr, self.b("mrow")], writes=[self.b("tsc")])
            p.add("vector", lambda e, A=A, W=W, T1=T1: e.scalar_tensor_tensor(
                out=W[:, :, 1:2, :], in0=A[:, :, 1:2, :], scalar=self.mrow[:, 7:8],
                in1=T1, op0=ALU.mult, op1=ALU.add),
                reads=[bxr, self.b("mrow"), self.b("tsc")], writes=[self.b("hs")])
        first_n = True
        for t in range(NBLK):
            for dc in range(NCH):
                p.add("vector", lambda e, t=t, dc=dc: e.tensor_copy(
                    self.xt[dc][:], self.stream[dc][:, BLK * t:BLK * (t + 1)]),
                    reads=[self.b(f"stream{dc}")], writes=[self.b(f"xt{dc}")])
            for fc in range(NCH):
                psv = self.psA[0][:, 0:512]
                bps = self.b("psA0")
                for dc in range(NCH):
                    p.add("tensor", lambda e, fc=fc, dc=dc, psv=psv: e.matmul(
                        psv, self.wpv(dc)[:, 128 * fc:128 * (fc + 1)], self.xt[dc][:],
                        start=(dc == 0), stop=(dc == NCH - 1)),
                        reads=[self.b("hs"), self.b(f"xt{dc}")], writes=[bps])
                p.add("vector", lambda e, psv=psv: e.tensor_copy(self.tsc2[:, 0:512], psv),
                      reads=[bps], writes=[self.b("tsc2")])
                p.add("sdma2", lambda e, fc=fc, t=t: e.dma_start(
                    out=self.kin[128 * fc:128 * (fc + 1), BLK * t:BLK * (t + 1)],
                    in_=self.tsc2[:, 0:512]), reads=[self.b("tsc2")],
                    writes=[self.b("kin")])
                self.kv_norm_acc(1, first_n)
                first_n = False
        self.flatten2_max(self.nacc[0:2, 1:2], self.b("nacc"), self.srow[0:1, 2:3], 1)
        p.add("vector", lambda e: e.tensor_copy(self.tsc2[0:1, 0:1], self.srow[0:1, 2:3]),
              reads=[self.b("srow")], writes=[self.b("tsc2")])
        p.add("stw", lambda e: e.dma_start(out=self.kin[D:D + 1, 0:1],
                                             in_=self.tsc2[0:1, 0:1]),
              reads=[self.b("tsc2")], writes=[self.b("kin")])
        self.load_proj_weight(self.wv1)
        for t in range(NBLK):
            for dc in range(NCH):
                p.add("vector", lambda e, t=t, dc=dc: e.tensor_copy(
                    self.xt[dc][:], self.stream[dc][:, BLK * t:BLK * (t + 1)]),
                    reads=[self.b(f"stream{dc}")], writes=[self.b(f"xt{dc}")])
            for s in range(BLK // 128):
                for half in range(2):
                    psv = self.psA[1][:, 512 * half:512 * (half + 1)]
                    bps = self.b("psA1")
                    for dc in range(NCH):
                        p.add("tensor", lambda e, s=s, dc=dc, half=half, psv=psv:
                              e.matmul(
                            psv, self.xt[dc][:, 128 * s:128 * (s + 1)],
                            self.wpv(dc)[:, 512 * half:512 * (half + 1)],
                            start=(dc == 0), stop=(dc == NCH - 1)),
                            reads=[self.b("hs"), self.b(f"xt{dc}")], writes=[bps])
                p.add("vector", lambda e: e.tensor_copy(self.tsc2[:, 0:1024],
                                                        self.psA[1][:]),
                      reads=[self.b("psA1")], writes=[self.b("tsc2")])
                p.add("sdma2", lambda e, t=t, s=s: e.dma_start(
                    out=self.vin[BLK * t + 128 * s:BLK * t + 128 * (s + 1), :],
                    in_=self.tsc2[:, 0:1024]), reads=[self.b("tsc2")],
                    writes=[self.b("vin")])
        p.add("gpsimd", lambda e: e.collective_compute(
            "AllGather", ALU.bypass, replica_groups=GROUPS,
            ins=[self.kin], outs=[self.kout]),
            reads=[self.b("kin")], writes=[self.b("kout")])
        p.add("gpsimd", lambda e: e.collective_compute(
            "AllGather", ALU.bypass, replica_groups=GROUPS,
            ins=[self.vin], outs=[self.vout]),
            reads=[self.b("vin")], writes=[self.b("vout")])
        p.add("sbc", lambda e: e.dma_start(out=self.tsc2[0:1, 4:5],
                                            in_=self.kout[D:D + 1, 0:1]),
              reads=[self.b("kout")], writes=[self.b("tsc2")])
        p.add("sbc", lambda e: e.dma_start(out=self.tsc2[0:1, 5:6],
                                            in_=self.kout[2 * D + 1:2 * D + 2, 0:1]),
              reads=[self.b("kout")], writes=[self.b("tsc2")])
        p.add("vector", lambda e: e.tensor_copy(self.srow[0:1, 8:10],
                                                self.tsc2[0:1, 4:6]),
              reads=[self.b("tsc2")], writes=[self.b("srow")])
        p.add("vector", lambda e: e.reduce_max(self.srow[0:1, 2:3],
                                               self.srow[0:1, 8:10], axis=AXX),
              reads=[self.b("srow")], writes=[self.b("srow")])

    def emit_attention(self, layer):
        nc, p = self.nc, self.p
        scale = 0.125 if layer == 0 else float(np.float32(32.0 ** -0.5))
        wq = self.wq0 if layer == 0 else self.wq1
        self.load_proj_weight(wq, into="wp")
        if layer == 1:
            self.load_proj_weight(self.wo1, into="wo")
        for b_ in range(NBLK):
            self.emit_qproj(layer, b_, scale)
            if layer == 0 and b_ == 0:
                self.tap("srowA", self.srow[0:1, 0:16], self.b("srow"))
                self.tap("expb0", self.expb[:, 0:4], self.b("expb"))
                self.tap("qpA", self.qp[0][:], self.b("qp0"), BF16)
            pending = None
            for P in range(NH // 2):
                pending = self.emit_pair(layer, b_, P, scale, pending_tail=pending)
                if layer == 0 and b_ == 0 and P == 0:
                    self.tap("ao0", self.ao[0][:], self.b("ao0"), BF16)
            pending()
            if layer == 1:
                self.emit_oproj(b_)
            else:
                for fc in range(NCH):
                    p.add("vector", lambda e, fc=fc, b_=b_: e.tensor_tensor(
                        self.stream[fc][:, BLK * b_:BLK * (b_ + 1)],
                        self.stream[fc][:, BLK * b_:BLK * (b_ + 1)],
                        self.ao[fc][:], ALU.add),
                        reads=[self.b(f"ao{fc}")], writes=[self.b(f"stream{fc}")])
        self.emit_ln_full()

    def emit_qproj(self, layer, b_, scale):
        nc, p = self.nc, self.p
        for dc in range(NCH):
            if layer == 0:
                p.add("sync", lambda e, b_=b_, dc=dc: e.dma_start(
                    out=self.xt[dc][:], in_=self.xb16[BLK * b_:BLK * (b_ + 1),
                                                      128 * dc:128 * (dc + 1)],
                    transpose=True), reads=[self.b("xb16")], writes=[self.b(f"xt{dc}")])
            else:
                p.add("vector", lambda e, b_=b_, dc=dc: e.tensor_copy(
                    self.xt[dc][:], self.stream[dc][:, BLK * b_:BLK * (b_ + 1)]),
                    reads=[self.b(f"stream{dc}")], writes=[self.b(f"xt{dc}")])
        first = True
        for fc in range(NCH):
            psv = self.psA[0][:, 0:512]
            bps = self.b("psA0")
            for dc in range(NCH):
                p.add("tensor", lambda e, fc=fc, dc=dc, psv=psv: e.matmul(
                    psv, self.wpv(dc)[:, 128 * fc:128 * (fc + 1)], self.xt[dc][:],
                    start=(dc == 0), stop=(dc == NCH - 1)),
                    reads=[self.b("hs"), self.b(f"xt{dc}")], writes=[bps])
            if layer == 0:
                p.add("vector", lambda e, fc=fc, psv=psv: e.tensor_scalar(
                    out=self.qp[fc][:], in0=psv, scalar1=self.kssb[:, fc:fc + 1],
                    scalar2=None, op0=ALU.mult),
                    reads=[bps, self.b("kssb")], writes=[self.b(f"qp{fc}")])
            else:
                p.add("vector", lambda e, fc=fc, psv=psv: e.tensor_copy(
                    self.qp[fc][:], psv), reads=[bps], writes=[self.b(f"qp{fc}")])
            p.add("vector", lambda e, fc=fc: e.tensor_mul(
                self.tsc2[:, 2048:2560], self.qp[fc][:], self.qp[fc][:]),
                reads=[self.b(f"qp{fc}")], writes=[self.b("tsc2")])
            p.add("tensor", lambda e: e.matmul(self.psD[0:2, 512:1024], self.ones2b[:],
                                               self.tsc2[:, 2048:2560],
                                               start=True, stop=True),
                  reads=[self.b("tsc2"), self.b("ones2b")], writes=[self.b("psD")])
            p.add("vector", lambda e: e.tensor_copy(self.tsc2[0:2, 2560:3072],
                                                    self.psD[0:2, 512:1024]),
                  reads=[self.b("psD")], writes=[self.b("tsc2")])
            p.add("vector", lambda e: e.reduce_max(self.tsc[0:2, 2:3],
                                                   self.tsc2[0:2, 2560:3072], axis=AXX),
                  reads=[self.b("tsc2")], writes=[self.b("tsc")])
            if layer == 0 and b_ == 0:
                self.tap(f"cp{fc}", self.tsc2[0:2, 2560:3072], self.b("tsc2"), BF16)
                self.tap(f"ts{fc}", self.tsc[0:2, 2:3], self.b("tsc"))
            if first:
                p.add("vector", lambda e: e.tensor_copy(self.nacc[0:2, 2:3],
                                                        self.tsc[0:2, 2:3]),
                      reads=[self.b("tsc")], writes=[self.b("nacc")])
                first = False
            else:
                p.add("vector", lambda e: e.tensor_max(
                    self.nacc[0:2, 2:3], self.nacc[0:2, 2:3], self.tsc[0:2, 2:3]),
                    reads=[self.b("tsc")], writes=[self.b("nacc")])
        if layer == 0 and b_ == 0:
            self.tap("naccq", self.nacc[0:2, 2:3], self.b("nacc"))
        self.flatten2_max(self.nacc[0:2, 2:3], self.b("nacc"), self.srow[0:1, 3:4], 2)
        kcol = 1 if layer == 0 else 2
        p.add("vector", lambda e, kcol=kcol: e.tensor_mul(
            self.srow[0:1, 4:5], self.srow[0:1, 3:4], self.srow[0:1, kcol:kcol + 1]),
            reads=[self.b("srow")], writes=[self.b("srow")])
        p.add("scalar", lambda e: e.activation(self.srow[0:1, 5:6], self.srow[0:1, 4:5],
                                               AF.Ln),
              reads=[self.b("srow")], writes=[self.b("srow")])
        p.add("vector", lambda e, scale=scale: e.memset(self.srow[0:1, 6:7],
                                                        float(np.log(scale))),
              writes=[self.b("srow")])
        p.add("vector", lambda e: e.tensor_scalar(
            out=self.srow[0:1, 7:8], in0=self.srow[0:1, 5:6], scalar1=0.5,
            scalar2=self.srow[0:1, 6:7], op0=ALU.mult, op1=ALU.add),
            reads=[self.b("srow")], writes=[self.b("srow")])
        p.add("scalar", lambda e: e.activation(self.srow[0:1, 12:13],
                                               self.srow[0:1, 7:8], AF.Exp),
              reads=[self.b("srow")], writes=[self.b("srow")])
        p.add("vector", lambda e: e.tensor_scalar(
            out=self.srow[0:1, 13:14], in0=self.srow[0:1, 12:13], scalar1=-1.0,
            scalar2=EXPSHIFT, op0=ALU.mult, op1=ALU.add),
            reads=[self.b("srow")], writes=[self.b("srow")])
        col = 4 * layer + b_
        p.add("tensor", lambda e: e.matmul(
            self.psD[:, 0:1], self.onesF[0:1, :], self.srow[0:1, 13:14],
            start=True, stop=True),
            reads=[self.b("srow"), self.b("onesF")], writes=[self.b("psD")])
        p.add("vector", lambda e, col=col: e.tensor_copy(
            self.expb[:, col:col + 1], self.psD[:, 0:1]),
            reads=[self.b("psD")], writes=[self.b("expb")])

    def emit_pair(self, layer, b_, P, scale, pending_tail=None):
        nc, p = self.nc, self.p
        h0, h1 = 2 * P, 2 * P + 1
        bk = self.b("kpair")
        bv0, bv1 = self.b("vt0"), self.b("vt1")
        vv0 = self.vt[0][:].rearrange("p (c d) -> p c d", d=65)
        vv1 = self.vt[1][:].rearrange("p (c d) -> p c d", d=65)
        if layer == 0:
            kv = 64 * (h0 // 4)
            if P % 2 == 0:
                p.add("svt", lambda e, kv=kv: e.dma_start(
                    out=self.kpair[0:64, :], in_=self.k0scr[kv:kv + 64, :]),
                    reads=[self.b("k0scr")], writes=[bk])
                p.add("svt", lambda e, kv=kv: e.dma_start(
                    out=self.kpair[64:128, :], in_=self.k0scr[kv:kv + 64, :]),
                    reads=[self.b("k0scr")], writes=[bk])
                p.add("svt", lambda e, kv=kv, vv0=vv0: e.dma_start(
                    out=vv0[:, :, 0:64],
                    in_=self.v0scr.rearrange("(c p) f -> p c f", p=128)[
                        :, :, kv:kv + 64]),
                    reads=[self.b("v0scr")], writes=[bv0])
                p.add("svt", lambda e, kv=kv, vv1=vv1: e.dma_start(
                    out=vv1[:, :, 0:64],
                    in_=self.v0scr.rearrange("(c p) f -> p c f", p=128)[
                        :, :, kv:kv + 64]),
                    reads=[self.b("v0scr")], writes=[bv1])
        else:
            for half, h in ((0, h0), (1, h1)):
                prows = slice(64 * half, 64 * half + 64)
                p.add("sync", lambda e, h=h, prows=prows: e.dma_start(
                    out=self.kpair[prows, 0:TOK],
                    in_=self.kout[64 * h:64 * h + 64, :]),
                    reads=[self.b("kout")], writes=[bk])
                p.add("svt", lambda e, h=h, prows=prows: e.dma_start(
                    out=self.kpair[prows, TOK:SEQ],
                    in_=self.kout[D + 1 + 64 * h:D + 1 + 64 * h + 64, :]),
                    reads=[self.b("kout")], writes=[bk])
            p.add("svt", lambda e, h0=h0, vv0=vv0: e.dma_start(
                out=vv0[:, :, 0:64],
                in_=self.vout.rearrange("(c p) f -> p c f", p=128)[
                    :, :, 64 * h0:64 * h0 + 64]),
                reads=[self.b("vout")], writes=[bv0])
            p.add("svt", lambda e, h1=h1, vv1=vv1: e.dma_start(
                out=vv1[:, :, 0:64],
                in_=self.vout.rearrange("(c p) f -> p c f", p=128)[
                    :, :, 64 * h1:64 * h1 + 64]),
                reads=[self.b("vout")], writes=[bv1])
        col = 4 * layer + b_
        qbuf = self.qp[P]
        bq = self.b(f"qp{P}")
        bexp = self.b("expb")
        bpsE, bpsO = self.b("psE"), self.b("psO")
        def emit_score(c):
            psA = self.psA[c % 2]
            bpsA = self.b(f"psA{c % 2}")
            p.add("tensor", lambda e, c=c, psA=psA, qbuf=qbuf: e.matmul(
                psA[:, 0:512], self.kpair[0:64, 128 * c:128 * (c + 1)], qbuf[0:64, :],
                start=True, stop=True, tile_position=(0, 0)),
                reads=[bk, bq], writes=[bpsA])
            p.add("tensor", lambda e, c=c, psA=psA, qbuf=qbuf: e.matmul(
                psA[:, 512:1024], self.kpair[64:128, 128 * c:128 * (c + 1)],
                qbuf[64:128, :], start=True, stop=True, tile_position=(64, 0)),
                reads=[bk, bq], writes=[bpsA])

        emit_score(0)
        for c in range(NJC):
            if c + 1 < NJC:
                emit_score(c + 1)
            if c == 0 and pending_tail is not None:
                pending_tail()
                pending_tail = None
            psA = self.psA[c % 2]
            bpsA = self.b(f"psA{c % 2}")
            ptb = self.pt[c % 2]
            bpt = self.b(f"pt{c % 2}")
            p.add("scalar", lambda e, psA=psA, ptb=ptb, col=col, scale=scale:
                  e.activation(
                ptb[:], psA[:], AF.Exp, bias=self.expb[:, col:col + 1], scale=scale),
                reads=[bpsA, bexp], writes=[bpt])
            p.add("tensor", lambda e, c=c, ptb=ptb, vv0=vv0: e.matmul(
                self.psE[0:65, :], vv0[:, c, :], ptb[:, 0:512],
                start=(c == 0), stop=(c == NJC - 1)),
                reads=[bpt, bv0], writes=[bpsE])
            p.add("tensor", lambda e, c=c, ptb=ptb, vv1=vv1: e.matmul(
                self.psO[0:65, :], vv1[:, c, :], ptb[:, 512:1024],
                start=(c == 0), stop=(c == NJC - 1)),
                reads=[bpt, bv1], writes=[bpsO])
        if pending_tail is not None:
            pending_tail()

        def tail():
            p.add("vector", lambda e: e.reciprocal(self.bcsb[64:65, 0:512],
                                                   self.psE[64:65, :]),
                  reads=[bpsE], writes=[self.b("bcsb")])
            p.add("vector", lambda e: e.reciprocal(self.bcsb[64:65, 512:1024],
                                                   self.psO[64:65, :]),
                  reads=[bpsO], writes=[self.b("bcsb")])
            p.add("tensor", lambda e: e.matmul(
                self.psD[0:64, 0:512], self.onesF[64:65, 0:64],
                self.bcsb[64:65, 0:512],
                start=True, stop=True, tile_position=(64, 0)),
                reads=[self.b("bcsb"), self.b("onesF")], writes=[self.b("psD")])
            p.add("tensor", lambda e: e.matmul(
                self.psD[0:64, 512:1024], self.onesF[64:65, 0:64],
                self.bcsb[64:65, 512:1024],
                start=True, stop=True, tile_position=(64, 0)),
                reads=[self.b("bcsb"), self.b("onesF")], writes=[self.b("psD")])
            p.add("vector", lambda e: e.tensor_copy(self.bcsb[0:64, 0:1024],
                                                    self.psD[0:64, 0:1024]),
                  reads=[self.b("psD")], writes=[self.b("bcsb")])
            p.add("vector", lambda e: e.tensor_tensor(
                self.nrmp[0:64, 0:512], self.psE[0:64, :], self.bcsb[0:64, 0:512],
                ALU.mult),
                reads=[bpsE, self.b("bcsb")], writes=[self.b("nrmp")])
            p.add("vector", lambda e: e.tensor_tensor(
                self.nrmp[0:64, 512:1024], self.psO[0:64, :],
                self.bcsb[0:64, 512:1024],
                ALU.mult),
                reads=[bpsO, self.b("bcsb")], writes=[self.b("nrmp")])
            p.add("tensor", lambda e: e.matmul(self.psD[:, 0:512],
                                               self.identb[0:64, :],
                                               self.nrmp[0:64, 0:512],
                                               start=True, stop=False),
                  reads=[self.b("nrmp"), self.b("identb")], writes=[self.b("psD")])
            p.add("tensor", lambda e: e.matmul(self.psD[:, 0:512],
                                               self.ishif[0:64, :],
                                               self.nrmp[0:64, 512:1024],
                                               start=False, stop=True),
                  reads=[self.b("nrmp"), self.b("ishif")], writes=[self.b("psD")])
            p.add("vector", lambda e, P=P: e.tensor_copy(self.ao[P][:],
                                                         self.psD[:, 0:512]),
                  reads=[self.b("psD")], writes=[self.b(f"ao{P}")])
        return tail

    def emit_oproj(self, b_):
        nc, p = self.nc, self.p
        for fc in range(NCH):
            psv = self.psA[0][:, 0:512]
            bps = self.b("psA0")
            for dc in range(NCH):
                p.add("tensor", lambda e, fc=fc, dc=dc, psv=psv: e.matmul(
                    psv, self.wov(dc)[:, 128 * fc:128 * (fc + 1)], self.ao[dc][:],
                    start=(dc == 0), stop=(dc == NCH - 1)),
                    reads=[self.b("hs"), self.b(f"ao{dc}")], writes=[bps])
            p.add("vector", lambda e, fc=fc, b_=b_, psv=psv: e.tensor_tensor(
                self.stream[fc][:, BLK * b_:BLK * (b_ + 1)],
                self.stream[fc][:, BLK * b_:BLK * (b_ + 1)], psv, ALU.add),
                reads=[bps], writes=[self.b(f"stream{fc}")])

    def emit_ln_full(self):
        nc, p = self.nc, self.p
        for tt in range(NBLK):
            sl = slice(BLK * tt, BLK * (tt + 1))
            for fc in range(NCH):
                p.add("vector", lambda e, fc=fc, sl=sl: e.tensor_copy(
                    self.tsc2[:, 0:512], self.stream[fc][:, sl]),
                    reads=[self.b(f"stream{fc}")], writes=[self.b("tsc2")])
                p.add("vector", lambda e, fc=fc, sl=sl: e.tensor_mul(
                    self.tsc2[:, 2048:2560], self.stream[fc][:, sl],
                    self.stream[fc][:, sl]),
                    reads=[self.b(f"stream{fc}")], writes=[self.b("tsc2")])
                p.add("tensor", lambda e, fc=fc: e.matmul(
                    self.psD[0:1, 0:512], self.ones1b[:], self.tsc2[:, 0:512],
                    start=(fc == 0), stop=(fc == NCH - 1)),
                    reads=[self.b("tsc2"), self.b("ones1b")], writes=[self.b("psD")])
                p.add("tensor", lambda e, fc=fc: e.matmul(
                    self.psD[0:1, 512:1024], self.ones1b[:], self.tsc2[:, 2048:2560],
                    start=(fc == 0), stop=(fc == NCH - 1)),
                    reads=[self.b("tsc2"), self.b("ones1b")], writes=[self.b("psD")])
            p.add("vector", lambda e, sl=sl: e.tensor_copy(
                self.rowsSS[0:1, sl], self.psD[0:1, 0:512]),
                reads=[self.b("psD")], writes=[self.b("rowsSS")])
            p.add("vector", lambda e, tt=tt: e.tensor_copy(
                self.rowsSS[0:1, TOK + BLK * tt:TOK + BLK * (tt + 1)],
                self.psD[0:1, 512:1024]),
                reads=[self.b("psD")], writes=[self.b("rowsSS")])
        self.tap("rows0", self.rowsSS[0:1, :], self.b("rowsSS"))
        self.emit_ln_apply()

    def emit_ln_apply(self):
        nc, p = self.nc, self.p
        p.add("vector", lambda e: e.tensor_scalar(
            out=self.rowsSS[0:1, 0:TOK], in0=self.rowsSS[0:1, 0:TOK],
            scalar1=1.0 / D, scalar2=None, op0=ALU.mult),
            reads=[self.b("rowsSS")], writes=[self.b("rowsSS")])
        p.add("vector", lambda e: e.tensor_scalar(
            out=self.rowsSS[0:1, TOK:2 * TOK], in0=self.rowsSS[0:1, TOK:2 * TOK],
            scalar1=1.0 / D, scalar2=None, op0=ALU.mult),
            reads=[self.b("rowsSS")], writes=[self.b("rowsSS")])
        for tt in range(NBLK):
            sl = slice(BLK * tt, BLK * (tt + 1))
            sl2 = slice(TOK + BLK * tt, TOK + BLK * (tt + 1))
            p.add("vector", lambda e, sl=sl: e.tensor_mul(
                self.rowA[0:1, :], self.rowsSS[0:1, sl], self.rowsSS[0:1, sl]),
                reads=[self.b("rowsSS")], writes=[self.b("rowA")])
            p.add("vector", lambda e, sl2=sl2: e.tensor_sub(
                self.rowsSS[0:1, sl2], self.rowsSS[0:1, sl2], self.rowA[0:1, :]),
                reads=[self.b("rowA")], writes=[self.b("rowsSS")])
            p.add("scalar", lambda e, sl2=sl2: e.activation(
                self.rowsSS[0:1, sl2], self.rowsSS[0:1, sl2], AF.Ln,
                bias=self.srow[0:1, 30:31]),
                reads=[self.b("srow")], writes=[self.b("rowsSS")])
            p.add("scalar", lambda e, sl2=sl2: e.activation(
                self.rowsSS[0:1, sl2], self.rowsSS[0:1, sl2], AF.Exp, scale=-0.5),
                writes=[self.b("rowsSS")])
            p.add("vector", lambda e, sl=sl, sl2=sl2: e.tensor_mul(
                self.rowA[0:1, :], self.rowsSS[0:1, sl], self.rowsSS[0:1, sl2]),
                reads=[self.b("rowsSS")], writes=[self.b("rowA")])
            p.add("vector", lambda e: e.tensor_scalar(
                out=self.rowA[0:1, :], in0=self.rowA[0:1, :], scalar1=-1.0,
                scalar2=None, op0=ALU.mult),
                reads=[self.b("rowA")], writes=[self.b("rowA")])
            p.add("tensor", lambda e, sl2=sl2: e.matmul(
                self.psD[:, 0:512], self.onesF[0:1, :], self.rowsSS[0:1, sl2],
                start=True, stop=True),
                reads=[self.b("rowsSS"), self.b("onesF")], writes=[self.b("psD")])
            p.add("tensor", lambda e: e.matmul(
                self.psD[:, 512:1024], self.onesF[0:1, :], self.rowA[0:1, :],
                start=True, stop=True),
                reads=[self.b("rowA"), self.b("onesF")], writes=[self.b("psD")])
            for fc in range(NCH):
                p.add("vector", lambda e, fc=fc, sl=sl: e.tensor_tensor(
                    self.tsc[:, 0:512], self.stream[fc][:, sl], self.psD[:, 0:512],
                    ALU.mult),
                    reads=[self.b(f"stream{fc}"), self.b("psD")],
                    writes=[self.b("tsc")])
                p.add("vector", lambda e, fc=fc, sl=sl: e.tensor_tensor(
                    self.stream[fc][:, sl], self.tsc[:, 0:512], self.psD[:, 512:1024],
                    ALU.add),
                    reads=[self.b("tsc"), self.b("psD")],
                    writes=[self.b(f"stream{fc}")])

    def quant_scale(self, maxbuf, bmaxb, minbuf, bminb, W, toks0, wscol, use_bf16):
        nc, p = self.nc, self.p
        Wpad = ((W + 127) // 128) * 128
        ns = Wpad // 128
        if use_bf16:
            p.add("sdma2", lambda e: e.dma_start(out=self.qscr[:, 0:Wpad],
                                                 in_=maxbuf[:, 0:Wpad]),
                  reads=[bmaxb], writes=[self.b("qscr")])
            p.add("sdma2", lambda e: e.dma_start(out=self.qscr[:, 512:512 + Wpad],
                                                 in_=minbuf[:, 0:Wpad]),
                  reads=[bminb], writes=[self.b("qscr")])
        else:
            p.add("vector", lambda e: e.tensor_copy(self.tsc2[:, 3072:3072 + Wpad],
                                                    maxbuf[:, 0:Wpad]),
                  reads=[bmaxb], writes=[self.b("tsc2")])
            p.add("sdma2", lambda e: e.dma_start(out=self.qscr[:, 0:Wpad],
                                                 in_=self.tsc2[:, 3072:3072 + Wpad]),
                  reads=[self.b("tsc2")], writes=[self.b("qscr")])
            p.add("vector", lambda e: e.tensor_copy(self.tsc2[:, 3072:3072 + Wpad],
                                                    minbuf[:, 0:Wpad]),
                  reads=[bminb], writes=[self.b("tsc2")])
            p.add("sdma2", lambda e: e.dma_start(out=self.qscr[:, 512:512 + Wpad],
                                                 in_=self.tsc2[:, 3072:3072 + Wpad]),
                  reads=[self.b("tsc2")], writes=[self.b("qscr")])
        for s_ in range(ns):
            p.add("sbb", lambda e, s_=s_: e.dma_start(
                out=self.tsc2[:, 128 * s_:128 * (s_ + 1)],
                in_=self.qscr[:, 128 * s_:128 * (s_ + 1)], transpose=True),
                reads=[self.b("qscr")], writes=[self.b("tsc2")])
            p.add("sbb", lambda e, s_=s_: e.dma_start(
                out=self.tsc2[:, 512 + 128 * s_:512 + 128 * (s_ + 1)],
                in_=self.qscr[:, 512 + 128 * s_:512 + 128 * (s_ + 1)], transpose=True),
                reads=[self.b("qscr")], writes=[self.b("tsc2")])
            p.add("vector", lambda e, s_=s_: e.reduce_max(
                self.tsc[:, 16 + s_:17 + s_], self.tsc2[:, 128 * s_:128 * (s_ + 1)],
                axis=AXX),
                reads=[self.b("tsc2")], writes=[self.b("tsc")])
            p.add("vector", lambda e, s_=s_: e.tensor_reduce(
                self.tsc[:, 20 + s_:21 + s_],
                self.tsc2[:, 512 + 128 * s_:512 + 128 * (s_ + 1)], axis=AXX,
                op=ALU.min),
                reads=[self.b("tsc2")], writes=[self.b("tsc")])
        p.add("vector", lambda e, ns=ns: e.tensor_scalar(
            out=self.tsc[:, 20:20 + ns], in0=self.tsc[:, 20:20 + ns], scalar1=-1.0,
            scalar2=None, op0=ALU.mult), reads=[self.b("tsc")], writes=[self.b("tsc")])
        p.add("vector", lambda e, ns=ns: e.tensor_max(
            self.tsc[:, 16:16 + ns], self.tsc[:, 16:16 + ns], self.tsc[:, 20:20 + ns]),
            reads=[self.b("tsc")], writes=[self.b("tsc")])
        p.add("vector", lambda e, ns=ns: e.tensor_scalar(
            out=self.tsc[:, 16:16 + ns], in0=self.tsc[:, 16:16 + ns], scalar1=1e-5,
            scalar2=None, op0=ALU.max), reads=[self.b("tsc")], writes=[self.b("tsc")])
        p.add("vector", lambda e, ns=ns: e.reciprocal(
            self.tsc[:, 20:20 + ns], self.tsc[:, 16:16 + ns]),
            reads=[self.b("tsc")], writes=[self.b("tsc")])
        p.add("vector", lambda e, ns=ns: e.tensor_scalar(
            out=self.tsc[:, 20:20 + ns], in0=self.tsc[:, 20:20 + ns], scalar1=127.0,
            scalar2=None, op0=ALU.mult), reads=[self.b("tsc")], writes=[self.b("tsc")])
        p.add("vector", lambda e, ns=ns, wscol=wscol: e.tensor_scalar(
            out=self.tsc[:, 16:16 + ns], in0=self.tsc[:, 16:16 + ns],
            scalar1=self.mrow[:, 12 + wscol:13 + wscol], scalar2=None, op0=ALU.mult),
            reads=[self.b("tsc"), self.b("mrow")], writes=[self.b("tsc")])
        slots = [(self.psA[0], "psA0", 0), (self.psA[0], "psA0", 512),
                 (self.psA[1], "psA1", 0), (self.psA[1], "psA1", 512),
                 (self.psD, "psD", 0), (self.psD, "psD", 512),
                 (self.psE, "psE", 0), (self.psO, "psO", 0)]
        for s_ in range(ns):
            for k_, srccol in ((0, 20 + s_), (4, 16 + s_)):
                pst, pnm, off = slots[k_ + s_]
                p.add("tensor", lambda e, pst=pst, off=off, srccol=srccol: _tr(
                    e, pst[0:1, off:off + 128], self.tsc[:, srccol:srccol + 1],
                    self.ident[:]),
                    reads=[self.b("tsc"), self.b("ident")], writes=[self.b(pnm)])
        g = self.tguard()
        for s_ in range(ns):
            for k_ in (0, 4):
                pst, pnm, off = slots[k_ + s_]
                dst = self.cst[0] if k_ == 0 else self.cst[1]
                dnm = "cst0" if k_ == 0 else "cst1"
                p.add("vector", lambda e, pst=pst, off=off, s_=s_, dst=dst:
                      e.tensor_copy(dst[0:1, 128 * s_:128 * (s_ + 1)],
                                    pst[0:1, off:off + 128]),
                    reads=[self.b(pnm)], writes=[self.b(dnm)], deps=[g])
        p.add("tensor", lambda e: e.matmul(
            self.psD[:, 0:512], self.onesF[0:1, :], self.cst[0][0:1, 0:512],
            start=True, stop=True),
            reads=[self.b("cst0"), self.b("onesF")], writes=[self.b("psD")])
        p.add("tensor", lambda e: e.matmul(
            self.psD[:, 512:1024], self.onesF[0:1, :], self.cst[1][0:1, 0:512],
            start=True, stop=True),
            reads=[self.b("cst1"), self.b("onesF")], writes=[self.b("psD")])
        p.add("vector", lambda e: e.tensor_copy(self.bcsb[:, 0:512],
                                                self.psD[:, 0:512]),
              reads=[self.b("psD")], writes=[self.b("bcsb")])
        p.add("vector", lambda e: e.tensor_copy(self.bcsb[:, 512:1024],
                                                self.psD[:, 512:1024]),
              reads=[self.b("psD")], writes=[self.b("bcsb")])

    def emit_ffn(self, layer):
        nc, p = self.nc, self.p
        w1T = self.wqTp[layer]
        w2T = self.wqTp2[layer]
        bw1T = self.b(f"w1qTp_l{layer}")
        bw2T = self.b(f"w2qTp_l{layer}")
        cw = (self.cw0, self.cw1)[layer]
        cb = (self.cb0, self.cb1)[layer]
        jcol1 = 2 * layer + 0
        jcol2 = 2 * layer + 1
        cwv = self.cwsb[:].rearrange("p (c t) -> p c t", t=4)
        p.add("sync", lambda e, cw=cw, cwv=cwv: e.dma_start(
            out=cwv[:, :, 0:3], in_=cw.rearrange("(c p) t -> p c t", p=128),
            allow_slow_non_contiguous=True), writes=[self.b("cwsb")])
        p.add("sync", lambda e, cb=cb, cwv=cwv: e.dma_start(
            out=cwv[:, :, 3:4], in_=cb.rearrange("(c p one) -> p c one", p=128, one=1),
            allow_slow_non_contiguous=True), writes=[self.b("cwsb")])
        p.add("vector", lambda e: e.memset(self.saved2[:], 0.0),
              writes=[self.b("saved2")])
        hsv = self.hs[:].rearrange("p (c t) -> p c t", t=FT)
        sv2 = self.saved2[:].rearrange("p (c t) -> p c t", t=2)
        h0v = self.h0sv[:].rearrange("p (c t) -> p c t", t=2)
        bpv = self.bpay[:].rearrange("p (c t) -> p c t", t=2)
        for T in range(NFT):
            self.emit_quant_x(layer, T, jcol1)
            if T == 0 and layer == 0:
                self.tap("ser1", self.qp[7][:], self.b("qp7"), BF16)
            wN = 510 if T == 0 else 512
            w0_ = 2 if T == 0 else 0
            for hc in range(NFH):
                wb = self.w1buf[hc % 2]
                bwb = self.b(f"w1buf{hc % 2}")
                p.add("sync", lambda e, w1T=w1T, hc=hc, wb=wb: e.dma_start(
                    out=wb[:], in_=w1T[:, 1024 * hc:1024 * (hc + 1)]),
                    reads=[bw1T], writes=[bwb])
                psv = self.psE if hc % 2 == 0 else self.psO
                bps = self.b("psE") if hc % 2 == 0 else self.b("psO")
                for dc in range(NCH):
                    p.add("tensor", lambda e, dc=dc, wb=wb, psv=psv: e.matmul(
                        psv[:], wb[:, 128 * dc:128 * (dc + 1)], self.qp[dc][:],
                        start=(dc == 0), stop=(dc == NCH - 1)),
                        reads=[bwb, self.b(f"qp{dc}")], writes=[bps])
                hx = self.hx[hc % 2]
                bhx = self.b(f"hx{hc % 2}")
                cst = self.cst[hc % 2]
                bcst = self.b(f"cst{hc % 2}")
                p.add("vector", lambda e, hc=hc, sv2=sv2, hx=hx: e.tensor_copy(
                    hx[:, 0:2], sv2[:, hc, :]),
                    reads=[self.b("saved2")], writes=[bhx])
                p.add("scalar", lambda e, psv=psv, hx=hx: e.copy(hx[:, 2:514],
                                                                 psv[:]),
                      reads=[bps], writes=[bhx])
                if T == 0 and layer == 0 and hc == 0:
                    self.tap("h0", hx[:], bhx)
                p.add("vector", lambda e, hc=hc, sv2=sv2, hx=hx: e.tensor_copy(
                    sv2[:, hc, :], hx[:, 512:514]),
                    reads=[bhx], writes=[self.b("saved2")])
                if T == 0:
                    p.add("vector", lambda e, hc=hc, h0v=h0v, hx=hx: e.tensor_copy(
                        h0v[:, hc, :], hx[:, 2:4]),
                        reads=[bhx], writes=[self.b("h0sv")])
                    p.add("vector", lambda e, hc=hc, bpv=bpv, hx=hx: e.tensor_copy(
                        bpv[:, hc, 1:2], hx[:, 2:3]),
                        reads=[bhx], writes=[self.b("bpay")])
                if T == NFT - 1:
                    p.add("vector", lambda e, hc=hc, bpv=bpv, hx=hx: e.tensor_copy(
                        bpv[:, hc, 0:1], hx[:, 513:514]),
                        reads=[bhx], writes=[self.b("bpay")])
                p.add("vector", lambda e, hc=hc, cwv=cwv, w0_=w0_, wN=wN, hx=hx,
                      cst=cst: e.tensor_scalar(
                    out=cst[:, 0:wN], in0=hx[:, w0_:w0_ + wN],
                    scalar1=cwv[:, hc, 0:1], scalar2=cwv[:, hc, 3:4],
                    op0=ALU.mult, op1=ALU.add),
                    reads=[bhx, self.b("cwsb")], writes=[bcst])
                p.add("vector", lambda e, hc=hc, cwv=cwv, w0_=w0_, wN=wN, hx=hx,
                      cst=cst: e.scalar_tensor_tensor(
                    out=cst[:, 0:wN], in0=hx[:, w0_ + 1:w0_ + 1 + wN],
                    scalar=cwv[:, hc, 1:2], in1=cst[:, 0:wN],
                    op0=ALU.mult, op1=ALU.add),
                    reads=[bhx, self.b("cwsb")], writes=[bcst])
                p.add("vector", lambda e, hc=hc, cwv=cwv, w0_=w0_, wN=wN, hx=hx,
                      cst=cst: e.scalar_tensor_tensor(
                    out=cst[:, 0:wN], in0=hx[:, w0_ + 2:w0_ + 2 + wN],
                    scalar=cwv[:, hc, 2:3], in1=cst[:, 0:wN],
                    op0=ALU.mult, op1=ALU.add),
                    reads=[bhx, self.b("cwsb")], writes=[bcst])
                p.add("scalar", lambda e, hc=hc, hsv=hsv, wN=wN, cst=cst:
                      e.activation(
                    hsv[:, hc, 0:wN], cst[:, 0:wN], AF.Silu),
                    reads=[bcst], writes=[self.b("hs")])
                if T == 0 and layer == 0 and hc == 0:
                    self.tap("y0", self.cst[0][:], self.b("cst0"))
                    self.tap("hs0", self.hs[:, 0:512], self.b("hs"), BF16)
                if T == 0 and layer == 0 and hc == 15:
                    self.tap("ser2", self.hs[:, 7680:8192], self.b("hs"), BF16)
                if hc == 0:
                    p.add("vector", lambda e, hsv=hsv, wN=wN: e.tensor_copy(
                        self.runmax[:, 0:wN], hsv[:, 0, 0:wN]),
                        reads=[self.b("hs")], writes=[self.b("runmax")])
                    p.add("vector", lambda e, hsv=hsv, wN=wN: e.tensor_copy(
                        self.runmin[:, 0:wN], hsv[:, 0, 0:wN]),
                        reads=[self.b("hs")], writes=[self.b("runmin")])
                else:
                    p.add("vector", lambda e, hc=hc, hsv=hsv, wN=wN: e.tensor_max(
                        self.runmax[:, 0:wN], self.runmax[:, 0:wN], hsv[:, hc, 0:wN]),
                        reads=[self.b("hs")], writes=[self.b("runmax")])
                    p.add("vector", lambda e, hc=hc, hsv=hsv, wN=wN: e.tensor_tensor(
                        self.runmin[:, 0:wN], self.runmin[:, 0:wN], hsv[:, hc, 0:wN],
                        ALU.min),
                        reads=[self.b("hs")], writes=[self.b("runmin")])
            self.emit_tile_tail(layer, T, jcol2, w2T, bw2T)
        self.emit_halo_fix(layer, jcol2, w2T, bw2T)
        self.emit_ln_apply()

    def emit_quant_x(self, layer, T, jcol1):
        nc, p = self.nc, self.p
        sl = slice(FT * T, FT * (T + 1))
        for fc in range(NCH):
            if fc == 0:
                p.add("vector", lambda e, sl=sl: e.tensor_copy(
                    self.tsc[:, 0:512], self.stream[0][:, sl]),
                    reads=[self.b("stream0")], writes=[self.b("tsc")])
                p.add("vector", lambda e, sl=sl: e.tensor_copy(
                    self.ysc[:, 0:512], self.stream[0][:, sl]),
                    reads=[self.b("stream0")], writes=[self.b("ysc")])
            else:
                p.add("vector", lambda e, fc=fc, sl=sl: e.tensor_max(
                    self.tsc[:, 0:512], self.tsc[:, 0:512], self.stream[fc][:, sl]),
                    reads=[self.b(f"stream{fc}")], writes=[self.b("tsc")])
                p.add("vector", lambda e, fc=fc, sl=sl: e.tensor_tensor(
                    self.ysc[:, 0:512], self.ysc[:, 0:512], self.stream[fc][:, sl],
                    ALU.min),
                    reads=[self.b(f"stream{fc}")], writes=[self.b("ysc")])
        self.quant_scale(self.tsc, self.b("tsc"), self.ysc, self.b("ysc"),
                         512, FT * T, jcol1, False)
        if T == 0 and layer == 0:
            self.tap("xasinv", self.bcsb[:], self.b("bcsb"))
            self.tap("mrowt", self.mrow[:], self.b("mrow"))
            self.tap("tsct", self.tsc[0:128, 14:26], self.b("tsc"))
        if T == 0 and layer == 1:
            self.tap("xasinv1", self.bcsb[:], self.b("bcsb"))
            self.tap("tsct1", self.tsc[0:128, 14:26], self.b("tsc"))
        for fc in range(NCH):
            p.add("vector", lambda e, fc=fc, sl=sl: e.tensor_tensor(
                self.ysc[:, 0:512], self.stream[fc][:, sl], self.bcsb[:, 0:512],
                ALU.mult),
                reads=[self.b(f"stream{fc}"), self.b("bcsb")], writes=[self.b("ysc")])
            p.add("vector", lambda e: e.tensor_scalar(
                out=self.ysc[:, 0:512], in0=self.ysc[:, 0:512], scalar1=MAGIC,
                scalar2=MAGIC, op0=ALU.add, op1=ALU.subtract),
                reads=[self.b("ysc")], writes=[self.b("ysc")])
            p.add("vector", lambda e, fc=fc: e.tensor_tensor(
                self.qp[fc][:], self.ysc[:, 0:512], self.bcsb[:, 512:1024], ALU.mult),
                reads=[self.b("ysc"), self.b("bcsb")], writes=[self.b(f"qp{fc}")])
            if T == 0 and layer == 0 and fc == 0:
                self.tap("xq0", self.qp[0][:], self.b("qp0"), BF16)

    def emit_tile_tail(self, layer, T, jcol2, w2T, bw2T):
        nc, p = self.nc, self.p
        wN = 510 if T == 0 else 512
        toks0 = FT * T - 1 if T > 0 else 1
        hsv = self.hs[:].rearrange("p (c t) -> p c t", t=FT)
        if T == 0 and layer == 0:
            self.tap("ser3", self.runmax[:], self.b("runmax"), BF16)
            self.tap("hsall", self.hs[:], self.b("hs"), BF16)
        self.quant_scale(self.runmax, self.b("runmax"), self.runmin, self.b("runmin"),
                         wN, toks0, jcol2, True)
        if T == 0 and layer == 0:
            self.tap("hasinv", self.bcsb[:], self.b("bcsb"))
        if T == 0 and layer == 1:
            self.tap("hasinv1", self.bcsb[:], self.b("bcsb"))
            self.tap("hsall1", self.hs[:], self.b("hs"), BF16)
        for hc in range(NFH):
            p.add("vector", lambda e, hc=hc, hsv=hsv, wN=wN: e.tensor_tensor(
                self.ysc[:, 0:wN], hsv[:, hc, 0:wN], self.bcsb[:, 0:wN], ALU.mult),
                reads=[self.b("hs"), self.b("bcsb")], writes=[self.b("ysc")])
            p.add("vector", lambda e, hc=hc, hsv=hsv, wN=wN: e.tensor_scalar(
                out=hsv[:, hc, 0:wN], in0=self.ysc[:, 0:wN], scalar1=MAGIC,
                scalar2=MAGIC, op0=ALU.add, op1=ALU.subtract),
                reads=[self.b("ysc")], writes=[self.b("hs")])
        for dc in range(NCH):
            bwb = self.b("w2buf")
            p.add("sync", lambda e, w2T=w2T, dc=dc: e.dma_start(
                out=self.w2buf[:], in_=w2T[:, 4096 * dc:4096 * (dc + 1)]),
                reads=[bw2T], writes=[bwb])
            psv = self.psE if dc % 2 == 0 else self.psO
            bps = self.b("psE") if dc % 2 == 0 else self.b("psO")
            for hc in range(NFH):
                p.add("tensor", lambda e, hc=hc, psv=psv, wN=wN, hsv=hsv: e.matmul(
                    psv[:, 0:wN], self.w2buf[:, 128 * hc:128 * (hc + 1)],
                    hsv[:, hc, 0:wN], start=(hc == 0), stop=(hc == NFH - 1)),
                    reads=[bwb, self.b("hs")], writes=[bps])
            sl = slice(toks0, toks0 + wN)
            p.add("vector", lambda e, psv=psv, wN=wN: e.tensor_tensor(
                self.tsc[:, 0:wN], psv[:, 0:wN], self.bcsb[:, 512:512 + wN], ALU.mult),
                reads=[bps, self.b("bcsb")], writes=[self.b("tsc")])
            p.add("vector", lambda e, dc=dc, sl=sl, wN=wN: e.tensor_tensor(
                self.stream[dc][:, sl], self.stream[dc][:, sl], self.tsc[:, 0:wN],
                ALU.add),
                reads=[self.b("tsc")], writes=[self.b(f"stream{dc}")])
            p.add("vector", lambda e, dc=dc, sl=sl, wN=wN: e.tensor_copy(
                self.tsc2[:, 0:wN], self.stream[dc][:, sl]),
                reads=[self.b(f"stream{dc}")], writes=[self.b("tsc2")])
            p.add("vector", lambda e, dc=dc, sl=sl, wN=wN: e.tensor_mul(
                self.tsc2[:, 2048:2048 + wN], self.stream[dc][:, sl],
                self.stream[dc][:, sl]),
                reads=[self.b(f"stream{dc}")], writes=[self.b("tsc2")])
            p.add("tensor", lambda e, dc=dc, wN=wN: e.matmul(
                self.psA[0][0:1, 0:wN], self.ones1b[:], self.tsc2[:, 0:wN],
                start=(dc == 0), stop=(dc == NCH - 1)),
                reads=[self.b("tsc2"), self.b("ones1b")], writes=[self.b("psA0")])
            p.add("tensor", lambda e, dc=dc, wN=wN: e.matmul(
                self.psA[0][0:1, 512:512 + wN], self.ones1b[:],
                self.tsc2[:, 2048:2048 + wN],
                start=(dc == 0), stop=(dc == NCH - 1)),
                reads=[self.b("tsc2"), self.b("ones1b")], writes=[self.b("psA0")])
        p.add("vector", lambda e, toks0=toks0, wN=wN: e.tensor_copy(
            self.rowsSS[0:1, toks0:toks0 + wN], self.psA[0][0:1, 0:wN]),
            reads=[self.b("psA0")], writes=[self.b("rowsSS")])
        p.add("vector", lambda e, toks0=toks0, wN=wN: e.tensor_copy(
            self.rowsSS[0:1, TOK + toks0:TOK + toks0 + wN],
            self.psA[0][0:1, 512:512 + wN]),
            reads=[self.b("psA0")], writes=[self.b("rowsSS")])
        if T == 0 and layer == 0:
            self.tap("rh0", self.hs[:, 0:512], self.b("hs"), BF16)
            self.tap("res0", self.stream[0][:, 0:512], self.b("stream0"))

    def emit_halo_fix(self, layer, jcol2, w2T, bw2T):
        nc, p = self.nc, self.p
        hin, hout = self.hin[layer], self.hout[layer]
        bhin, bhout = self.b(f"hin{layer}"), self.b(f"hout{layer}")
        bpv = self.bpay[:].rearrange("p (c t) -> p c t", t=2)
        bhL = self.bhalo[:, 0:NFH].rearrange("p (c one) -> p c one", one=1)
        bhR = self.bhalo[:, NFH:2 * NFH].rearrange("p (c one) -> p c one", one=1)
        cwv = self.cwsb[:].rearrange("p (c t) -> p c t", t=4)
        h0v = self.h0sv[:].rearrange("p (c t) -> p c t", t=2)
        sv2 = self.saved2[:].rearrange("p (c t) -> p c t", t=2)
        t32 = self.tsc[:, 0:2 * NFH].rearrange("p (a c) -> p a c", a=2)
        p.add("vector", lambda e, t32=t32, bpv=bpv: e.tensor_scalar(
            out=t32[:, 0:1, :], in0=bpv[:, :, 0:1].rearrange("p c t -> p t c"),
            scalar1=self.mrow[:, 0:1], scalar2=None, op0=ALU.mult),
            reads=[self.b("bpay"), self.b("mrow")], writes=[self.b("tsc")])
        p.add("vector", lambda e, t32=t32, bpv=bpv: e.tensor_scalar(
            out=t32[:, 1:2, :], in0=bpv[:, :, 1:2].rearrange("p c t -> p t c"),
            scalar1=self.mrow[:, 1:2], scalar2=None, op0=ALU.mult),
            reads=[self.b("bpay"), self.b("mrow")], writes=[self.b("tsc")])
        p.add("stw", lambda e, hin=hin, t32=t32: e.dma_start(
            out=hin.rearrange("r (p c) -> p r c", p=128), in_=t32),
            reads=[self.b("tsc")], writes=[bhin])
        p.add("gpsimd", lambda e, hin=hin, hout=hout: e.collective_compute(
            "AllReduce", ALU.add, replica_groups=GROUPS, ins=[hin], outs=[hout]),
            reads=[bhin], writes=[bhout])
        p.add("sbb", lambda e, hout=hout: e.dma_start(
            out=self.bhalo[:].rearrange("p (t c) -> p t c", c=NFH),
            in_=hout.rearrange("r (p c) -> p r c", p=128)),
            reads=[bhout], writes=[self.b("bhalo")])
        p.add("vector", lambda e: e.tensor_scalar(
            out=self.bhalo[:, 0:NFH], in0=self.bhalo[:, 0:NFH],
            scalar1=self.mrow[:, 1:2], scalar2=None, op0=ALU.mult),
            reads=[self.b("mrow")], writes=[self.b("bhalo")])
        p.add("vector", lambda e: e.tensor_scalar(
            out=self.bhalo[:, NFH:2 * NFH], in0=self.bhalo[:, NFH:2 * NFH],
            scalar1=self.mrow[:, 0:1], scalar2=None, op0=ALU.mult),
            reads=[self.b("mrow")], writes=[self.b("bhalo")])
        y2 = self.ysc[:, 0:2 * NFH].rearrange("p (c t) -> p c t", t=2)
        scr = self.tsc[:, 64:64 + NFH].rearrange("p (c one) -> p c one", one=1)
        p.add("vector", lambda e, y2=y2, bhL=bhL, cwv=cwv: e.tensor_mul(
            y2[:, :, 0:1], bhL, cwv[:, :, 0:1]),
            reads=[self.b("bhalo"), self.b("cwsb")], writes=[self.b("ysc")])
        p.add("vector", lambda e, y2=y2, cwv=cwv: e.tensor_add(
            y2[:, :, 0:1], y2[:, :, 0:1], cwv[:, :, 3:4]),
            reads=[self.b("cwsb")], writes=[self.b("ysc")])
        p.add("vector", lambda e, scr=scr, h0v=h0v, cwv=cwv: e.tensor_mul(
            scr, h0v[:, :, 0:1], cwv[:, :, 1:2]),
            reads=[self.b("h0sv"), self.b("cwsb")], writes=[self.b("tsc")])
        p.add("vector", lambda e, y2=y2, scr=scr: e.tensor_add(
            y2[:, :, 0:1], y2[:, :, 0:1], scr),
            reads=[self.b("tsc")], writes=[self.b("ysc")])
        p.add("vector", lambda e, scr=scr, h0v=h0v, cwv=cwv: e.tensor_mul(
            scr, h0v[:, :, 1:2], cwv[:, :, 2:3]),
            reads=[self.b("h0sv"), self.b("cwsb")], writes=[self.b("tsc")])
        p.add("vector", lambda e, y2=y2, scr=scr: e.tensor_add(
            y2[:, :, 0:1], y2[:, :, 0:1], scr),
            reads=[self.b("tsc")], writes=[self.b("ysc")])
        p.add("vector", lambda e, y2=y2, sv2=sv2, cwv=cwv: e.tensor_mul(
            y2[:, :, 1:2], sv2[:, :, 0:1], cwv[:, :, 0:1]),
            reads=[self.b("saved2"), self.b("cwsb")], writes=[self.b("ysc")])
        p.add("vector", lambda e, y2=y2, cwv=cwv: e.tensor_add(
            y2[:, :, 1:2], y2[:, :, 1:2], cwv[:, :, 3:4]),
            reads=[self.b("cwsb")], writes=[self.b("ysc")])
        p.add("vector", lambda e, scr=scr, sv2=sv2, cwv=cwv: e.tensor_mul(
            scr, sv2[:, :, 1:2], cwv[:, :, 1:2]),
            reads=[self.b("saved2"), self.b("cwsb")], writes=[self.b("tsc")])
        p.add("vector", lambda e, y2=y2, scr=scr: e.tensor_add(
            y2[:, :, 1:2], y2[:, :, 1:2], scr),
            reads=[self.b("tsc")], writes=[self.b("ysc")])
        p.add("vector", lambda e, scr=scr, bhR=bhR, cwv=cwv: e.tensor_mul(
            scr, bhR, cwv[:, :, 2:3]),
            reads=[self.b("bhalo"), self.b("cwsb")], writes=[self.b("tsc")])
        p.add("vector", lambda e, y2=y2, scr=scr: e.tensor_add(
            y2[:, :, 1:2], y2[:, :, 1:2], scr),
            reads=[self.b("tsc")], writes=[self.b("ysc")])
        p.add("scalar", lambda e: e.activation(self.bhalo[:, 0:2 * NFH],
                                               self.ysc[:, 0:2 * NFH], AF.Silu),
              reads=[self.b("ysc")], writes=[self.b("bhalo")])
        hq2 = self.bhalo[:, 0:2 * NFH].rearrange("p (c t) -> p c t", t=2)
        ymx = self.ysc[:, 0:2 * NFH].rearrange("p (c t) -> p c t", t=2)
        ymn = self.hx[0][:, 0:2 * NFH].rearrange("p (c t) -> p c t", t=2)
        p.add("vector", lambda e: e.tensor_copy(self.ysc[:, 0:2 * NFH],
                                                self.bhalo[:, 0:2 * NFH]),
              reads=[self.b("bhalo")], writes=[self.b("ysc")])
        p.add("vector", lambda e: e.tensor_copy(self.hx[0][:, 0:2 * NFH],
                                                self.bhalo[:, 0:2 * NFH]),
              reads=[self.b("bhalo")], writes=[self.b("hx0")])
        n = NFH // 2
        while n >= 1:
            p.add("vector", lambda e, n=n, ymx=ymx: e.tensor_max(
                ymx[:, 0:n, :], ymx[:, 0:n, :], ymx[:, n:2 * n, :]),
                reads=[self.b("ysc")], writes=[self.b("ysc")])
            p.add("vector", lambda e, n=n, ymn=ymn: e.tensor_tensor(
                ymn[:, 0:n, :], ymn[:, 0:n, :], ymn[:, n:2 * n, :], ALU.min),
                reads=[self.b("hx0")], writes=[self.b("hx0")])
            n //= 2
        p.add("tensor", lambda e: _tr(e, self.psD[0:2, 0:128], self.ysc[:, 0:2],
                                              self.ident[:]),
              reads=[self.b("ysc"), self.b("ident")], writes=[self.b("psD")])
        p.add("tensor", lambda e: _tr(e, self.psD[0:2, 512:640],
                                      self.hx[0][:, 0:2],
                                              self.ident[:]),
              reads=[self.b("hx0"), self.b("ident")], writes=[self.b("psD")])
        g = self.tguard()
        p.add("vector", lambda e: e.tensor_copy(self.tsc2[0:2, 2560:2688],
                                                self.psD[0:2, 0:128]),
              reads=[self.b("psD")], writes=[self.b("tsc2")], deps=[g])
        p.add("vector", lambda e: e.tensor_copy(self.tsc2[0:2, 2688:2816],
                                                self.psD[0:2, 512:640]),
              reads=[self.b("psD")], writes=[self.b("tsc2")])
        p.add("vector", lambda e: e.reduce_max(self.tsc[0:2, 8:9],
                                               self.tsc2[0:2, 2560:2688], axis=AXX),
              reads=[self.b("tsc2")], writes=[self.b("tsc")])
        p.add("vector", lambda e: e.tensor_reduce(self.tsc[0:2, 9:10],
                                                  self.tsc2[0:2, 2688:2816],
                                                  axis=AXX, op=ALU.min),
              reads=[self.b("tsc2")], writes=[self.b("tsc")])
        p.add("vector", lambda e: e.tensor_scalar(
            out=self.tsc[0:2, 9:10], in0=self.tsc[0:2, 9:10], scalar1=-1.0,
            scalar2=None, op0=ALU.mult), reads=[self.b("tsc")], writes=[self.b("tsc")])
        p.add("vector", lambda e: e.tensor_max(self.tsc[0:2, 8:9], self.tsc[0:2, 8:9],
                                               self.tsc[0:2, 9:10]),
              reads=[self.b("tsc")], writes=[self.b("tsc")])
        p.add("vector", lambda e: e.tensor_scalar(
            out=self.tsc[0:2, 8:9], in0=self.tsc[0:2, 8:9], scalar1=1e-5,
            scalar2=None, op0=ALU.max), reads=[self.b("tsc")], writes=[self.b("tsc")])
        p.add("vector", lambda e: e.reciprocal(self.tsc[0:2, 10:11], self.tsc[0:2, 8:9]),
              reads=[self.b("tsc")], writes=[self.b("tsc")])
        p.add("vector", lambda e: e.tensor_scalar(
            out=self.tsc[0:2, 10:11], in0=self.tsc[0:2, 10:11], scalar1=127.0,
            scalar2=None, op0=ALU.mult), reads=[self.b("tsc")], writes=[self.b("tsc")])
        p.add("vector", lambda e, jcol2=jcol2: e.tensor_scalar(
            out=self.tsc[0:2, 8:9], in0=self.tsc[0:2, 8:9],
            scalar1=self.mrow[0:2, 12 + jcol2:13 + jcol2], scalar2=None, op0=ALU.mult),
            reads=[self.b("tsc"), self.b("mrow")], writes=[self.b("tsc")])
        p.add("stw", lambda e: e.dma_start(
            out=self.triph[0:1, 0:2].rearrange("one n -> n one"),
            in_=self.tsc[0:2, 10:11]),
            reads=[self.b("tsc")], writes=[self.b("triph")])
        p.add("stw", lambda e: e.dma_start(
            out=self.triph[0:1, 2:4].rearrange("one n -> n one"),
            in_=self.tsc[0:2, 8:9]),
            reads=[self.b("tsc")], writes=[self.b("triph")])
        p.add("sbc", lambda e: e.dma_start(
            out=self.bcsb[:, 0:4], in_=self.triph[0:1, 0:4].broadcast_to([128, 4])),
            reads=[self.b("triph")], writes=[self.b("bcsb")])
        p.add("vector", lambda e, hq2=hq2: e.tensor_tensor(
            self.ysc[:, 0:2 * NFH].rearrange("p (c t) -> p c t", t=2), hq2,
            self.bcsb[:, 0:2].rearrange("p (one t) -> p one t", one=1).broadcast_to([128, NFH, 2]),
            ALU.mult),
            reads=[self.b("bhalo"), self.b("bcsb")], writes=[self.b("ysc")])
        p.add("vector", lambda e: e.tensor_scalar(
            out=self.tsc2[:, 0:2 * NFH], in0=self.ysc[:, 0:2 * NFH], scalar1=MAGIC,
            scalar2=MAGIC, op0=ALU.add, op1=ALU.subtract),
            reads=[self.b("ysc")], writes=[self.b("tsc2")])
        for dc in range(NCH):
            bwb = self.b("w2buf")
            p.add("sync", lambda e, w2T=w2T, dc=dc: e.dma_start(
                out=self.w2buf[:], in_=w2T[:, 4096 * dc:4096 * (dc + 1)]),
                reads=[bw2T], writes=[bwb])
            for hc in range(NFH):
                p.add("tensor", lambda e, hc=hc: e.matmul(
                    self.psE[:, 0:2], self.w2buf[:, 128 * hc:128 * (hc + 1)],
                    self.tsc2[:, 2 * hc:2 * hc + 2], start=(hc == 0),
                    stop=(hc == NFH - 1)),
                    reads=[bwb, self.b("tsc2")], writes=[self.b("psE")])
            p.add("vector", lambda e: e.tensor_tensor(
                self.tsc[:, 4:6], self.psE[:, 0:2], self.bcsb[:, 2:4], ALU.mult),
                reads=[self.b("psE"), self.b("bcsb")], writes=[self.b("tsc")])
            p.add("vector", lambda e, dc=dc: e.tensor_add(
                self.stream[dc][:, 0:1], self.stream[dc][:, 0:1], self.tsc[:, 4:5]),
                reads=[self.b("tsc")], writes=[self.b(f"stream{dc}")])
            p.add("vector", lambda e, dc=dc: e.tensor_add(
                self.stream[dc][:, TOK - 1:TOK], self.stream[dc][:, TOK - 1:TOK],
                self.tsc[:, 5:6]),
                reads=[self.b("tsc")], writes=[self.b(f"stream{dc}")])
            p.add("vector", lambda e, dc=dc: e.tensor_copy(
                self.tsc2[:, 3000:3001], self.stream[dc][:, 0:1]),
                reads=[self.b(f"stream{dc}")], writes=[self.b("tsc2")])
            p.add("vector", lambda e, dc=dc: e.tensor_copy(
                self.tsc2[:, 3001:3002], self.stream[dc][:, TOK - 1:TOK]),
                reads=[self.b(f"stream{dc}")], writes=[self.b("tsc2")])
            p.add("vector", lambda e: e.tensor_mul(
                self.tsc2[:, 3002:3004], self.tsc2[:, 3000:3002],
                self.tsc2[:, 3000:3002]),
                reads=[self.b("tsc2")], writes=[self.b("tsc2")])
            p.add("tensor", lambda e, dc=dc: e.matmul(
                self.psA[1][0:1, 0:4], self.ones1b[:], self.tsc2[:, 3000:3004],
                start=(dc == 0), stop=(dc == NCH - 1)),
                reads=[self.b("tsc2"), self.b("ones1b")], writes=[self.b("psA1")])
        p.add("vector", lambda e: e.tensor_copy(self.rowsSS[0:1, 0:1],
                                                self.psA[1][0:1, 0:1]),
              reads=[self.b("psA1")], writes=[self.b("rowsSS")])
        p.add("vector", lambda e: e.tensor_copy(self.rowsSS[0:1, TOK - 1:TOK],
                                                self.psA[1][0:1, 1:2]),
              reads=[self.b("psA1")], writes=[self.b("rowsSS")])
        p.add("vector", lambda e: e.tensor_copy(self.rowsSS[0:1, TOK:TOK + 1],
                                                self.psA[1][0:1, 2:3]),
              reads=[self.b("psA1")], writes=[self.b("rowsSS")])
        p.add("vector", lambda e: e.tensor_copy(
            self.rowsSS[0:1, 2 * TOK - 1:2 * TOK], self.psA[1][0:1, 3:4]),
            reads=[self.b("psA1")], writes=[self.b("rowsSS")])

    def emit_output(self):
        nc, p = self.nc, self.p
        sts = []
        for fc in range(NCH):
            sts.append(p.add("sync", lambda e, fc=fc: e.dma_start(
                out=self.out[128 * fc:128 * (fc + 1), :], in_=self.stream[fc][:]),
                reads=[self.b(f"stream{fc}")]))
        p.add("sync", lambda e: e.nop(), deps=sts)


def build_program(stage=8):
    return K(stage=stage).build()


def kernel(**inputs):
    stage = int(os.environ.get("STAGE", "8"))
    nc = build_program(stage)
    x = np.ascontiguousarray(np.asarray(inputs["L0_attn_x"], np.float32))
    in_maps = []
    for core in range(NCORE):
        b_, h_ = core // 2, core % 2
        m = {
            "x_own": np.ascontiguousarray(x[b_, TOK * h_:TOK * (h_ + 1), :]),
            "x_oth": np.ascontiguousarray(x[b_, TOK * (1 - h_):TOK * (2 - h_), :]),
            "hmask": np.array([[1.0 - h_, float(h_)]], np.float32),
            "L0_attn_Wq": np.asarray(inputs["L0_attn_Wq"], np.float32),
            "L0_attn_Wkv": np.asarray(inputs["L0_attn_Wkv"], np.float32),
            "L0_attn_k_scale": np.ascontiguousarray(
                np.asarray(inputs["L0_attn_k_scale"], np.float32).reshape(-1)),
            "L0_ffn_w1": np.asarray(inputs["L0_ffn_w1"], np.float32),
            "L0_ffn_conv_w": np.ascontiguousarray(
                np.asarray(inputs["L0_ffn_conv_w"], np.float32).reshape(FF, 3)),
            "L0_ffn_conv_b": np.asarray(inputs["L0_ffn_conv_b"], np.float32),
            "L0_ffn_w2": np.asarray(inputs["L0_ffn_w2"], np.float32),
            "L1_attn_Wq": np.asarray(inputs["L1_attn_Wq"], np.float32),
            "L1_attn_Wk": np.asarray(inputs["L1_attn_Wk"], np.float32),
            "L1_attn_Wv": np.asarray(inputs["L1_attn_Wv"], np.float32),
            "L1_attn_Wo": np.asarray(inputs["L1_attn_Wo"], np.float32),
            "L1_attn_quadrant_weights": np.ascontiguousarray(
                np.asarray(inputs["L1_attn_quadrant_weights"],
                           np.float32).reshape(1, 4)),
            "L1_ffn_w1": np.asarray(inputs["L1_ffn_w1"], np.float32),
            "L1_ffn_conv_w": np.ascontiguousarray(
                np.asarray(inputs["L1_ffn_conv_w"], np.float32).reshape(FF, 3)),
            "L1_ffn_conv_b": np.asarray(inputs["L1_ffn_conv_b"], np.float32),
            "L1_ffn_w2": np.asarray(inputs["L1_ffn_w2"], np.float32),
        }
        in_maps.append(m)
    res = run_bass_kernel_spmd(nc, in_maps, core_ids=list(range(NCORE)))
    out = np.empty((4, SEQ, D), np.float32)
    for core in range(NCORE):
        b_, h_ = core // 2, core % 2
        out[b_, TOK * h_:TOK * (h_ + 1), :] = res.results[core]["out"].T
    return out
